# version 01
# baseline (speedup 1.0000x reference)
import numpy as np
import ml_dtypes
from contextlib import ExitStack
import concourse.bass as bass
import concourse.mybir as mybir
from concourse.bass_utils import run_bass_kernel_spmd

F32 = mybir.dt.float32
BF16 = mybir.dt.bfloat16
AF = mybir.ActivationFunctionType
ALU = mybir.AluOpType

C = 1024
T = 8192
NKB = 8
ST = 256
NST = T // ST
LCH = 64
NCH = ST // LCH
EMB = float(np.exp(-0.5))

VP_ORDER = ["ln_pre_w", "ln_pre_b", "ln_attn_w", "ln_attn_b", "x_r", "x_w",
            "x_k", "x_v", "x_a", "x_g", "w0", "a0", "k_k", "k_a", "r_k",
            "gn_w", "gn_b", "ln_ffn_w", "ln_ffn_b", "ffn_x_k"]
VPC = {n: 8 * i for i, n in enumerate(VP_ORDER)}
NVP = 8 * len(VP_ORDER)

STB = 1024
NSTB = T // STB
STC = 1024
NSTC = T // STC
STD = 512
NSTD = T // STD


class Sem:
    def __init__(self, handle):
        self.h = handle
        self.n = 0

    def bump(self, k):
        self.n += k
        return self.n


class Pass:

    def __init__(self, active, eng):
        self.active = active
        self.e = eng

    def do(self, engine, fn, wait=(), inc=None, inc_by=1):
        tgt = inc.bump(inc_by) if inc is not None else None
        if engine == self.active:
            for (s, v) in wait:
                self.e.wait_ge(s.h, v)
            last = fn(self.e)
            if inc is not None:
                assert last is not None
                last.then_inc(inc.h, inc_by)
        return tgt


def build(nc, dbg=(), stage=4, nst=NST):
    dt = nc.dram_tensor
    x0_d = dt("x0", [C, T], F32, kind="ExternalInput").ap()
    wr_d = dt("wr", [C, C], F32, kind="ExternalInput").ap()
    wk_d = dt("wk", [C, C], F32, kind="ExternalInput").ap()
    wv_d = dt("wv", [C, C], F32, kind="ExternalInput").ap()
    wo_d = dt("wo", [C, C], F32, kind="ExternalInput").ap()
    w1_d = dt("w1", [C, 64], F32, kind="ExternalInput").ap()
    a1_d = dt("a1", [C, 64], F32, kind="ExternalInput").ap()
    g1_d = dt("g1", [C, 128], F32, kind="ExternalInput").ap()
    w2_d = dt("w2", [64, C], F32, kind="ExternalInput").ap()
    a2_d = dt("a2", [64, C], F32, kind="ExternalInput").ap()
    g2_d = dt("g2", [128, C], F32, kind="ExternalInput").ap()
    wf1_d = dt("wf1", [C, 4096], F32, kind="ExternalInput").ap()
    wf2_d = dt("wf2", [4096, C], F32, kind="ExternalInput").ap()
    egw_d = dt("egw", [15, C, 512], F32, kind="ExternalInput").ap()
    euw_d = dt("euw", [15, C, 512], F32, kind="ExternalInput").ap()
    edw_d = dt("edw", [15, 512, C], F32, kind="ExternalInput").ap()
    wc_d = dt("wc", [3, C, 512], F32, kind="ExternalInput").ap()
    sup_d = dt("sup", [C, 512], F32, kind="ExternalInput").ap()
    sdw_d = dt("sdw", [512, C], F32, kind="ExternalInput").ap()
    rw_d = dt("rw", [C, 16], F32, kind="ExternalInput").ap()
    rb_d = dt("rb", [16, 1], F32, kind="ExternalInput").ap()
    vp_d = dt("vp", [128, NVP], F32, kind="ExternalInput").ap()
    idb_d = dt("idb", [128, 64], BF16, kind="ExternalInput").ap()
    mk4_d = dt("mk4", [128, 256], BF16, kind="ExternalInput").ap()
    mk1_d = dt("mk1", [128, 64], BF16, kind="ExternalInput").ap()
    onc_d = dt("onc", [128, 1], BF16, kind="ExternalInput").ap()
    onr_d = dt("onr", [1, 128], BF16, kind="ExternalInput").ap()
    e16_d = dt("e16", [128, 128], BF16, kind="ExternalInput").ap()
    f16_d = dt("f16", [16, 1024], BF16, kind="ExternalInput").ap()
    if16_d = dt("if16", [16, 16], F32, kind="ExternalInput").ap()
    s15_d = dt("s15", [16, 128], BF16, kind="ExternalInput").ap()
    sel_d = dt("sel15", [16, 1920], BF16, kind="ExternalInput").ap()
    if128_d = dt("if128", [128, 128], F32, kind="ExternalInput").ap()
    _k = lambda n: "ExternalOutput" if n in dbg else "Internal"
    x2_d = dt("x2s", [C, T], F32, kind=_k("x2s")).ap()
    h1_d = dt("h1s", [C, T], F32, kind=_k("h1s")).ap()
    rtd_d = dt("rtds", [C, T], F32, kind=_k("rtds")).ap()
    cmb_d = dt("cmbs", [16, T], F32, kind=_k("cmbs")).ap()
    out_d = dt("out", [C, T], F32, kind="ExternalOutput").ap()
    dbg_d = {}
    for name in dbg:
        shp = {"cmb": [16, T], "sar": [128, 1024], "sbs": [128, 1024],
               "skt": [128, 512], "scb": [128, 520], "sci": [128, 512],
               "ssol": [128, 2048], "sotm": [128, 512], "spre": [128, 2048],
               "sgp3": [128, 1536]}.get(name, [C, T])
        dbg_d[name] = dt("dbg_" + name, shp, F32, kind="ExternalOutput").ap()

    x0_r = x0_d.rearrange("(a p) t -> p a t", p=128)
    x2_r = x2_d.rearrange("(a p) t -> p a t", p=128)
    h1_r = h1_d.rearrange("(a p) t -> p a t", p=128)
    rtd_r = rtd_d.rearrange("(a p) t -> p a t", p=128)
    out_r = out_d.rearrange("(a p) t -> p a t", p=128)

    es = ExitStack()
    with es:
        vp_ = es.enter_context(nc.sbuf_tensor("vp_s", [128, NVP], F32))
        aux_ = es.enter_context(nc.sbuf_tensor("aux", [128, 32], F32))
        idb_ = es.enter_context(nc.sbuf_tensor("idb_s", [128, 64], BF16))
        mk4_ = es.enter_context(nc.sbuf_tensor("mk4_s", [128, 256], BF16))
        mk1_ = es.enter_context(nc.sbuf_tensor("mk1_s", [128, 64], BF16))
        onc_ = es.enter_context(nc.sbuf_tensor("onc_s", [128, 1], BF16))
        onr_ = es.enter_context(nc.sbuf_tensor("onr_s", [1, 128], BF16))
        e16_ = es.enter_context(nc.sbuf_tensor("e16_s", [128, 128], BF16))
        f16_ = es.enter_context(nc.sbuf_tensor("f16_s", [16, 1024], BF16))
        sthi_ = es.enter_context(nc.sbuf_tensor("sthi", [16, 2048], BF16))
        zer_ = es.enter_context(nc.sbuf_tensor("zer", [128, 64], F32))
        selb_ = es.enter_context(nc.sbuf_tensor("selb", [16, 1920], BF16))
        hbx_ = es.enter_context(nc.sbuf_tensor("hbx", [128, 2080], BF16))
        hDs = es.enter_context(nc.semaphore("sDs"))
        hDg = es.enter_context(nc.semaphore("sDg"))
        hP = es.enter_context(nc.semaphore("sP"))
        hV = es.enter_context(nc.semaphore("sV"))
        hA = es.enter_context(nc.semaphore("sA"))
        block = es.enter_context(nc.Block())
        vp = vp_.ap()
        aux = aux_.ap()
        idb = idb_.ap()
        mk4 = mk4_.ap()
        mk1 = mk1_.ap()
        onc = onc_.ap()
        onr = onr_.ap()
        e16 = e16_.ap()
        f16 = f16_.ap()
        sthi = sthi_.ap()
        zer = zer_.ap()
        selb = selb_.ap()
        hbx = hbx_.ap()

        def vcol(name, k):
            c0 = VPC[name]
            return vp[:, c0 + k:c0 + k + 1]

        def emit_all(P, ctx):
            sDs, sDg = Sem(hDs), Sem(hDg)
            sP, sV, sA = Sem(hP), Sem(hV), Sem(hA)

            def dma(out, in_, wait=(), eng="sync"):
                sem = sDs if eng == "sync" else sDg
                return P.do(eng, lambda e: e.dma_start(out=out, in_=in_),
                            wait=wait, inc=sem, inc_by=16)

            def bar2():
                vals = [(sDs, sDs.n), (sDg, sDg.n), (sP, sP.n), (sV, sV.n),
                        (sA, sA.n)]
                for eng in ("tensor", "vector", "scalar", "sync", "gpsimd"):
                    def f(e, vals=vals):
                        for (s, v) in vals:
                            if v > 0:
                                e.wait_ge(s.h, v)
                        return None
                    P.do(eng, f)

            emit_pa(P, ctx, dma, bar2, sDs, sDg, sP, sV, sA)
            if stage >= 2:
                emit_pb(P, ctx, dma, bar2, sDs, sDg, sP, sV, sA)
            if stage >= 3:
                emit_pc(P, ctx, dma, bar2, sDs, sDg, sP, sV, sA)
            if stage >= 4:
                emit_pd(P, ctx, dma, bar2, sDs, sDg, sP, sV, sA)
            bar2()

        def emit_pa(P, ctx, dma, bar2, sDs, sDg, sP, sV, sA):
            (x0a, x0b, x1t, Wt, scum, tmpA, tmpB, tmpC, x0f, x0g, x1f, xnt,
             dxt, sqt, m_r, m_kraw, m_kp, m_kk, m_b, m_v, m_g, m_o, hw, ha,
             hg, cxn, cbuf, cibuf, clast, AR, BS, ktil, vbtm, ktm, gp3, sol0,
             sol1, otm, sfp, wkb, wvb, slotA, w1b, a1b, g1b, w2b, a2b, g2b,
             PAB, PT, PT2, PCD, PS, PO) = ctx

            def kb(ap2d, k, w=ST):
                return ap2d[:, k * w:(k + 1) * w]

            def stsl(r3, st, w=ST):
                return r3[:, :, st * w:(st + 1) * w]

            def r3(tile, a=8):
                return tile.rearrange("p (a t) -> p a t", a=a)

            t_vp = dma(vp, vp_d)
            dma(idb, idb_d)
            dma(mk4, mk4_d)
            dma(mk1, mk1_d)
            dma(onc, onc_d)
            dma(onr, onr_d)
            dma(e16, e16_d)
            t_cst = dma(f16, f16_d)
            wkb_r = wkb.rearrange("p (a m) -> p a m", a=8)
            wvb_r = wvb.rearrange("p (a m) -> p a m", a=8)
            slotA_r = slotA.rearrange("p (a m) -> p a m", a=8)
            t_wk = dma(wkb_r, wk_d.rearrange("(a p) m -> p a m", p=128),
                       eng="gpsimd")
            t_wv = dma(wvb_r, wv_d.rearrange("(a p) m -> p a m", p=128),
                       eng="gpsimd")
            dma(w1b.rearrange("p (a m) -> p a m", a=8),
                w1_d.rearrange("(a p) m -> p a m", p=128), eng="gpsimd")
            dma(a1b.rearrange("p (a m) -> p a m", a=8),
                a1_d.rearrange("(a p) m -> p a m", p=128), eng="gpsimd")
            dma(g1b.rearrange("p (a m) -> p a m", a=8),
                g1_d.rearrange("(a p) m -> p a m", p=128), eng="gpsimd")
            dma(w2b[:], w2_d, eng="gpsimd")
            dma(a2b[:], a2_d, eng="gpsimd")
            t_smallw = dma(g2b[:], g2_d, eng="gpsimd")
            t_wr = dma(slotA_r, wr_d.rearrange("(a p) m -> p a m", p=128),
                       eng="gpsimd")

            def f_aux(e):
                e.tensor_scalar(aux[:, 0:8], vp[:, VPC["w0"]:VPC["w0"] + 8],
                                0.5, None, ALU.mult)
                e.tensor_scalar(aux[:, 8:16], vp[:, VPC["a0"]:VPC["a0"] + 8],
                                0.5, None, ALU.mult)
                e.memset(aux[:, 24:25], 1e-5)
                e.memset(aux[:, 25:26], 0.00064)
                e.memset(aux[:, 26:27], 0.5)
                e.memset(aux[:, 27:28], EMB * 0.5)
                e.memset(aux[:, 29:30], -1.0)
                return e.tensor_scalar(aux[:, 16:24],
                                       vp[:, VPC["k_a"]:VPC["k_a"] + 8],
                                       -1.0, 1.0, ALU.mult, ALU.add)
            t_aux = P.do("vector", f_aux, wait=[(sDs, t_vp)], inc=sV)

            def f_zi(e):
                s0 = r3(sol0, a=8)
                s1 = r3(sol1, a=8)
                e.memset(s0[:, :, 192:256], 0.0)
                e.memset(s1[:, :, 192:256], 0.0)
                e.memset(sfp, 0.0)
                e.memset(r3(BS)[:, :, 64:128], 0.0)
                e.memset(zer, 0.0)
                return e.memset(cxn, 0.0)
            t_zi = P.do("vector", f_zi, inc=sV)

            x0t_ = [x0a, x0b]
            x0f_ = [x0f, x0g]
            t_x0 = [0, 0]
            t_x0b = [0, 0]
            t_x0[0] = dma(r3(x0t_[0]), stsl(x0_r, 0))
            t_x0b[0] = dma(r3(x0f_[0]), stsl(x0_r, 0), eng="gpsimd")
            t_wr_st = t_wr

            for st in range(nst):
                cur = st % 2
                nxt = 1 - cur
                x0t = x0t_[cur]
                x0bf = x0f_[cur]
                bar2()
                if st + 1 < nst:
                    t_x0[nxt] = dma(r3(x0t_[nxt]), stsl(x0_r, st + 1))
                    t_x0b[nxt] = dma(r3(x0f_[nxt]), stsl(x0_r, st + 1),
                                     eng="gpsimd")
                wx = [(sDs, t_x0[cur]), (sDg, t_x0b[cur])]
                if st == 0:
                    wx += [(sDg, t_smallw), (sDs, t_cst), (sV, t_aux),
                           (sV, t_zi), (sDg, t_wk), (sDg, t_wv)]

                def ln_block(xsrc_f32, xsrc_bf, wname, bname, dst_f32, dst_bf,
                             first_wait):
                    tq = P.do("scalar",
                              lambda e: e.activation(sqt, xsrc_bf, AF.Square),
                              wait=first_wait, inc=sA)

                    def f_sums(e):
                        for k in range(NKB):
                            e.matmul(PS[0:1, 0:256], onc, kb(xsrc_bf, k),
                                     start=(k == 0), stop=False,
                                     skip_group_check=True)
                        last = None
                        for k in range(NKB):
                            last = e.matmul(PS[0:1, 256:512], onc, kb(sqt, k),
                                            start=False, stop=(k == 7),
                                            skip_group_check=True)
                        return last
                    tp = P.do("tensor", f_sums, wait=first_wait + [(sA, tq)],
                              inc=sP)

                    def f_stats(e):
                        e.tensor_scalar(tmpA[0:1, :], PS[0:1, 0:256],
                                        1.0 / C, None, ALU.mult)
                        e.tensor_mul(tmpB[0:1, :], tmpA[0:1, :], tmpA[0:1, :])
                        e.scalar_tensor_tensor(tmpB[0:1, :], PS[0:1, 256:512],
                                               1.0 / C, tmpB[0:1, :],
                                               ALU.mult, ALU.subtract)
                        e.tensor_copy(sthi[0:1, 0:256], tmpA[0:1, :])
                        return e.tensor_sub(tmpA[0:1, :], tmpA[0:1, :],
                                            sthi[0:1, 0:256])
                    tv = P.do("vector", f_stats, wait=[(sP, tp)], inc=sV)

                    def f_rstd(e):
                        e.activation(tmpC[0:1, :], tmpB[0:1, :], AF.Ln,
                                     bias=aux[0:1, 24:25])
                        return e.activation(tmpC[0:1, :], tmpC[0:1, :],
                                            AF.Exp, scale=-0.5)
                    ta = P.do("scalar", f_rstd, wait=[(sV, tv)], inc=sA)

                    def f_hl(e):
                        e.tensor_copy(sthi[0:1, 256:512], tmpA[0:1, :])
                        e.tensor_copy(sthi[0:1, 512:768], tmpC[0:1, :])
                        e.tensor_sub(tmpC[0:1, :], tmpC[0:1, :],
                                     sthi[0:1, 512:768])
                        return e.tensor_copy(sthi[0:1, 768:1024],
                                             tmpC[0:1, :])
                    tv2 = P.do("vector", f_hl, wait=[(sA, ta)], inc=sV)

                    def f_bcast(e):
                        e.matmul(PO[:, 0:256], onr, sthi[0:1, 0:256],
                                 start=True, stop=False, skip_group_check=True)
                        e.matmul(PO[:, 0:256], onr, sthi[0:1, 256:512],
                                 start=False, stop=True, skip_group_check=True)
                        e.matmul(PO[:, 256:512], onr, sthi[0:1, 512:768],
                                 start=False, stop=False,
                                 skip_group_check=True)
                        return e.matmul(PO[:, 256:512], onr,
                                        sthi[0:1, 768:1024], start=False,
                                        stop=True, skip_group_check=True)
                    tp2 = P.do("tensor", f_bcast, wait=[(sV, tv2)], inc=sP)

                    def f_apply(e):
                        last = None
                        for k in range(NKB):
                            e.tensor_sub(tmpA, kb(xsrc_f32, k), PO[:, 0:256])
                            e.scalar_tensor_tensor(tmpB, tmpA, vcol(wname, k),
                                                   PO[:, 256:512], ALU.mult,
                                                   ALU.mult)
                            if dst_f32 is not None:
                                e.tensor_scalar(kb(dst_f32, k), tmpB,
                                                vcol(bname, k), None, ALU.add)
                                if dst_bf is not None:
                                    last = e.tensor_copy(kb(dst_bf, k),
                                                         kb(dst_f32, k))
                                else:
                                    last = None or e.engine_nop()
                            else:
                                last = e.tensor_scalar(kb(dst_bf, k), tmpB,
                                                       vcol(bname, k), None,
                                                       ALU.add)
                        return last
                    return P.do("vector", f_apply, wait=[(sP, tp2)], inc=sV)

                tln1 = ln_block(x0t[:], x0bf[:], "ln_pre_w", "ln_pre_b",
                                x1t[:], x1f[:], wx)
                tln2 = ln_block(x1t[:], x1f[:], "ln_attn_w", "ln_attn_b",
                                None, xnt[:], [(sV, tln1)])

                def f_dx(e):
                    xv = r3(xnt)
                    dv = r3(dxt)
                    e.tensor_sub(dv[:, :, 1:ST], xv[:, :, 0:ST - 1],
                                 xv[:, :, 1:ST])
                    e.tensor_sub(dv[:, :, 0:1],
                                 cxn.rearrange("p (a o) -> p a o", o=1),
                                 xv[:, :, 0:1])
                    return e.tensor_copy(
                        cxn.rearrange("p (a o) -> p a o", o=1), xv[:, :, ST - 1:ST])
                tdx = P.do("vector", f_dx, wait=[(sV, tln2)], inc=sV)

                def mk_mix(name, wait):
                    def f(e):
                        last = None
                        for k in range(NKB):
                            last = e.scalar_tensor_tensor(
                                kb(sqt, k), kb(dxt, k), vcol(name, k),
                                kb(xnt, k), ALU.mult, ALU.add)
                        return last
                    return P.do("vector", f, wait=wait, inc=sV)

                def proj_big(wsb, dst, extra_wait):
                    tt = [0, 0]
                    tev = [0, 0]
                    for mb in range(8):
                        pb = mb % 2
                        ps = PAB[:, 0:256] if pb == 0 else PAB[:, 512:768]
                        w = list(extra_wait) if mb == 0 else []
                        if mb >= 2:
                            w.append((sV, tev[pb]))

                        def f(e, mb=mb, ps=ps):
                            last = None
                            for k in range(NKB):
                                last = e.matmul(
                                    ps, wsb[:, (k * 8 + mb) * 128:
                                            (k * 8 + mb) * 128 + 128],
                                    kb(sqt, k), start=(k == 0), stop=(k == 7),
                                    skip_group_check=True)
                            return last
                        tt[pb] = P.do("tensor", f, wait=w, inc=sP)
                        tev[pb] = P.do(
                            "vector",
                            lambda e, mb=mb, ps=ps: e.tensor_copy(kb(dst, mb),
                                                                  ps),
                            wait=[(sP, tt[pb])], inc=sV)
                    return tev

                tmx = mk_mix("x_r", [(sV, tdx)])
                er = proj_big(slotA, m_r[:], [(sV, tmx), (sDg, t_wr_st)])
                tmx = mk_mix("x_v", [(sV, er[0]), (sV, er[1])])
                ev = proj_big(wvb, m_v[:], [(sV, tmx)])
                tmx = mk_mix("x_k", [(sV, ev[0]), (sV, ev[1])])
                ek = proj_big(wkb, m_kraw[:], [(sV, tmx)])
                t_wo_st = dma(slotA_r,
                              wo_d.rearrange("(a p) m -> p a m", p=128),
                              wait=[(sV, er[0]), (sV, er[1])], eng="gpsimd")

                tmx = mk_mix("x_w", [(sV, ek[0]), (sV, ek[1])])

                def f_hw(e):
                    last = None
                    for k in range(NKB):
                        last = e.matmul(PS[0:64, 0:256],
                                        w1b[:, k * 64:(k + 1) * 64],
                                        kb(sqt, k), start=(k == 0),
                                        stop=(k == 7), skip_group_check=True)
                    return last
                tphw = P.do("tensor", f_hw, wait=[(sV, tmx)], inc=sP)
                tahw = P.do("scalar",
                            lambda e: e.activation(hw, PS[0:64, 0:256],
                                                   AF.Tanh),
                            wait=[(sP, tphw)], inc=sA)

                tmxa = mk_mix("x_a", [(sP, tphw)])

                def f_ha(e):
                    last = None
                    for k in range(NKB):
                        last = e.matmul(PO[0:64, 0:256],
                                        a1b[:, k * 64:(k + 1) * 64],
                                        kb(sqt, k), start=(k == 0),
                                        stop=(k == 7), skip_group_check=True)
                    return last
                tpha = P.do("tensor", f_ha, wait=[(sV, tmxa)], inc=sP)
                taha = P.do("scalar",
                            lambda e: e.copy(ha, PO[0:64, 0:256]),
                            wait=[(sP, tpha)], inc=sA)

                tmxg = mk_mix("x_g", [(sP, tpha)])

                def f_hg(e):
                    last = None
                    for k in range(NKB):
                        last = e.matmul(PCD[:, 0:256],
                                        g1b[:, k * 128:(k + 1) * 128],
                                        kb(sqt, k), start=(k == 0),
                                        stop=(k == 7), skip_group_check=True)
                    return last
                tphg = P.do("tensor", f_hg, wait=[(sV, tmxg)], inc=sP)

                def f_hg2(e):
                    e.activation(tmpA, PCD[:, 0:256], AF.Tanh, scale=0.5)
                    return None
                tahg0 = P.do("scalar", f_hg2, wait=[(sP, tphg)])
                tahg = P.do("scalar",
                            lambda e: e.activation(hg, tmpA, AF.Copy,
                                                   scale=0.5, bias=0.5),
                            inc=sA)

                t_wev = 0
                for mb in range(8):
                    ps = PAB[:, 0:256] if mb % 2 == 0 else PAB[:, 512:768]
                    w = [(sA, tahw)] if mb < 2 else [(sA, t_wev)]
                    tpw = P.do("tensor",
                               lambda e, mb=mb, ps=ps: e.matmul(
                                   ps, w2b[:, mb * 128:(mb + 1) * 128], hw,
                                   start=True, stop=True,
                                   skip_group_check=True),
                               wait=w, inc=sP)

                    def f_W(e, mb=mb, ps=ps):
                        e.activation(tmpC, ps, AF.Tanh,
                                     bias=aux[:, mb:mb + 1], scale=0.5)
                        return e.activation(kb(Wt[:], mb), tmpC, AF.Copy,
                                            scale=EMB * 0.5, bias=EMB * 0.5)
                    t_wev = P.do("scalar", f_W, wait=[(sP, tpw)], inc=sA)

                t_aev = 0
                for mb in range(8):
                    ps = PCD[:, 256:512] if mb % 2 == 0 else PCD[:, 512:768]
                    w = [(sA, taha), (sA, tahg)] if mb < 2 else [(sV, t_aev)]
                    tpa = P.do("tensor",
                               lambda e, mb=mb, ps=ps: e.matmul(
                                   ps, a2b[:, mb * 128:(mb + 1) * 128], ha,
                                   start=True, stop=True,
                                   skip_group_check=True),
                               wait=w, inc=sP)
                    wa2 = [(sP, tpa)] if mb == 0 else [(sP, tpa),
                                                         (sV, t_aev)]
                    ta2 = P.do("scalar",
                               lambda e, mb=mb, ps=ps: e.activation(
                                   tmpC, ps, AF.Tanh,
                                   bias=aux[:, 8 + mb:9 + mb], scale=0.5),
                               wait=wa2, inc=sA)
                    t_aev = P.do("vector",
                                 lambda e, mb=mb: e.tensor_scalar(
                                     kb(dxt, mb), tmpC, 0.5, 0.5, ALU.mult,
                                     ALU.add),
                                 wait=[(sA, ta2)], inc=sV)

                t_gev = 0
                for mb in range(8):
                    ps = PCD[:, 256:512] if mb % 2 == 0 else PCD[:, 512:768]
                    w = [(sV, t_aev)] if mb < 2 else [(sV, t_gev)]
                    tpg = P.do("tensor",
                               lambda e, mb=mb, ps=ps: e.matmul(
                                   ps, g2b[:, mb * 128:(mb + 1) * 128], hg,
                                   start=True, stop=True,
                                   skip_group_check=True),
                               wait=w, inc=sP)
                    t_gev = P.do("vector",
                                 lambda e, mb=mb, ps=ps: e.tensor_copy(
                                     kb(m_g[:], mb), ps),
                                 wait=[(sP, tpg)], inc=sV)

                def f_t1(e):
                    last = None
                    for k in range(NKB):
                        last = e.tensor_scalar(kb(m_kk[:], k), kb(m_kraw[:], k),
                                               vcol("k_k", k), None, ALU.mult)
                    return last
                tkk = P.do("vector", f_t1, wait=[(sV, ek[0]), (sV, ek[1])],
                           inc=sV)
                tkq = P.do("scalar",
                           lambda e: e.activation(sqt, m_kk[:], AF.Square),
                           wait=[(sV, tkk), (sV, t_gev), (sP, tphg)], inc=sA)

                def f_ksum(e):
                    last = None
                    for k in range(NKB):
                        last = e.matmul(PS[0:16, 0:256],
                                        e16[:, k * 16:(k + 1) * 16],
                                        kb(sqt, k), start=(k == 0),
                                        stop=(k == 7), skip_group_check=True)
                    return last
                tks = P.do("tensor", f_ksum, wait=[(sA, tkq)], inc=sP)

                def f_kn1(e):
                    e.activation(tmpA[0:16, :], PS[0:16, 0:256], AF.Ln)
                    return e.activation(tmpA[0:16, :], tmpA[0:16, :], AF.Exp,
                                        scale=0.5)
                tkn = P.do("scalar", f_kn1, wait=[(sP, tks)], inc=sA)

                def f_kn2(e):
                    e.tensor_scalar_max(tmpA[0:16, :], tmpA[0:16, :], 1e-12)
                    e.reciprocal(tmpB[0:16, :], tmpA[0:16, :])
                    e.tensor_copy(sthi[0:16, 0:256], tmpB[0:16, :])
                    e.tensor_sub(tmpB[0:16, :], tmpB[0:16, :],
                                 sthi[0:16, 0:256])
                    return e.tensor_copy(sthi[0:16, 256:512], tmpB[0:16, :])
                tkn2 = P.do("vector", f_kn2, wait=[(sA, tkn)], inc=sV)

                t_kbev = 0
                for k in range(NKB):
                    def f_bc(e, k=k):
                        fsl = f16[:, k * 128:(k + 1) * 128]
                        e.matmul(PO[:, 0:256], fsl, sthi[0:16, 0:256],
                                 start=True, stop=False, skip_group_check=True)
                        return e.matmul(PO[:, 0:256], fsl,
                                        sthi[0:16, 256:512], start=False,
                                        stop=True, skip_group_check=True)
                    w = [(sV, tkn2)] if k == 0 else [(sV, t_kbev)]
                    tbc = P.do("tensor", f_bc, wait=w, inc=sP)

                    def f_kb2(e, k=k):
                        e.tensor_mul(kb(m_kk[:], k), kb(m_kk[:], k),
                                     PO[:, 0:256])
                        e.tensor_scalar(tmpA, kb(dxt, k), vcol("k_a", k),
                                        aux[:, 16 + k:17 + k], ALU.mult,
                                        ALU.add)
                        e.tensor_mul(kb(m_kp[:], k), kb(m_kraw[:], k), tmpA)
                        return e.tensor_mul(kb(m_b[:], k), kb(m_kk[:], k),
                                            kb(dxt, k))
                    t_kbev = P.do("vector", f_kb2, wait=[(sP, tbc)], inc=sV)

                if "x1" in dbg_d and st == 0:
                    dma(stsl(dbg_d["x1"].rearrange("(a p) t -> p a t", p=128),
                             0), r3(x1t), wait=[(sV, tln1)])
                    for nm, tl in (("r", m_r), ("kp", m_kp), ("kk", m_kk),
                                   ("bb", m_b), ("v", m_v), ("g", m_g)):
                        if nm in dbg_d:
                            dma(stsl(dbg_d[nm].rearrange(
                                "(a p) t -> p a t", p=128), 0),
                                r3(tl), wait=[(sV, t_kbev)], eng="gpsimd")
                    if "W" in dbg_d:
                        dma(stsl(dbg_d["W"].rearrange(
                            "(a p) t -> p a t", p=128), 0),
                            r3(Wt), wait=[(sA, t_wev), (sV, t_kbev)])

                ARr = r3(AR)
                BSr = r3(BS)
                cbr = cbuf.rearrange("p (a q) -> p a q", a=8)
                cir = r3(cibuf)
                ktr = r3(ktil)
                scr = r3(scum)
                sol = [r3(sol0), r3(sol1)]
                tsu = 0
                for ch in range(NCH):
                    c0 = ch * LCH

                    def chsl(tile, k=None, c0=c0):
                        rr = tile.rearrange("p (a t) -> p a t", a=8)
                        if k is None:
                            return rr[:, :, c0:c0 + LCH]
                        return rr[:, k, c0:c0 + LCH]

                    def f_cum(e, c0=c0):
                        last = None
                        for k in range(NKB):
                            last = e.tensor_tensor_scan(
                                scr[:, k, :], chsl(Wt[:], k), zer, 0.0,
                                ALU.add, ALU.add)
                        return last
                    wcu = [(sV, t_kbev), (sA, t_wev)] if ch == 0 \
                        else [(sV, tsu)]
                    tcu = P.do("vector", f_cum, wait=wcu, inc=sV)
                    ton = P.do("vector",
                               lambda e: e.memset(cbr[:, :, 0:1], 1.0),
                               inc=sV)

                    def f_exp(e):
                        e.activation(cbr[:, :, 1:65], scr[:], AF.Exp,
                                     scale=-1.0)
                        e.activation(cir[:], scr[:], AF.Exp, scale=1.0)
                        return e.activation(
                            clast.rearrange("p (a o) -> p a o", o=1),
                            scr[:, :, 63:64], AF.Exp, scale=-1.0)
                    tex = P.do("scalar", f_exp,
                               wait=[(sV, tcu), (sV, ton)], inc=sA)

                    def f_scales(e, c0=c0):
                        e.scalar_tensor_tensor(ARr[:, :, 0:64],
                                               chsl(m_kk[:]), -1.0,
                                               cbr[:, :, 0:64], ALU.mult,
                                               ALU.mult)
                        e.tensor_mul(ARr[:, :, 64:128], chsl(m_r[:]),
                                     cbr[:, :, 1:65])
                        e.tensor_mul(BSr[:, :, 0:64], chsl(m_b[:]), cir[:])
                        return e.tensor_mul(ktr[:], chsl(m_kp[:]), cir[:])
                    tsc = P.do("vector", f_scales, wait=[(sA, tex)], inc=sV)
                    if st == 0 and ch == 0 and "sar" in dbg_d:
                        for nm, tl in (("sar", AR), ("sbs", BS),
                                       ("skt", ktil), ("scb", cbuf),
                                       ("sci", cibuf)):
                            dma(dbg_d[nm], tl, wait=[(sV, tsc)],
                                eng="gpsimd")
                        bar2()

                    def f_tr(e, c0=c0):
                        last = None
                        for k in range(NKB):
                            for hh in range(2):
                                p0 = hh * 64
                                tp = (p0, p0) if hh else None
                                last = e.transpose(
                                    PT[p0:p0 + 64, k * 64:(k + 1) * 64],
                                    chsl(m_v[:], k)[p0:p0 + 64, :],
                                    idb[p0:p0 + 64, :], tile_position=tp)
                                last = e.transpose(
                                    PT[p0:p0 + 64,
                                       512 + k * 64:512 + (k + 1) * 64],
                                    BSr[:, k, 0:64][p0:p0 + 64, :],
                                    idb[p0:p0 + 64, :], tile_position=tp)
                                last = e.transpose(
                                    PT2[p0:p0 + 64, k * 64:(k + 1) * 64],
                                    ktr[:, k, :][p0:p0 + 64, :],
                                    idb[p0:p0 + 64, :], tile_position=tp)
                        return last
                    ttr = P.do("tensor", f_tr, wait=[(sV, tsc)], inc=sP)

                    def f_trev(e):
                        e.tensor_copy(vbtm, PT[:])
                        return e.tensor_copy(ktm, PT2[:, 0:512])
                    ttre = P.do("vector", f_trev, wait=[(sP, ttr)], inc=sV)

                    def f_g1(e):
                        last = None
                        for k in range(NKB):
                            for hh in range(2):
                                p0 = hh * 64
                                tp = (p0, p0) if hh else None
                                last = e.matmul(
                                    PAB[p0:p0 + 64, k * 128:k * 128 + 64],
                                    ARr[:, k, 0:64][p0:p0 + 64, :],
                                    BSr[:, k, 0:64][p0:p0 + 64, :],
                                    start=(k in (0, 4)),
                                    stop=True, tile_position=tp,
                                    skip_group_check=True)
                                last = e.matmul(
                                    PAB[p0:p0 + 64, k * 128 + 64:k * 128 + 128],
                                    ARr[:, k, 0:64][p0:p0 + 64, :],
                                    BSr[:, k, 64:128][p0:p0 + 64, :],
                                    start=False,
                                    stop=True, tile_position=tp,
                                    skip_group_check=True)
                        return last
                    tg1 = P.do("tensor", f_g1, wait=[(sV, tsc)], inc=sP)

                    def f_g1ev(e):
                        last = None
                        for k in range(NKB):
                            last = e.tensor_mul(sol[0][:, k, 64:128],
                                                PAB[:, k * 128:k * 128 + 64],
                                                mk1)
                        return last
                    tg1e = P.do("vector", f_g1ev, wait=[(sP, tg1)], inc=sV)

                    te_w = tg1e
                    for wave in range(2):
                        def f_g2(e, wave=wave):
                            last = None
                            for i, k in enumerate(range(wave * 4,
                                                        wave * 4 + 4)):
                                off = i * 256
                                for hh in range(2):
                                    p0 = hh * 64
                                    tp = (p0, p0) if hh else None
                                    for half in range(2):
                                        last = e.matmul(
                                            PCD[p0:p0 + 64,
                                                off + half * 64:
                                                off + half * 64 + 64],
                                            BSr[:, k, 0:64][p0:p0 + 64, :],
                                            ARr[:, k, half * 64:
                                                half * 64 + 64][p0:p0 + 64, :],
                                            start=(half == 0 and i in (0, 2)),
                                            stop=True, tile_position=tp,
                                            skip_group_check=True)
                                    for half in range(2):
                                        last = e.matmul(
                                            PCD[p0:p0 + 64,
                                                off + 128 + half * 64:
                                                off + 128 + half * 64 + 64],
                                            ktr[:, k, :][p0:p0 + 64, :],
                                            ARr[:, k, half * 64:
                                                half * 64 + 64][p0:p0 + 64, :],
                                            start=False, stop=True,
                                            tile_position=tp,
                                            skip_group_check=True)
                            return last
                        tw = P.do("tensor", f_g2,
                                  wait=[(sV, te_w)] if wave else
                                  [(sV, tsc)], inc=sP)

                        def f_g2ev(e, wave=wave):
                            last = None
                            for i, k in enumerate(range(wave * 4,
                                                        wave * 4 + 4)):
                                off = i * 256
                                e.tensor_mul(sol[0][:, k, 128:192],
                                             PCD[:, off:off + 64],
                                             mk4[:, 0:64])
                                last = e.tensor_mul(
                                    gp3[:, k * 192:k * 192 + 192],
                                    PCD[:, off + 64:off + 256],
                                    mk4[:, 64:256])
                            return last
                        te_w = P.do("vector", f_g2ev, wait=[(sP, tw)], inc=sV)

                    def f_ost(e):
                        last = None
                        for k in range(NKB):
                            for hh in range(2):
                                p0 = hh * 64
                                tp = (p0, p0) if hh else None
                                last = e.matmul(
                                    PO[p0:p0 + 64, k * 64:(k + 1) * 64],
                                    ARr[:, k, 64:128][p0:p0 + 64, :],
                                    BSr[:, k, 64:128][p0:p0 + 64, :],
                                    start=(k == 0), stop=True,
                                    tile_position=tp, skip_group_check=True)
                        return last
                    tos = P.do("tensor", f_ost,
                               wait=[(sV, tsc), (sV, t_kbev)], inc=sP)

                    def f_kap(e):
                        last = None
                        for k in range(NKB):
                            for hh in range(2):
                                p0 = hh * 64
                                tp = (p0, p0) if hh else None
                                last = e.matmul(
                                    PAB[p0:p0 + 64,
                                        k * 128 + 64:k * 128 + 128],
                                    gp3[:, k * 192 + 64:
                                        k * 192 + 128][p0:p0 + 64, :],
                                    vbtm[:, k * 64:(k + 1) * 64][p0:p0 + 64, :],
                                    start=False, stop=True, tile_position=tp,
                                    skip_group_check=True)
                        return last
                    tka = P.do("tensor", f_kap,
                               wait=[(sV, te_w), (sV, tg1e), (sV, ttre)],
                               inc=sP)

                    def f_x0(e):
                        last = None
                        for k in range(NKB):
                            last = e.tensor_copy(
                                sol[0][:, k, 0:64],
                                PAB[:, k * 128 + 64:k * 128 + 128])
                        return last
                    tlv = P.do("vector", f_x0, wait=[(sP, tka)], inc=sV)
                    if st == 0 and ch == 0 and "spre" in dbg_d:
                        dma(dbg_d["spre"], sol0, wait=[(sV, tlv)],
                            eng="gpsimd")
                        bar2()

                    for j in range(6):
                        cb = j % 2
                        nb = 1 - cb
                        dual = (j <= 3)

                        def f_slv(e, cb=cb, dual=dual, j=j):
                            last = None
                            for k in range(NKB):
                                for hh in range(2):
                                    p0 = hh * 64
                                    tp = (p0, p0) if hh else None
                                    nh = 2 if dual else 1
                                    for half in range(nh):
                                        last = e.matmul(
                                            PAB[p0:p0 + 64,
                                                k * 128 + half * 64:
                                                k * 128 + half * 64 + 64],
                                            sol[cb][:, k, 128:192][p0:p0 + 64, :],
                                            sol[cb][:, k, half * 64:
                                                    half * 64 + 64][p0:p0 + 64, :],
                                            start=(half == 0 and k in (0, 4)),
                                            stop=True, tile_position=tp,
                                            skip_group_check=True)
                                    if j <= 4:
                                        last = e.matmul(
                                            PCD[p0:p0 + 64,
                                                k * 64:(k + 1) * 64],
                                            sol[cb][:, k, 64:128][p0:p0 + 64, :],
                                            sol[cb][:, k, 128:192][p0:p0 + 64, :],
                                            start=(k == 0),
                                            stop=True, tile_position=tp,
                                            skip_group_check=True)
                            return last
                        tslv = P.do("tensor", f_slv, wait=[(sV, tlv)], inc=sP)

                        def f_sev(e, cb=cb, nb=nb, dual=dual, j=j):
                            if dual:
                                src = sol[cb].rearrange(
                                    "p a (u q) -> p a u q", u=4)
                                in1 = src[:, :, 0:4:3, :]
                                e.tensor_add(
                                    sol[nb][:, :, 0:128].rearrange(
                                        "p a (u q) -> p a u q", u=2),
                                    PAB.rearrange("p (a u q) -> p a u q",
                                                  a=8, u=2), in1)
                            else:
                                e.tensor_add(
                                    sol[nb][:, :, 0:64],
                                    PAB.rearrange("p (a q) -> p a q",
                                                  a=8)[:, :, 0:64],
                                    sol[cb][:, :, 0:64])
                            if j <= 4:
                                return e.tensor_copy(
                                    sol[nb][:, :, 128:192],
                                    PCD[:, 0:512].rearrange(
                                        "p (a q) -> p a q", q=64))
                            return e.tensor_copy(sol[nb][:, :, 64:128],
                                                 sol[cb][:, :, 64:128])
                        tlv = P.do("vector", f_sev, wait=[(sP, tslv)], inc=sV)

                    if st == 0 and ch == 0 and "ssol" in dbg_d:
                        dma(dbg_d["ssol"], sol0, wait=[(sV, tlv)],
                            eng="gpsimd")
                        dma(dbg_d["sgp3"], gp3, wait=[(sV, tlv)],
                            eng="gpsimd")
                        bar2()
                    ubuf = sol[0]

                    def f_outap(e):
                        last = None
                        for k in range(NKB):
                            for hh in range(2):
                                p0 = hh * 64
                                tp = (p0, p0) if hh else None
                                last = e.matmul(
                                    PO[p0:p0 + 64, k * 64:(k + 1) * 64],
                                    gp3[:, k * 192:k * 192 + 64][p0:p0 + 64, :],
                                    ubuf[:, k, 0:64][p0:p0 + 64, :],
                                    start=False, stop=True, tile_position=tp,
                                    skip_group_check=True)
                                last = e.matmul(
                                    PO[p0:p0 + 64, k * 64:(k + 1) * 64],
                                    gp3[:, k * 192 + 128:
                                        k * 192 + 192][p0:p0 + 64, :],
                                    vbtm[:, k * 64:(k + 1) * 64][p0:p0 + 64, :],
                                    start=False, stop=True, tile_position=tp,
                                    skip_group_check=True)
                        return last
                    toa = P.do("tensor", f_outap, wait=[(sV, tlv)], inc=sP)

                    def f_ds(e):
                        last = None
                        for k in range(NKB):
                            for hh in range(2):
                                p0 = hh * 64
                                tp = (p0, p0) if hh else None
                                last = e.matmul(
                                    PS[p0:p0 + 64, k * 64:(k + 1) * 64],
                                    vbtm[:, 512 + k * 64:
                                         512 + (k + 1) * 64][p0:p0 + 64, :],
                                    ubuf[:, k, 0:64][p0:p0 + 64, :],
                                    start=(k == 0), stop=True,
                                    tile_position=tp, skip_group_check=True)
                                last = e.matmul(
                                    PS[p0:p0 + 64, k * 64:(k + 1) * 64],
                                    ktm[:, k * 64:(k + 1) * 64][p0:p0 + 64, :],
                                    vbtm[:, k * 64:(k + 1) * 64][p0:p0 + 64, :],
                                    start=False, stop=True, tile_position=tp,
                                    skip_group_check=True)
                        return last
                    tds = P.do("tensor", f_ds, wait=[(sV, tlv)], inc=sP)

                    tooe = P.do("vector", lambda e: e.tensor_copy(otm, PO[:]),
                                wait=[(sP, toa)], inc=sV)
                    if st == 0 and ch == 0 and "sotm" in dbg_d:
                        dma(dbg_d["sotm"], otm, wait=[(sV, tooe)],
                            eng="gpsimd")
                        bar2()

                    def f_otr(e):
                        last = None
                        for k in range(NKB):
                            for hh in range(2):
                                p0 = hh * 64
                                tp = (p0, p0) if hh else None
                                last = e.transpose(
                                    PT2[p0:p0 + 64,
                                        512 + k * 64:512 + (k + 1) * 64],
                                    otm[p0:p0 + 64, k * 64:(k + 1) * 64],
                                    idb[p0:p0 + 64, :], tile_position=tp)
                        return last
                    totr = P.do("tensor", f_otr, wait=[(sV, tooe)], inc=sP)
                    tofm = P.do(
                        "vector",
                        lambda e, c0=c0: e.tensor_copy(
                            r3(m_o[:])[:, :, c0:c0 + LCH],
                            PT2.rearrange("p (w a q) -> p w a q",
                                          w=2, a=8)[:, 1, :, :]),
                        wait=[(sP, totr)], inc=sV)

                    def f_sup(e):
                        sf = r3(sfp)
                        e.tensor_add(sfp, sfp, PS[:])
                        for k in range(NKB):
                            e.tensor_scalar(sf[:, k, :], sf[:, k, :],
                                            clast[:, k:k + 1], None, ALU.mult)
                        return e.tensor_copy(BSr[:, :, 64:128], sfp)
                    tsu = P.do("vector", f_sup,
                               wait=[(sP, tds), (sV, tofm)], inc=sV)

                if "o" in dbg_d:
                    dma(stsl(dbg_d["o"].rearrange("(a p) t -> p a t", p=128),
                             st), r3(m_o), wait=[(sV, tsu)], eng="gpsimd")

                tq3 = P.do("scalar",
                           lambda e: e.activation(sqt, m_o[:], AF.Square),
                           wait=[(sV, tsu)], inc=sA)

                def f_osum(e):
                    for k in range(NKB):
                        e.matmul(PS[0:16, 0:256], e16[:, k * 16:(k + 1) * 16],
                                 kb(m_o[:], k), start=(k == 0), stop=False,
                                 skip_group_check=True)
                    last = None
                    for k in range(NKB):
                        last = e.matmul(PS[0:16, 256:512],
                                        e16[:, k * 16:(k + 1) * 16],
                                        kb(sqt, k), start=False, stop=(k == 7),
                                        skip_group_check=True)
                    return last
                tp5 = P.do("tensor", f_osum, wait=[(sA, tq3)], inc=sP)

                def f_gnst(e):
                    e.tensor_scalar(tmpA[0:16, :], PS[0:16, 0:256], 1.0 / 64,
                                    None, ALU.mult)
                    e.tensor_mul(tmpB[0:16, :], tmpA[0:16, :], tmpA[0:16, :])
                    e.scalar_tensor_tensor(tmpB[0:16, :], PS[0:16, 256:512],
                                           1.0 / 64, tmpB[0:16, :], ALU.mult,
                                           ALU.subtract)
                    e.tensor_copy(sthi[0:16, 0:256], tmpA[0:16, :])
                    return e.tensor_sub(tmpA[0:16, :], tmpA[0:16, :],
                                        sthi[0:16, 0:256])
                tv7 = P.do("vector", f_gnst, wait=[(sP, tp5)], inc=sV)

                def f_gnr(e):
                    e.activation(tmpC[0:16, :], tmpB[0:16, :], AF.Ln,
                                 bias=aux[0:16, 25:26])
                    return e.activation(tmpC[0:16, :], tmpC[0:16, :], AF.Exp,
                                        scale=-0.5)
                ta3 = P.do("scalar", f_gnr, wait=[(sV, tv7)], inc=sA)

                def f_gnhl(e):
                    e.tensor_copy(sthi[0:16, 256:512], tmpA[0:16, :])
                    e.tensor_copy(sthi[0:16, 512:768], tmpC[0:16, :])
                    e.tensor_sub(tmpC[0:16, :], tmpC[0:16, :],
                                 sthi[0:16, 512:768])
                    return e.tensor_copy(sthi[0:16, 768:1024], tmpC[0:16, :])
                tv8 = P.do("vector", f_gnhl, wait=[(sA, ta3)], inc=sV)

                def f_rkk(e):
                    last = None
                    for k in range(NKB):
                        last = e.scalar_tensor_tensor(
                            kb(m_kk[:], k), kb(m_r[:], k), vcol("r_k", k),
                            kb(m_kp[:], k), ALU.mult, ALU.mult)
                    return last
                tv10 = P.do("vector", f_rkk, wait=[(sV, tv8)], inc=sV)

                def f_qsum(e):
                    last = None
                    for k in range(NKB):
                        last = e.matmul(PS[0:16, 0:256],
                                        e16[:, k * 16:(k + 1) * 16],
                                        kb(m_kk[:], k), start=(k == 0),
                                        stop=(k == 7), skip_group_check=True)
                    return last
                tp6 = P.do("tensor", f_qsum, wait=[(sV, tv10)], inc=sP)

                def f_qhl(e):
                    e.tensor_copy(sthi[0:16, 1024:1280], PS[0:16, 0:256])
                    e.tensor_sub(tmpA[0:16, :], PS[0:16, 0:256],
                                 sthi[0:16, 1024:1280])
                    return e.tensor_copy(sthi[0:16, 1280:1536],
                                         tmpA[0:16, :])
                tv11 = P.do("vector", f_qhl, wait=[(sP, tp6)], inc=sV)

                t_pev = 0
                for k in range(NKB):
                    def f_bc2(e, k=k):
                        fsl = f16[:, k * 128:(k + 1) * 128]
                        e.matmul(PO[:, 0:256], fsl, sthi[0:16, 0:256],
                                 start=True, stop=False, skip_group_check=True)
                        e.matmul(PO[:, 0:256], fsl, sthi[0:16, 256:512],
                                 start=False, stop=True, skip_group_check=True)
                        e.matmul(PO[:, 256:512], fsl, sthi[0:16, 512:768],
                                 start=False, stop=False,
                                 skip_group_check=True)
                        e.matmul(PO[:, 256:512], fsl, sthi[0:16, 768:1024],
                                 start=False, stop=True, skip_group_check=True)
                        e.matmul(PCD[:, 0:256], fsl, sthi[0:16, 1024:1280],
                                 start=True, stop=False, skip_group_check=True)
                        return e.matmul(PCD[:, 0:256], fsl,
                                        sthi[0:16, 1280:1536], start=False,
                                        stop=True, skip_group_check=True)
                    w = [(sV, tv11)] if k == 0 else [(sV, t_pev)]
                    tb2 = P.do("tensor", f_bc2, wait=w, inc=sP)

                    def f_pap(e, k=k):
                        e.tensor_sub(tmpA, kb(m_o[:], k), PO[:, 0:256])
                        e.scalar_tensor_tensor(tmpB, tmpA, vcol("gn_w", k),
                                               PO[:, 256:512], ALU.mult,
                                               ALU.mult)
                        e.tensor_mul(tmpA, kb(m_v[:], k), PCD[:, 0:256])
                        e.scalar_tensor_tensor(tmpA, tmpA, vcol("gn_b", k),
                                               tmpB, ALU.add, ALU.add)
                        return e.tensor_mul(kb(m_o[:], k), tmpA,
                                            kb(m_g[:], k))
                    t_pev = P.do("vector", f_pap, wait=[(sP, tb2)], inc=sV)

                t_x2ev = 0
                for mb in range(8):
                    ps = PAB[:, 0:256] if mb % 2 == 0 else PAB[:, 512:768]
                    w = [(sV, t_pev), (sDg, t_wo_st)] if mb < 2 \
                        else [(sV, t_x2ev)]

                    def f_wo(e, mb=mb, ps=ps):
                        last = None
                        for k in range(NKB):
                            last = e.matmul(
                                ps, slotA[:, (k * 8 + mb) * 128:
                                          (k * 8 + mb) * 128 + 128],
                                kb(m_o[:], k), start=(k == 0), stop=(k == 7),
                                skip_group_check=True)
                        return last
                    two = P.do("tensor", f_wo, wait=w, inc=sP)
                    t_x2ev = P.do("vector",
                                  lambda e, mb=mb, ps=ps: e.tensor_add(
                                      kb(x1t[:], mb), kb(x1t[:], mb), ps),
                                  wait=[(sP, two)], inc=sV)

                dma(stsl(x2_r, st), r3(x1t), wait=[(sV, t_x2ev)])
                if st + 1 < nst:
                    t_wr_st = dma(slotA_r,
                                  wr_d.rearrange("(a p) m -> p a m", p=128),
                                  wait=[(sV, t_x2ev)], eng="gpsimd")
                if "x2" in dbg_d:
                    dma(stsl(dbg_d["x2"].rearrange("(a p) t -> p a t", p=128),
                             st), r3(x1t), wait=[(sV, t_x2ev)])
            return None

        def emit_pb(P, ctx, dma, bar2, sDs, sDg, sP, sV, sA):
            (x0a, x0b, x1t, Wt, scum, tmpA, tmpB, tmpC, x0f, x0g, x1f, xnt,
             dxt, sqt, m_r, m_kraw, m_kp, m_kk, m_b, m_v, m_g, m_o, hw, ha,
             hg, cxn, cbuf, cibuf, clast, AR, BS, ktil, vbtm, ktm, gp3, sol0,
             sol1, otm, sfp, wkb, wvb, slotA, w1b, a1b, g1b, w2b, a2b, g2b,
             PAB, PT, PT2, PCD, PS, PO) = ctx
            x2t = x0a
            x0t = x0b
            x2f = x0f
            xft = x1f
            kft = xnt
            h1f = x1t
            h1b = sqt
            sqb = m_g
            acA = hg
            acB = m_v
            idf16 = vbtm
            bar2()
            t_i16 = dma(tmpC[0:16, 0:16], if16_d)
            t_i128 = dma(Wt[:, 0:128], if128_d)
            t_rw = dma(a1b[:, 0:128].rearrange("p (a m) -> p a m", a=8),
                       rw_d.rearrange("(a p) m -> p a m", p=128), eng="gpsimd")
            t_s15 = dma(w2b[0:16, 0:128], s15_d)
            t_rb = dma(aux[0:16, 28:29], rb_d)
            tz = P.do("vector", lambda e: e.memset(cxn, 0.0), inc=sV)

            def kb(ap2d, k, w=ST):
                return ap2d[:, k * w:(k + 1) * w]

            def r3(tile, a=8):
                return tile.rearrange("p (a t) -> p a t", a=a)

            def stsl(rr, st, w=ST):
                return rr[:, :, st * w:(st + 1) * w]

            cmb_r = cmb_d
            t_wf = [0, 0, 0, 0]
            for st in range(nst):
                bar2()
                t_x2 = dma(r3(x2t), stsl(x2_r, st))
                t_x2b = dma(r3(x2f), stsl(x2_r, st), eng="gpsimd")
                t_x0 = dma(r3(x0t), stsl(x0_r, st))
                wx = [(sDs, t_x2), (sDg, t_x2b)]
                if st == 0:
                    wx += [(sDs, t_i16), (sDs, t_i128), (sDg, t_rw),
                           (sDs, t_rb), (sV, tz), (sDs, t_s15)]

                tq = P.do("scalar",
                          lambda e: e.activation(sqb, x2f, AF.Square),
                          wait=wx, inc=sA)

                def f_sums(e):
                    for k in range(NKB):
                        e.matmul(PS[0:1, 0:256], onc, kb(x2f, k),
                                 start=(k == 0), stop=False,
                                 skip_group_check=True)
                    last = None
                    for k in range(NKB):
                        last = e.matmul(PS[0:1, 256:512], onc, kb(sqb, k),
                                        start=False, stop=(k == 7),
                                        skip_group_check=True)
                    return last
                tp = P.do("tensor", f_sums, wait=wx + [(sA, tq)], inc=sP)

                def f_stats(e, n=C):
                    e.tensor_scalar(tmpA[0:1, :], PS[0:1, 0:256], 1.0 / n,
                                    None, ALU.mult)
                    e.tensor_mul(tmpB[0:1, :], tmpA[0:1, :], tmpA[0:1, :])
                    e.scalar_tensor_tensor(tmpB[0:1, :], PS[0:1, 256:512],
                                           1.0 / n, tmpB[0:1, :], ALU.mult,
                                           ALU.subtract)
                    e.tensor_copy(sthi[0:1, 0:256], tmpA[0:1, :])
                    return e.tensor_sub(tmpA[0:1, :], tmpA[0:1, :],
                                        sthi[0:1, 0:256])
                tv = P.do("vector", f_stats, wait=[(sP, tp)], inc=sV)

                def f_rstd(e):
                    e.activation(tmpB[0:1, :], tmpB[0:1, :], AF.Ln,
                                 bias=aux[0:1, 24:25])
                    return e.activation(tmpB[0:1, :], tmpB[0:1, :], AF.Exp,
                                        scale=-0.5)
                ta = P.do("scalar", f_rstd, wait=[(sV, tv)], inc=sA)

                def f_hl(e):
                    e.tensor_copy(sthi[0:1, 256:512], tmpA[0:1, :])
                    e.tensor_copy(sthi[0:1, 512:768], tmpB[0:1, :])
                    e.tensor_sub(tmpB[0:1, :], tmpB[0:1, :],
                                 sthi[0:1, 512:768])
                    return e.tensor_copy(sthi[0:1, 768:1024], tmpB[0:1, :])
                tv2 = P.do("vector", f_hl, wait=[(sA, ta)], inc=sV)

                def f_bcast(e):
                    e.matmul(PO[:, 0:256], onr, sthi[0:1, 0:256],
                             start=True, stop=False, skip_group_check=True)
                    e.matmul(PO[:, 0:256], onr, sthi[0:1, 256:512],
                             start=False, stop=True, skip_group_check=True)
                    e.matmul(PO[:, 256:512], onr, sthi[0:1, 512:768],
                             start=False, stop=False, skip_group_check=True)
                    return e.matmul(PO[:, 256:512], onr,
                                    sthi[0:1, 768:1024], start=False,
                                    stop=True, skip_group_check=True)
                tp2 = P.do("tensor", f_bcast, wait=[(sV, tv2)], inc=sP)

                def f_xf(e):
                    last = None
                    for k in range(NKB):
                        e.tensor_sub(tmpA, kb(x2t, k), PO[:, 0:256])
                        e.scalar_tensor_tensor(tmpB, tmpA,
                                               vcol("ln_ffn_w", k),
                                               PO[:, 256:512], ALU.mult,
                                               ALU.mult)
                        last = e.tensor_scalar(kb(xft, k), tmpB,
                                               vcol("ln_ffn_b", k), None,
                                               ALU.add)
                    return last
                tv3 = P.do("vector", f_xf, wait=[(sP, tp2)], inc=sV)

                def f_kf(e):
                    xv = r3(xft)
                    dv = r3(kft)
                    e.tensor_sub(dv[:, :, 1:ST], xv[:, :, 0:ST - 1],
                                 xv[:, :, 1:ST])
                    e.tensor_sub(dv[:, :, 0:1],
                                 cxn.rearrange("p (a o) -> p a o", o=1),
                                 xv[:, :, 0:1])
                    e.tensor_copy(cxn.rearrange("p (a o) -> p a o", o=1),
                                  xv[:, :, ST - 1:ST])
                    last = None
                    for k in range(NKB):
                        last = e.scalar_tensor_tensor(
                            kb(kft, k), kb(kft, k), vcol("ffn_x_k", k),
                            kb(xft, k), ALU.mult, ALU.add)
                    return last
                tv4 = P.do("vector", f_kf, wait=[(sV, tv3)], inc=sV)

                t_ev = 0
                for hb in range(32):
                    par = hb % 2
                    w1sl = wkb[:, par * 1024:(par + 1) * 1024]
                    w2sl = wkb[:, 2048 + par * 1024:2048 + (par + 1) * 1024]
                    wwait = [] if hb < 2 else [(sP, t_dn)]
                    t_wf[par] = dma(
                        w1sl.rearrange("p (a m) -> p a m", a=8),
                        wf1_d.rearrange("(a p) m -> p a m",
                                        p=128)[:, :, hb * 128:(hb + 1) * 128],
                        wait=wwait, eng="gpsimd")
                    t_wf[2 + par] = dma(
                        w2sl, wf2_d[hb * 128:(hb + 1) * 128, :],
                        wait=[], eng="gpsimd")
                    actps = PS[:, 0:256] if par == 0 else PO[:, 0:256]

                    def f_act(e, w1sl=w1sl, actps=actps):
                        last = None
                        for k in range(NKB):
                            last = e.matmul(actps,
                                            w1sl[:, k * 128:(k + 1) * 128],
                                            kb(kft, k), start=(k == 0),
                                            stop=(k == 7),
                                            skip_group_check=True)
                        return last
                    wv_ = [(sV, tv4), (sDg, t_wf[par])]
                    if hb >= 2:
                        wv_.append((sA, t_rl))
                    tac = P.do("tensor", f_act, wait=wv_, inc=sP)
                    act = acA if par == 0 else acB

                    def f_relu(e, actps=actps, act=act):
                        return e.activation(act[0:128, 0:256], actps, AF.Relu)
                    t_rl = P.do("scalar", f_relu, wait=[(sP, tac)], inc=sA)

                    def f_sq2(e, act=act):
                        return e.tensor_mul(act[0:128, 0:256],
                                            act[0:128, 0:256],
                                            act[0:128, 0:256])
                    t_s2 = P.do("vector", f_sq2, wait=[(sA, t_rl)], inc=sV)

                    def f_dn(e, w2sl=w2sl, act=act, hb=hb):
                        last = None
                        for mb in range(8):
                            ps = (PAB[:, (mb % 4) * 256:(mb % 4) * 256 + 256]
                                  if mb < 4 else
                                  PCD[:, (mb % 4) * 256:(mb % 4) * 256 + 256])
                            last = e.matmul(
                                ps, w2sl[:, mb * 128:(mb + 1) * 128],
                                act[0:128, 0:256],
                                start=(hb == 0 and mb in (0, 2, 4, 6)),
                                stop=(hb == 31), skip_group_check=True)
                        return last
                    t_dn = P.do("tensor", f_dn,
                                wait=[(sV, t_s2), (sDg, t_wf[2 + par])],
                                inc=sP)

                def f_h1pre(e):
                    last = None
                    for mb in range(8):
                        ps = (PAB[:, (mb % 4) * 256:(mb % 4) * 256 + 256]
                              if mb < 4 else
                              PCD[:, (mb % 4) * 256:(mb % 4) * 256 + 256])
                        e.tensor_add(kb(x2t, mb), kb(x2t, mb), ps)
                        last = e.tensor_add(kb(x2t, mb), kb(x2t, mb),
                                            kb(x0t, mb))
                    return last
                tv5 = P.do("vector", f_h1pre, wait=[(sP, t_dn), (sDs, t_x0)],
                           inc=sV)

                def f_cp(e):
                    last = None
                    for k in range(NKB):
                        last = e.tensor_copy(kb(x2f, k), kb(x2t, k))
                    return last
                tv6 = P.do("vector", f_cp, wait=[(sV, tv5)], inc=sV)
                tq2 = P.do("scalar",
                           lambda e: e.activation(sqb, x2f, AF.Square),
                           wait=[(sV, tv6)], inc=sA)
                tp3 = P.do("tensor", f_sums, wait=[(sA, tq2)], inc=sP)
                tv = P.do("vector", f_stats, wait=[(sP, tp3)], inc=sV)
                ta = P.do("scalar", f_rstd, wait=[(sV, tv)], inc=sA)
                tv2 = P.do("vector", f_hl, wait=[(sA, ta)], inc=sV)
                tp4 = P.do("tensor", f_bcast, wait=[(sV, tv2)], inc=sP)

                def f_h1(e):
                    last = None
                    for k in range(NKB):
                        e.tensor_sub(tmpA, kb(x2t, k), PO[:, 0:256])
                        e.tensor_mul(kb(h1f, k), tmpA, PO[:, 256:512])
                        last = e.tensor_copy(kb(h1b, k), kb(h1f, k))
                    return last
                tv7 = P.do("vector", f_h1, wait=[(sP, tp4)], inc=sV)
                dma(stsl(h1_r, st), r3(h1f), wait=[(sV, tv7)])

                def f_lg(e):
                    last = None
                    for k in range(NKB):
                        last = e.matmul(PS[0:16, 0:256],
                                        a1b[:, k * 16:(k + 1) * 16],
                                        kb(h1b, k), start=(k == 0),
                                        stop=(k == 7), skip_group_check=True)
                    return last
                tp5 = P.do("tensor", f_lg, wait=[(sV, tv7)], inc=sP)
                tv8 = P.do("vector",
                           lambda e: e.tensor_scalar(
                               tmpA[0:16, :], PS[0:16, 0:256],
                               aux[0:16, 28:29], None, ALU.add),
                           wait=[(sP, tp5)], inc=sV)

                AX = mybir.AxisListType.X
                for half in range(2):
                    hsl = slice(half * 128, half * 128 + 128)
                    wtt = [(sV, tv8)]
                    if half == 1:
                        wtt.append((sV, tv11))

                    def f_tt1(e, hsl=hsl):
                        e.transpose(PO[:, 0:16], tmpA[0:16, hsl],
                                    tmpC[0:16, 0:16])
                        return e.transpose(PO[:, 32:48], tmpA[0:16, hsl],
                                           tmpC[0:16, 0:16])
                    tt1 = P.do("tensor", f_tt1, wait=wtt, inc=sP)
                    ta2 = P.do("scalar",
                               lambda e: e.activation(tmpB[:, 0:16],
                                                      PO[:, 0:16], AF.Exp),
                               wait=[(sP, tt1)], inc=sA)

                    def f_top2(v):
                        A2 = tmpB
                        B2 = tmpC[:, 100:200]
                        C2 = scum[:, 0:64]

                        def pad():
                            v.tensor_copy(B2[:, 50:52], B2[:, 52:54])
                            v.tensor_copy(B2[:, 52:54], B2[:, 54:56])
                        v.reduce_max(B2[:, 0:1], A2[:, 0:16], axis=AX)
                        pad()
                        v.reduce_sum(B2[:, 1:2], A2[:, 0:16], axis=AX)
                        pad()
                        v.tensor_scalar(A2[:, 16:17], B2[:, 0:1], -1.0,
                                        None, ALU.mult)
                        pad()
                        v.tensor_scalar(B2[:, 16:32], A2[:, 0:16],
                                        A2[:, 16:17], None, ALU.add)
                        pad()
                        v.tensor_scalar(A2[:, 32:48], B2[:, 16:32], 1e6,
                                        1.0, ALU.mult, ALU.add)
                        pad()
                        v.tensor_scalar_max(B2[:, 32:48], A2[:, 32:48], 0.0)
                        pad()
                        v.tensor_scalar_min(A2[:, 48:64], B2[:, 32:48], 1.0)
                        pad()
                        v.scalar_tensor_tensor(B2[:, 64:80], A2[:, 48:64],
                                               -1e6, A2[:, 0:16], ALU.mult,
                                               ALU.add)
                        pad()
                        v.reduce_max(A2[:, 17:18], B2[:, 64:80], axis=AX)
                        pad()
                        v.tensor_scalar(B2[:, 2:3], A2[:, 17:18], -1.0,
                                        None, ALU.mult)
                        pad()
                        v.tensor_tensor(C2[:, 0:1], B2[:, 0:1],
                                        A2[:, 17:18], ALU.add)
                        pad()
                        v.scalar_tensor_tensor(A2[:, 18:19], B2[:, 1:2],
                                               1e-6, C2[:, 0:1], ALU.mult,
                                               ALU.add)
                        pad()
                        v.reciprocal(B2[:, 3:4], A2[:, 18:19])
                        pad()
                        v.tensor_scalar(C2[:, 16:32], A2[:, 0:16],
                                        B2[:, 2:3], None, ALU.add)
                        pad()
                        v.tensor_scalar(B2[:, 16:32], C2[:, 16:32], 1e6,
                                        1.0, ALU.mult, ALU.add)
                        pad()
                        v.tensor_scalar_max(C2[:, 32:48], B2[:, 16:32], 0.0)
                        pad()
                        v.tensor_scalar_min(A2[:, 64:80], C2[:, 32:48], 1.0)
                        pad()
                        v.tensor_tensor(B2[:, 16:32], A2[:, 0:16],
                                        A2[:, 64:80], ALU.mult)
                        pad()
                        return v.tensor_scalar(A2[:, 80:96], B2[:, 16:32],
                                               B2[:, 3:4], None, ALU.mult)
                    tv10 = P.do("vector", f_top2, wait=[(sA, ta2)], inc=sV)

                    def f_tt2(e):
                        e.transpose(PS[0:16, 256:384], tmpB[:, 80:96],
                                    Wt[:, 0:128])
                        return e.transpose(PS[0:16, 384:512],
                                           tmpB[:, 80:96], Wt[:, 0:128])
                    tt2 = P.do("tensor", f_tt2, wait=[(sV, tv10)], inc=sP)
                    tv11 = P.do("vector",
                                lambda e, hsl=hsl: e.tensor_copy(
                                    tmpA[0:16, hsl], PS[0:16, 256:384]),
                                wait=[(sP, tt2)], inc=sV)
                dma(cmb_r[:, st * ST:(st + 1) * ST], tmpA[0:16, :],
                    wait=[(sV, tv11)])
                def f_c15(e):
                    e.tensor_copy(sthi[0:16, 1536:1792], tmpA[0:16, :])
                    e.tensor_sub(tmpB[0:16, :], tmpA[0:16, :],
                                 sthi[0:16, 1536:1792])
                    return e.tensor_copy(sthi[0:16, 1792:2048],
                                         tmpB[0:16, :])
                tv12 = P.do("vector", f_c15, wait=[(sV, tv11)], inc=sV)
                tbc15 = P.do("tensor",
                             lambda e: (
                                 e.matmul(PS[:, 256:512], w2b[0:16, 0:128],
                                          sthi[0:16, 1536:1792], start=True,
                                          stop=False, skip_group_check=True),
                                 e.matmul(PS[:, 256:512], w2b[0:16, 0:128],
                                          sthi[0:16, 1792:2048], start=False,
                                          stop=True,
                                          skip_group_check=True))[-1],
                             wait=[(sV, tv12)], inc=sP)

                def f_rt0(e):
                    last = None
                    for k in range(NKB):
                        last = e.tensor_mul(kb(x2f, k), kb(h1f, k),
                                            PS[:, 256:512])
                    return last
                tv13 = P.do("vector", f_rt0, wait=[(sP, tbc15)], inc=sV)
                dma(stsl(rtd_r, st), r3(x2f), wait=[(sV, tv13)],
                    eng="gpsimd")

        def emit_pc(P, ctx, dma, bar2, sDs, sDg, sP, sV, sA):
            (x0a, x0b, x1t, Wt, scum, tmpA, tmpB, tmpC, x0f, x0g, x1f, xnt,
             dxt, sqt, m_r, m_kraw, m_kp, m_kk, m_b, m_v, m_g, m_o, hw, ha,
             hg, cxn, cbuf, cibuf, clast, AR, BS, ktil, vbtm, ktm, gp3, sol0,
             sol1, otm, sfp, wkb, wvb, slotA, w1b, a1b, g1b, w2b, a2b, g2b,
             PAB, PT, PT2, PCD, PS, PO) = ctx
            hbf = x0f
            stg = x0a

            def kb(ap2d, k, w=ST):
                return ap2d[:, k * w:(k + 1) * w]

            def r3(tile, a=8):
                return tile.rearrange("p (a t) -> p a t", a=a)

            def stsl(rr, st, w=ST):
                return rr[:, :, st * w:(st + 1) * w]

            bar2()
            t_sel = dma(selb, sel_d)
            t_wg = [0, 0]
            t_wu = [0, 0]
            t_wd = [0, 0]
            for e_i in range(15):
                par = e_i % 2
                gsl = wkb[:, par * 4096:(par + 1) * 4096]
                usl = wvb[:, par * 4096:(par + 1) * 4096]
                dsl = slotA[:, par * 4096:(par + 1) * 4096]
                ww = [] if e_i < 2 else [(sP, t_lastmm[par])]
                t_wg[par] = dma(gsl.rearrange("p (a m) -> p a m", a=8),
                                egw_d[e_i].rearrange("(a p) m -> p a m",
                                                     p=128),
                                wait=ww, eng="gpsimd")
                t_wu[par] = dma(usl.rearrange("p (a m) -> p a m", a=8),
                                euw_d[e_i].rearrange("(a p) m -> p a m",
                                                     p=128), eng="gpsimd")
                t_wd[par] = dma(dsl.rearrange("p (a m) -> p a m", a=4),
                                edw_d[e_i].rearrange("(a p) m -> p a m",
                                                     p=128), eng="gpsimd")
                if e_i < 2:
                    t_lastmm = [0, 0]
                for st in range(nst):
                    bar2()
                    t_h = dma(r3(hbf), stsl(h1_r, st), eng="gpsimd")
                    t_cb = dma(tmpB[0:16, :],
                               cmb_d[:, st * ST:(st + 1) * ST])
                    def f_chl(e):
                        e.tensor_copy(sthi[0:16, 0:256], tmpB[0:16, :])
                        e.tensor_sub(tmpA[0:16, :], tmpB[0:16, :],
                                     sthi[0:16, 0:256])
                        return e.tensor_copy(sthi[0:16, 256:512],
                                             tmpA[0:16, :])
                    tch = P.do("vector", f_chl, wait=[(sDs, t_cb)], inc=sV)
                    tbc = P.do("tensor",
                               lambda e, e_i=e_i: (
                                   e.matmul(PS[:, 256:512],
                                            selb[:, e_i * 128:
                                                 (e_i + 1) * 128],
                                            sthi[0:16, 0:256], start=True,
                                            stop=False,
                                            skip_group_check=True),
                                   e.matmul(PS[:, 256:512],
                                            selb[:, e_i * 128:
                                                 (e_i + 1) * 128],
                                            sthi[0:16, 256:512],
                                            start=False, stop=True,
                                            skip_group_check=True))[-1],
                               wait=[(sV, tch), (sDs, t_sel)], inc=sP)
                    tcv = P.do("vector",
                               lambda e: e.tensor_copy(tmpA, PS[:, 256:512]),
                               wait=[(sP, tbc)], inc=sV)

                    import os as _os
                    _skipexp = _os.environ.get("NOEXP") == "1"
                    t_ym = tcv
                    wle = [(sDg, t_wg[par]), (sDg, t_wu[par]),
                           (sDg, t_h), (sV, tcv)]
                    for ib in range([4, 0][int(_skipexp)]):
                        gps = PS[:, 0:256] if ib % 2 == 0 else PO[:, 0:256]
                        ups = PO[:, 0:256] if ib % 2 == 0 else PS[:, 0:256]

                        def f_g(e, gsl=gsl, ib=ib, gps=gps):
                            last = None
                            for k in range(NKB):
                                last = e.matmul(
                                    gps, gsl[:, k * 512 + ib * 128:
                                             k * 512 + ib * 128 + 128],
                                    kb(hbf, k), start=(k == 0),
                                    stop=(k == 7), skip_group_check=True)
                            return last
                        wg = list(wle) if ib == 0 else []
                        if ib >= 1:
                            wg.append((sA, t_si))
                            wg.append((sV, t_ym))
                        tg = P.do("tensor", f_g, wait=wg, inc=sP)
                        sil = acA_pc = hg if ib % 2 == 0 else ha
                        t_si = P.do("scalar",
                                    lambda e, gps=gps, sil=sil: e.activation(
                                        sil[0:128 if sil is hg else 64,
                                            0:256][0:64, 0:256]
                                        if sil is ha else sil[:, 0:256],
                                        gps, AF.Silu),
                                    wait=[(sP, tg)], inc=sA) if False else                             P.do("scalar",
                                 lambda e, gps=gps, ib=ib: e.activation(
                                     m_kraw[:, ib * 256:(ib + 1) * 256],
                                     gps, AF.Silu),
                                 wait=[(sP, tg)], inc=sA)

                        def f_u(e, usl=usl, ib=ib, ups=ups):
                            last = None
                            for k in range(NKB):
                                last = e.matmul(
                                    ups, usl[:, k * 512 + ib * 128:
                                             k * 512 + ib * 128 + 128],
                                    kb(hbf, k), start=(k == 0),
                                    stop=(k == 7), skip_group_check=True)
                            return last
                        tu = P.do("tensor", f_u, wait=[(sA, t_si)], inc=sP)

                        def f_ym(e, ib=ib, ups=ups):
                            e.tensor_tensor(
                                m_kp[:, ib * 256:(ib + 1) * 256],
                                m_kraw[:, ib * 256:(ib + 1) * 256], ups,
                                ALU.mult)
                            return e.tensor_mul(
                                m_kp[:, ib * 256:(ib + 1) * 256],
                                m_kp[:, ib * 256:(ib + 1) * 256], tmpA)
                        t_ym = P.do("vector", f_ym, wait=[(sP, tu)], inc=sV)

                    def f_d(e, dsl=dsl):
                        if _skipexp:
                            return e.matmul(PAB[:, 0:256], dsl[:, 0:128],
                                            m_kp[:, 0:256], start=True,
                                            stop=True, skip_group_check=True)
                        last = None
                        for mb in range(8):
                            ps = (PAB[:, (mb % 4) * 256:(mb % 4) * 256 + 256]
                                  if mb < 4 else
                                  PCD[:, (mb % 4) * 256:(mb % 4) * 256 + 256])
                            for ib in range(4):
                                last = e.matmul(
                                    ps, dsl[:, ib * 1024 + mb * 128:
                                            ib * 1024 + mb * 128 + 128],
                                    m_kp[:, ib * 256:(ib + 1) * 256],
                                    start=(ib == 0 and mb in (0, 2, 4, 6)),
                                    stop=(ib == 3), skip_group_check=True)
                        return last
                    t_d = P.do("tensor", f_d,
                               wait=[(sV, t_ym), (sDg, t_wd[par])], inc=sP)
                    t_lastmm[par] = t_d

                    def f_ev(e):
                        last = None
                        for mb in range(8):
                            ps = (PAB[:, (mb % 4) * 256:(mb % 4) * 256 + 256]
                                  if mb < 4 else
                                  PCD[:, (mb % 4) * 256:(mb % 4) * 256 + 256])
                            last = e.tensor_copy(kb(stg, mb), ps)
                        return last
                    t_e = P.do("vector", f_ev, wait=[(sP, t_d)], inc=sV)
                    import os as _os
                    if _os.environ.get("NOACC") == "1":
                        P.do("gpsimd",
                             lambda e, st=st: e.dma_start(
                                 out=stsl(rtd_r, st), in_=r3(stg)),
                             wait=[(sV, t_e)], inc=sDg, inc_by=16)
                    else:
                        P.do("gpsimd",
                             lambda e, st=st: e.dma_start(
                                 out=stsl(rtd_r, st), in_=r3(stg),
                                 accum_op=ALU.add),
                             wait=[(sV, t_e)], inc=sDg, inc_by=16)

        def emit_pd(P, ctx, dma, bar2, sDs, sDg, sP, sV, sA):
            (x0a, x0b, x1t, Wt, scum, tmpA, tmpB, tmpC, x0f, x0g, x1f, xnt,
             dxt, sqt, m_r, m_kraw, m_kp, m_kk, m_b, m_v, m_g, m_o, hw, ha,
             hg, cxn, cbuf, cibuf, clast, AR, BS, ktil, vbtm, ktm, gp3, sol0,
             sol1, otm, sfp, wkb, wvb, slotA, w1b, a1b, g1b, w2b, a2b, g2b,
             PAB, PT, PT2, PCD, PS, PO) = ctx
            hbf = x0f
            h1t = x0a
            rtt = x0b
            outt = x1t
            sg_t = m_r
            y_t = m_kp
            hprev = cxn
            bar2()

            def kb(ap2d, k, w=ST):
                return ap2d[:, k * w:(k + 1) * w]

            def r3(tile, a=8):
                return tile.rearrange("p (a t) -> p a t", a=a)

            def stsl(rr, st, w=ST):
                return rr[:, :, st * w:(st + 1) * w]

            t_w1 = dma(wkb[:, 0:4096].rearrange("p (a m) -> p a m", a=8),
                       wc_d[0].rearrange("(a p) m -> p a m", p=128),
                       eng="gpsimd")
            t_w2 = dma(wkb[:, 4096:8192].rearrange("p (a m) -> p a m", a=8),
                       wc_d[1].rearrange("(a p) m -> p a m", p=128),
                       eng="gpsimd")
            t_w3 = dma(wvb[:, 0:4096].rearrange("p (a m) -> p a m", a=8),
                       wc_d[2].rearrange("(a p) m -> p a m", p=128),
                       eng="gpsimd")
            t_w4 = dma(wvb[:, 4096:8192].rearrange("p (a m) -> p a m", a=8),
                       sup_d.rearrange("(a p) m -> p a m", p=128),
                       eng="gpsimd")
            t_w5 = dma(slotA[:, 0:4096].rearrange("p (a m) -> p a m", a=4),
                       sdw_d.rearrange("(a p) m -> p a m", p=128),
                       eng="gpsimd")
            hbr = hbx.rearrange("p (a q) -> p a q", q=260)
            tz = P.do("vector",
                      lambda e: e.memset(hbr[:, :, 0:2], 0.0), inc=sV)
            for st in range(nst):
                bar2()
                t_h = dma(hbr[:, :, 2:258], stsl(h1_r, st), eng="gpsimd")
                if st > 0:
                    t_h = dma(hbr[:, :, 0:2],
                              h1_r[:, :, st * ST - 2:st * ST], eng="gpsimd")
                t_hf = dma(r3(h1t), stsl(h1_r, st))
                t_rt = dma(r3(rtt), stsl(rtd_r, st))
                wx = [(sDg, t_h), (sDs, t_hf), (sDs, t_rt)]
                if st == 0:
                    wx += [(sDg, t_w1), (sDg, t_w2), (sDg, t_w3),
                           (sDg, t_w4), (sDg, t_w5), (sV, tz)]

                def gps_of(ib):
                    return (PS[:, 0:256], PS[:, 256:512], PO[:, 0:256],
                            PO[:, 256:512])[ib]

                def f_conv2(e):
                    last = None
                    for ib in range(4):
                        ps = gps_of(ib)
                        first = True
                        for wsh in range(3):
                            wsl = (wkb[:, 0:4096], wkb[:, 4096:8192],
                                   wvb[:, 0:4096])[wsh]
                            sh = 2 - wsh
                            for k in range(NKB):
                                lhs = wsl[:, k * 512 + ib * 128:
                                          k * 512 + ib * 128 + 128]
                                last = e.matmul(
                                    ps, lhs, hbr[:, k, 2 - sh:258 - sh],
                                    start=first,
                                    stop=(wsh == 2 and k == NKB - 1),
                                    skip_group_check=True)
                                first = False
                    return last
                import os as _os
                tcv = P.do("tensor", f_conv2, wait=wx, inc=sP)

                t_sl = 0
                for ib in range(4):
                    t_sl = P.do("scalar",
                                lambda e, ib=ib: e.activation(
                                    sg_t[:, ib * 256:(ib + 1) * 256],
                                    gps_of(ib), AF.Silu),
                                wait=[(sP, tcv)] if ib == 0 else [],
                                inc=sA)
                t_ym = 0
                for ib in range(4):
                    ups = PS[:, 0:256] if ib % 2 == 0 else PO[:, 0:256]

                    def f_u(e, ib=ib, ups=ups):
                        last = None
                        for k in range(NKB):
                            last = e.matmul(
                                ups, wvb[:, 4096 + k * 512 + ib * 128:
                                         4096 + k * 512 + ib * 128 + 128],
                                hbr[:, k, 2:258], start=(k == 0),
                                stop=(k == 7), skip_group_check=True)
                        return last
                    wu = [(sA, t_sl)] if ib < 2 else [(sV, t_ym)]
                    tu = P.do("tensor", f_u, wait=wu, inc=sP)
                    t_ym = P.do("vector",
                                lambda e, ib=ib, ups=ups: e.tensor_tensor(
                                    y_t[:, ib * 256:(ib + 1) * 256],
                                    sg_t[:, ib * 256:(ib + 1) * 256], ups,
                                    ALU.mult),
                                wait=[(sP, tu)], inc=sV)

                def f_d(e):
                    last = None
                    for mb in range(8):
                        ps = (PAB[:, (mb % 4) * 256:(mb % 4) * 256 + 256]
                              if mb < 4 else
                              PCD[:, (mb % 4) * 256:(mb % 4) * 256 + 256])
                        for ib in range(4):
                            last = e.matmul(
                                ps, slotA[:, ib * 1024 + mb * 128:
                                          ib * 1024 + mb * 128 + 128],
                                y_t[:, ib * 256:(ib + 1) * 256],
                                start=(ib == 0 and mb in (0, 2, 4, 6)),
                                stop=(ib == 3), skip_group_check=True)
                    return last
                t_d = P.do("tensor", f_d, wait=[(sV, t_ym)], inc=sP)

                def f_s(e):
                    last = None
                    for mb in range(8):
                        ps = (PAB[:, (mb % 4) * 256:(mb % 4) * 256 + 256]
                              if mb < 4 else
                              PCD[:, (mb % 4) * 256:(mb % 4) * 256 + 256])
                        e.tensor_add(kb(outt, mb), kb(rtt, mb), ps)
                        last = e.tensor_add(kb(outt, mb), kb(outt, mb),
                                            kb(h1t, mb))
                    return last
                tv5 = P.do("vector", f_s, wait=[(sP, t_d)], inc=sV)

                def f_cp(e):
                    last = None
                    for k in range(NKB):
                        last = e.tensor_copy(kb(x0g, k), kb(outt, k))
                    return last
                tv6 = P.do("vector", f_cp, wait=[(sV, tv5)], inc=sV)
                tq = P.do("scalar",
                          lambda e: e.activation(sqt, x0g, AF.Square),
                          wait=[(sV, tv6)], inc=sA)

                def f_sums(e):
                    for k in range(NKB):
                        e.matmul(PS[0:1, 0:256], onc, kb(x0g, k),
                                 start=(k == 0), stop=False,
                                 skip_group_check=True)
                    last = None
                    for k in range(NKB):
                        last = e.matmul(PS[0:1, 256:512], onc, kb(sqt, k),
                                        start=False, stop=(k == 7),
                                        skip_group_check=True)
                    return last
                tp = P.do("tensor", f_sums, wait=[(sA, tq)], inc=sP)

                def f_stats(e, n=C):
                    e.tensor_scalar(tmpA[0:1, :], PS[0:1, 0:256], 1.0 / n,
                                    None, ALU.mult)
                    e.tensor_mul(tmpB[0:1, :], tmpA[0:1, :], tmpA[0:1, :])
                    e.scalar_tensor_tensor(tmpB[0:1, :], PS[0:1, 256:512],
                                           1.0 / n, tmpB[0:1, :], ALU.mult,
                                           ALU.subtract)
                    e.tensor_copy(sthi[0:1, 0:256], tmpA[0:1, :])
                    return e.tensor_sub(tmpA[0:1, :], tmpA[0:1, :],
                                        sthi[0:1, 0:256])
                tv = P.do("vector", f_stats, wait=[(sP, tp)], inc=sV)

                def f_rstd(e):
                    e.activation(tmpB[0:1, :], tmpB[0:1, :], AF.Ln,
                                 bias=aux[0:1, 24:25])
                    return e.activation(tmpB[0:1, :], tmpB[0:1, :], AF.Exp,
                                        scale=-0.5)
                ta = P.do("scalar", f_rstd, wait=[(sV, tv)], inc=sA)

                def f_hl(e):
                    e.tensor_copy(sthi[0:1, 256:512], tmpA[0:1, :])
                    e.tensor_copy(sthi[0:1, 512:768], tmpB[0:1, :])
                    e.tensor_sub(tmpB[0:1, :], tmpB[0:1, :],
                                 sthi[0:1, 512:768])
                    return e.tensor_copy(sthi[0:1, 768:1024], tmpB[0:1, :])
                tv2 = P.do("vector", f_hl, wait=[(sA, ta)], inc=sV)

                def f_bcast(e):
                    e.matmul(PO[:, 0:256], onr, sthi[0:1, 0:256],
                             start=True, stop=False, skip_group_check=True)
                    e.matmul(PO[:, 0:256], onr, sthi[0:1, 256:512],
                             start=False, stop=True, skip_group_check=True)
                    e.matmul(PO[:, 256:512], onr, sthi[0:1, 512:768],
                             start=False, stop=False, skip_group_check=True)
                    return e.matmul(PO[:, 256:512], onr,
                                    sthi[0:1, 768:1024], start=False,
                                    stop=True, skip_group_check=True)
                tp2 = P.do("tensor", f_bcast, wait=[(sV, tv2)], inc=sP)

                def f_out(e):
                    last = None
                    for k in range(NKB):
                        e.tensor_sub(tmpA, kb(outt, k), PO[:, 0:256])
                        last = e.tensor_mul(kb(outt, k), tmpA,
                                            PO[:, 256:512])
                    return last
                tv7 = P.do("vector", f_out, wait=[(sP, tp2)], inc=sV)
                dma(stsl(out_d.rearrange("(a p) t -> p a t", p=128), st),
                    r3(outt), wait=[(sV, tv7)])

        x0a_ = es.enter_context(nc.sbuf_tensor("x0a", [128, 2048], F32))
        x0b_ = es.enter_context(nc.sbuf_tensor("x0b", [128, 2048], F32))
        x1t_ = es.enter_context(nc.sbuf_tensor("x1t", [128, 2048], F32))
        Wt_ = es.enter_context(nc.sbuf_tensor("Wt", [128, 2048], F32))
        scum_ = es.enter_context(nc.sbuf_tensor("scum", [128, 512], F32))
        tmpA_ = es.enter_context(nc.sbuf_tensor("tmpA", [128, 256], F32))
        tmpB_ = es.enter_context(nc.sbuf_tensor("tmpB", [128, 256], F32))
        tmpC_ = es.enter_context(nc.sbuf_tensor("tmpC", [128, 256], F32))
        x0f_ = es.enter_context(nc.sbuf_tensor("x0f", [128, 2048], BF16))
        x0g_ = es.enter_context(nc.sbuf_tensor("x0g", [128, 2048], BF16))
        x1f_ = es.enter_context(nc.sbuf_tensor("x1f", [128, 2048], BF16))
        xnt_ = es.enter_context(nc.sbuf_tensor("xnt", [128, 2048], BF16))
        dxt_ = es.enter_context(nc.sbuf_tensor("dxt", [128, 2048], BF16))
        sqt_ = es.enter_context(nc.sbuf_tensor("sqt", [128, 2048], BF16))
        m_r_ = es.enter_context(nc.sbuf_tensor("m_r", [128, 2048], BF16))
        m_kraw_ = es.enter_context(nc.sbuf_tensor("m_kraw", [128, 2048], BF16))
        m_kp_ = es.enter_context(nc.sbuf_tensor("m_kp", [128, 2048], BF16))
        m_kk_ = es.enter_context(nc.sbuf_tensor("m_kk", [128, 2048], BF16))
        m_b_ = es.enter_context(nc.sbuf_tensor("m_b", [128, 2048], BF16))
        m_v_ = es.enter_context(nc.sbuf_tensor("m_v", [128, 2048], BF16))
        m_g_ = es.enter_context(nc.sbuf_tensor("m_g", [128, 2048], BF16))
        m_o_ = es.enter_context(nc.sbuf_tensor("m_o", [128, 2048], BF16))
        hw_ = es.enter_context(nc.sbuf_tensor("hw", [64, 256], BF16))
        ha_ = es.enter_context(nc.sbuf_tensor("ha", [64, 256], BF16))
        hg_ = es.enter_context(nc.sbuf_tensor("hg", [128, 256], BF16))
        cxn_ = es.enter_context(nc.sbuf_tensor("cxn", [128, 8], BF16))
        cbuf_ = es.enter_context(nc.sbuf_tensor("cbuf", [128, 520], BF16))
        cibuf_ = es.enter_context(nc.sbuf_tensor("cibuf", [128, 512], BF16))
        clast_ = es.enter_context(nc.sbuf_tensor("clast", [128, 8], F32))
        AR_ = es.enter_context(nc.sbuf_tensor("AR", [128, 1024], BF16))
        BS_ = es.enter_context(nc.sbuf_tensor("BS", [128, 1024], BF16))
        ktil_ = es.enter_context(nc.sbuf_tensor("ktil", [128, 512], BF16))
        vbtm_ = es.enter_context(nc.sbuf_tensor("vbtm", [128, 1024], BF16))
        ktm_ = es.enter_context(nc.sbuf_tensor("ktm", [128, 512], BF16))
        gp3_ = es.enter_context(nc.sbuf_tensor("gp3", [128, 1536], BF16))
        sol0_ = es.enter_context(nc.sbuf_tensor("sol0", [128, 2048], BF16))
        sol1_ = es.enter_context(nc.sbuf_tensor("sol1", [128, 2048], BF16))
        otm_ = es.enter_context(nc.sbuf_tensor("otm", [128, 512], BF16))
        sfp_ = es.enter_context(nc.sbuf_tensor("sfp", [128, 512], F32))
        wkb_ = es.enter_context(nc.sbuf_tensor("wkb", [128, 8192], BF16))
        wvb_ = es.enter_context(nc.sbuf_tensor("wvb", [128, 8192], BF16))
        slotA_ = es.enter_context(nc.sbuf_tensor("slotA", [128, 8192], BF16))
        w1b_ = es.enter_context(nc.sbuf_tensor("w1b", [128, 512], BF16))
        a1b_ = es.enter_context(nc.sbuf_tensor("a1b", [128, 512], BF16))
        g1b_ = es.enter_context(nc.sbuf_tensor("g1b", [128, 1024], BF16))
        w2b_ = es.enter_context(nc.sbuf_tensor("w2b", [64, 1024], BF16))
        a2b_ = es.enter_context(nc.sbuf_tensor("a2b", [64, 1024], BF16))
        g2b_ = es.enter_context(nc.sbuf_tensor("g2b", [128, 1024], BF16))
        PAB_ = es.enter_context(nc.psum_tensor("PAB", [128, 1024], F32))
        PT_ = es.enter_context(nc.psum_tensor("PT", [128, 1024], BF16))
        PT2_ = es.enter_context(nc.psum_tensor("PT2", [128, 1024], BF16))
        PCD_ = es.enter_context(nc.psum_tensor("PCD", [128, 1024], F32))
        PS_ = es.enter_context(nc.psum_tensor("PS", [128, 512], F32))
        PO_ = es.enter_context(nc.psum_tensor("PO", [128, 512], F32))
        if True:
            ctx = tuple(x.ap() for x in (
                x0a_, x0b_, x1t_, Wt_, scum_, tmpA_, tmpB_, tmpC_, x0f_, x0g_,
                x1f_, xnt_, dxt_, sqt_, m_r_, m_kraw_, m_kp_, m_kk_, m_b_,
                m_v_, m_g_, m_o_, hw_, ha_, hg_, cxn_, cbuf_, cibuf_, clast_,
                AR_, BS_, ktil_, vbtm_, ktm_, gp3_, sol0_, sol1_, otm_, sfp_,
                wkb_, wvb_, slotA_, w1b_, a1b_, g1b_, w2b_, a2b_, g2b_,
                PAB_, PT_, PT2_, PCD_, PS_, PO_))
            block.sync(lambda e: emit_all(Pass("sync", e), ctx))
            block.tensor(lambda e: emit_all(Pass("tensor", e), ctx))
            block.vector(lambda e: emit_all(Pass("vector", e), ctx))
            block.scalar(lambda e: emit_all(Pass("scalar", e), ctx))
            block.gpsimd(lambda e: emit_all(Pass("gpsimd", e), ctx))
    return nc, dbg_d


def _prep_inputs(inputs):
    f = np.float32

    def t(x):
        return np.ascontiguousarray(np.asarray(x, f))
    P = {k.replace("L0_attn_", "").replace("L0_ffn_", ""): v
         for k, v in inputs.items()}
    ins = {}
    ins["wr"] = t(P["W_r"]); ins["wk"] = t(P["W_k"])
    ins["wv"] = t(P["W_v"]); ins["wo"] = t(P["W_o"])
    ins["w1"] = t(P["w1"]); ins["a1"] = t(P["a1"]); ins["g1"] = t(P["g1"])
    ins["w2"] = t(P["w2"]); ins["a2"] = t(P["a2"]); ins["g2"] = t(P["g2"])
    ins["wf1"] = t(P["W_ffn1"]); ins["wf2"] = t(P["W_ffn2"])
    ins["egw"] = t(P["expert_gate_w"]); ins["euw"] = t(P["expert_up_w"])
    ins["edw"] = t(P["expert_down_w"])
    ins["wc"] = np.ascontiguousarray(
        np.asarray(P["conv_w"], f).transpose(2, 1, 0))
    ins["sup"] = t(P["shared_up_w"]); ins["sdw"] = t(P["shared_down_w"])
    ins["rw"] = t(P["router_w"])
    ins["rb"] = np.ascontiguousarray(
        np.asarray(P["router_bias"], f).reshape(16, 1))
    vpa = np.zeros((128, NVP), f)
    for name in VP_ORDER:
        v = np.asarray(P[name], f).reshape(-1)
        vpa[:, VPC[name]:VPC[name] + 8] = v.reshape(8, 128).T
    ins["vp"] = vpa
    bf = ml_dtypes.bfloat16
    eye = np.eye(64, dtype=f)
    ins["idb"] = np.concatenate([eye, eye], axis=0).astype(bf)
    p = np.arange(64)[:, None]
    q = np.arange(64)[None, :]
    up_s = (p < q).astype(f)
    up_i = (p <= q).astype(f)
    lo_s = (q < p).astype(f)
    m4 = np.concatenate([up_s, up_i, up_s, up_i], axis=1)
    ins["mk4"] = np.concatenate([m4, m4], axis=0).astype(bf)
    ins["mk1"] = np.concatenate([lo_s, lo_s], axis=0).astype(bf)
    ins["onc"] = np.ones((128, 1), f).astype(bf)
    ins["onr"] = np.ones((1, 128), f).astype(bf)
    e16 = np.zeros((128, 128), f)
    for k in range(8):
        e16[0:64, k * 16 + 2 * k] = 1.0
        e16[64:128, k * 16 + 2 * k + 1] = 1.0
    ins["e16"] = e16.astype(bf)
    ins["if16"] = np.eye(16, dtype=f)
    s15 = np.zeros((16, 128), f)
    s15[15, :] = 1.0
    ins["s15"] = s15.astype(bf)
    sel = np.zeros((16, 1920), f)
    for e2 in range(15):
        sel[e2, e2 * 128:(e2 + 1) * 128] = 1.0
    ins["sel15"] = sel.astype(bf)
    ins["if128"] = np.eye(128, dtype=f)
    f16 = np.zeros((16, 1024), f)
    for k in range(8):
        f16[2 * k, k * 128:k * 128 + 64] = 1.0
        f16[2 * k + 1, k * 128 + 64:k * 128 + 128] = 1.0
    ins["f16"] = f16.astype(bf)
    return ins


def kernel(**inputs):
    x = np.asarray(inputs["L0_attn_x"], np.float32)
    B = x.shape[0]
    shared = _prep_inputs(inputs)
    xt = np.ascontiguousarray(x.transpose(0, 2, 1))
    nc, _ = build(bass.Bass("TRN2", target_bir_lowering=False, debug=False))
    in_maps = []
    for b in range(B):
        m = dict(shared)
        m["x0"] = xt[b]
        in_maps.append(m)
    res = run_bass_kernel_spmd(nc, in_maps, core_ids=list(range(B)))
    out = np.stack([np.asarray(r["out"]) for r in res.results], axis=0)
    return np.ascontiguousarray(out.transpose(0, 2, 1))


# version 06
# speedup vs baseline: 1.1611x; 1.1611x over previous
import numpy as np
import ml_dtypes
from contextlib import ExitStack
import concourse.bass as bass
import concourse.mybir as mybir
from concourse.bass_utils import run_bass_kernel_spmd

F32 = mybir.dt.float32
BF16 = mybir.dt.bfloat16
AF = mybir.ActivationFunctionType
ALU = mybir.AluOpType

C = 1024
T = 8192
NKB = 8
ST = 256
NST = T // ST
LCH = 64
NCH = ST // LCH
EMB = float(np.exp(-0.5))

VP_ORDER = ["ln_pre_w", "ln_pre_b", "ln_attn_w", "ln_attn_b", "x_r", "x_w",
            "x_k", "x_v", "x_a", "x_g", "w0", "a0", "k_k", "k_a", "r_k",
            "gn_w", "gn_b", "ln_ffn_w", "ln_ffn_b", "ffn_x_k"]
VPC = {n: 8 * i for i, n in enumerate(VP_ORDER)}
NVP = 8 * len(VP_ORDER)

STB = 1024
NSTB = T // STB
STC = 1024
NSTC = T // STC
STD = 512
NSTD = T // STD


class Sem:
    def __init__(self, handle):
        self.h = handle
        self.n = 0

    def bump(self, k):
        self.n += k
        return self.n


class Pass:

    def __init__(self, active, eng):
        self.active = active
        self.e = eng

    def do(self, engine, fn, wait=(), inc=None, inc_by=1):
        tgt = inc.bump(inc_by) if inc is not None else None
        if engine == self.active:
            for (s, v) in wait:
                self.e.wait_ge(s.h, v)
            last = fn(self.e)
            if inc is not None:
                assert last is not None
                last.then_inc(inc.h, inc_by)
        return tgt


def build(nc, dbg=(), stage=4, nst=NST):
    dt = nc.dram_tensor
    x0_d = dt("x0", [C, T], F32, kind="ExternalInput").ap()
    wr_d = dt("wr", [C, C], F32, kind="ExternalInput").ap()
    wk_d = dt("wk", [C, C], F32, kind="ExternalInput").ap()
    wv_d = dt("wv", [C, C], F32, kind="ExternalInput").ap()
    wo_d = dt("wo", [C, C], F32, kind="ExternalInput").ap()
    w1_d = dt("w1", [C, 64], F32, kind="ExternalInput").ap()
    a1_d = dt("a1", [C, 64], F32, kind="ExternalInput").ap()
    g1_d = dt("g1", [C, 128], F32, kind="ExternalInput").ap()
    w2_d = dt("w2", [64, C], F32, kind="ExternalInput").ap()
    a2_d = dt("a2", [64, C], F32, kind="ExternalInput").ap()
    g2_d = dt("g2", [128, C], F32, kind="ExternalInput").ap()
    wf1_d = dt("wf1", [C, 4096], F32, kind="ExternalInput").ap()
    wf2_d = dt("wf2", [4096, C], F32, kind="ExternalInput").ap()
    egw_d = dt("egw", [15, C, 512], F32, kind="ExternalInput").ap()
    euw_d = dt("euw", [15, C, 512], F32, kind="ExternalInput").ap()
    edw_d = dt("edw", [15, 512, C], F32, kind="ExternalInput").ap()
    wc_d = dt("wc", [3, C, 512], F32, kind="ExternalInput").ap()
    sup_d = dt("sup", [C, 512], F32, kind="ExternalInput").ap()
    sdw_d = dt("sdw", [512, C], F32, kind="ExternalInput").ap()
    rw_d = dt("rw", [C, 16], F32, kind="ExternalInput").ap()
    rb_d = dt("rb", [16, 1], F32, kind="ExternalInput").ap()
    vp_d = dt("vp", [128, NVP], F32, kind="ExternalInput").ap()
    idb_d = dt("idb", [128, 64], BF16, kind="ExternalInput").ap()
    mk4_d = dt("mk4", [128, 256], BF16, kind="ExternalInput").ap()
    mk1_d = dt("mk1", [128, 64], BF16, kind="ExternalInput").ap()
    onc_d = dt("onc", [128, 1], BF16, kind="ExternalInput").ap()
    onr_d = dt("onr", [1, 128], BF16, kind="ExternalInput").ap()
    e16_d = dt("e16", [128, 128], BF16, kind="ExternalInput").ap()
    f16_d = dt("f16", [16, 1024], BF16, kind="ExternalInput").ap()
    if16_d = dt("if16", [16, 16], F32, kind="ExternalInput").ap()
    s15_d = dt("s15", [16, 128], BF16, kind="ExternalInput").ap()
    sel_d = dt("sel15", [16, 1920], BF16, kind="ExternalInput").ap()
    if128_d = dt("if128", [128, 128], F32, kind="ExternalInput").ap()
    _k = lambda n: "ExternalOutput" if n in dbg else "Internal"
    x2_d = dt("x2s", [C, T], F32, kind=_k("x2s")).ap()
    h1_d = dt("h1s", [C, T], F32, kind=_k("h1s")).ap()
    rtd_d = dt("rtds", [C, T], F32, kind=_k("rtds")).ap()
    cmb_d = dt("cmbs", [16, T], F32, kind=_k("cmbs")).ap()
    out_d = dt("out", [C, T], F32, kind="ExternalOutput").ap()
    dbg_d = {}
    for name in dbg:
        shp = {"cmb": [16, T], "sar": [128, 1024], "sbs": [128, 1024],
               "skt": [128, 512], "scb": [128, 520], "sci": [128, 512],
               "ssol": [128, 2048], "sotm": [128, 512], "spre": [128, 2048],
               "sgp3": [128, 1536]}.get(name, [C, T])
        dbg_d[name] = dt("dbg_" + name, shp, F32, kind="ExternalOutput").ap()

    x0_r = x0_d.rearrange("(a p) t -> p a t", p=128)
    x2_r = x2_d.rearrange("(a p) t -> p a t", p=128)
    h1_r = h1_d.rearrange("(a p) t -> p a t", p=128)
    rtd_r = rtd_d.rearrange("(a p) t -> p a t", p=128)
    out_r = out_d.rearrange("(a p) t -> p a t", p=128)

    es = ExitStack()
    with es:
        vp_ = es.enter_context(nc.sbuf_tensor("vp_s", [128, NVP], F32))
        aux_ = es.enter_context(nc.sbuf_tensor("aux", [128, 32], F32))
        idb_ = es.enter_context(nc.sbuf_tensor("idb_s", [128, 64], BF16))
        mk4_ = es.enter_context(nc.sbuf_tensor("mk4_s", [128, 256], BF16))
        mk1_ = es.enter_context(nc.sbuf_tensor("mk1_s", [128, 64], BF16))
        onc_ = es.enter_context(nc.sbuf_tensor("onc_s", [128, 1], BF16))
        onr_ = es.enter_context(nc.sbuf_tensor("onr_s", [1, 128], BF16))
        e16_ = es.enter_context(nc.sbuf_tensor("e16_s", [128, 128], BF16))
        f16_ = es.enter_context(nc.sbuf_tensor("f16_s", [16, 1024], BF16))
        sthi_ = es.enter_context(nc.sbuf_tensor("sthi", [16, 2048], BF16))
        zer_ = es.enter_context(nc.sbuf_tensor("zer", [128, 64], F32))
        selb_ = es.enter_context(nc.sbuf_tensor("selb", [16, 1920], BF16))
        hbx_ = es.enter_context(nc.sbuf_tensor("hbx", [128, 2080], BF16))
        hDs = es.enter_context(nc.semaphore("sDs"))
        hDg = es.enter_context(nc.semaphore("sDg"))
        hP = es.enter_context(nc.semaphore("sP"))
        hV = es.enter_context(nc.semaphore("sV"))
        hA = es.enter_context(nc.semaphore("sA"))
        block = es.enter_context(nc.Block())
        vp = vp_.ap()
        aux = aux_.ap()
        idb = idb_.ap()
        mk4 = mk4_.ap()
        mk1 = mk1_.ap()
        onc = onc_.ap()
        onr = onr_.ap()
        e16 = e16_.ap()
        f16 = f16_.ap()
        sthi = sthi_.ap()
        zer = zer_.ap()
        selb = selb_.ap()
        hbx = hbx_.ap()

        def vcol(name, k):
            c0 = VPC[name]
            return vp[:, c0 + k:c0 + k + 1]

        def emit_all(P, ctx):
            sDs, sDg = Sem(hDs), Sem(hDg)
            sP, sV, sA = Sem(hP), Sem(hV), Sem(hA)

            def dma(out, in_, wait=(), eng="sync"):
                sem = sDs if eng == "sync" else sDg
                return P.do(eng, lambda e: e.dma_start(out=out, in_=in_),
                            wait=wait, inc=sem, inc_by=16)

            def bar2():
                vals = [(sDs, sDs.n), (sDg, sDg.n), (sP, sP.n), (sV, sV.n),
                        (sA, sA.n)]
                for eng in ("tensor", "vector", "scalar", "sync", "gpsimd"):
                    def f(e, vals=vals):
                        for (s, v) in vals:
                            if v > 0:
                                e.wait_ge(s.h, v)
                        return None
                    P.do(eng, f)

            emit_pa(P, ctx, dma, bar2, sDs, sDg, sP, sV, sA)
            if stage >= 2:
                emit_pb(P, ctx, dma, bar2, sDs, sDg, sP, sV, sA)
            if stage >= 3:
                emit_pc(P, ctx, dma, bar2, sDs, sDg, sP, sV, sA)
            if stage >= 4:
                emit_pd(P, ctx, dma, bar2, sDs, sDg, sP, sV, sA)
            bar2()

        def emit_pa(P, ctx, dma, bar2, sDs, sDg, sP, sV, sA):
            (x0a, x0b, x1t, Wt, scum, tmpA, tmpB, tmpC, x0f, x0g, x1f, xnt,
             dxt, sqt, m_r, m_kraw, m_kp, m_kk, m_b, m_v, m_g, m_o, hw, ha,
             hg, cxn, cbuf, cibuf, clast, AR, BS, ktil, vbtm, ktm, gp3, sol0,
             sol1, otm, sfp, wkb, wvb, slotA, w1b, a1b, g1b, w2b, a2b, g2b,
             PAB, PT, PT2, PCD, PS, PO) = ctx

            def kb(ap2d, k, w=ST):
                return ap2d[:, k * w:(k + 1) * w]

            def kb2x(ap2d, half):
                return ap2d[:, half * 1024:(half + 1) * 1024]

            def stsl(r3, st, w=ST):
                return r3[:, :, st * w:(st + 1) * w]

            def r3(tile, a=8):
                return tile.rearrange("p (a t) -> p a t", a=a)

            t_vp = dma(vp, vp_d)
            dma(idb, idb_d)
            dma(mk4, mk4_d)
            dma(mk1, mk1_d)
            dma(onc, onc_d)
            dma(onr, onr_d)
            dma(e16, e16_d)
            t_cst = dma(f16, f16_d)
            wkb_r = wkb.rearrange("p (a m) -> p a m", a=8)
            wvb_r = wvb.rearrange("p (a m) -> p a m", a=8)
            slotA_r = slotA.rearrange("p (a m) -> p a m", a=8)
            t_wk = dma(wkb_r, wk_d.rearrange("(a p) m -> p a m", p=128),
                       eng="gpsimd")
            t_wv = dma(wvb_r, wv_d.rearrange("(a p) m -> p a m", p=128),
                       eng="gpsimd")
            dma(w1b.rearrange("p (a m) -> p a m", a=8),
                w1_d.rearrange("(a p) m -> p a m", p=128), eng="gpsimd")
            dma(a1b.rearrange("p (a m) -> p a m", a=8),
                a1_d.rearrange("(a p) m -> p a m", p=128), eng="gpsimd")
            dma(g1b.rearrange("p (a m) -> p a m", a=8),
                g1_d.rearrange("(a p) m -> p a m", p=128), eng="gpsimd")
            dma(w2b[:], w2_d, eng="gpsimd")
            dma(a2b[:], a2_d, eng="gpsimd")
            t_smallw = dma(g2b[:], g2_d, eng="gpsimd")
            t_wr = dma(slotA_r, wr_d.rearrange("(a p) m -> p a m", p=128),
                       eng="gpsimd")

            def f_aux(e):
                e.tensor_scalar(aux[:, 0:8], vp[:, VPC["w0"]:VPC["w0"] + 8],
                                0.5, None, ALU.mult)
                e.tensor_scalar(aux[:, 8:16], vp[:, VPC["a0"]:VPC["a0"] + 8],
                                0.5, None, ALU.mult)
                e.memset(aux[:, 24:25], 1e-5)
                e.memset(aux[:, 25:26], 0.00064)
                e.memset(aux[:, 26:27], 0.5)
                e.memset(aux[:, 27:28], EMB * 0.5)
                e.memset(aux[:, 29:30], -1.0)
                return e.tensor_scalar(aux[:, 16:24],
                                       vp[:, VPC["k_a"]:VPC["k_a"] + 8],
                                       -1.0, 1.0, ALU.mult, ALU.add)
            t_aux = P.do("vector", f_aux, wait=[(sDs, t_vp)], inc=sV)

            def f_zi(e):
                s0 = r3(sol0, a=8)
                s1 = r3(sol1, a=8)
                e.memset(s0[:, :, 192:256], 0.0)
                e.memset(s1[:, :, 192:256], 0.0)
                e.memset(sfp, 0.0)
                e.memset(r3(BS)[:, :, 64:128], 0.0)
                e.memset(zer, 0.0)
                return e.memset(cxn, 0.0)
            t_zi = P.do("vector", f_zi, inc=sV)

            x0t_ = [x0a, x0b]
            x0f_ = [x0f, x0g]
            t_x0 = [0, 0]
            t_x0b = [0, 0]
            t_x0[0] = dma(r3(x0t_[0]), stsl(x0_r, 0))
            t_x0b[0] = dma(r3(x0f_[0]), stsl(x0_r, 0), eng="gpsimd")
            t_wr_st = t_wr

            for st in range(nst):
                cur = st % 2
                nxt = 1 - cur
                x0t = x0t_[cur]
                x0bf = x0f_[cur]
                bar2()
                if st + 1 < nst:
                    t_x0[nxt] = dma(r3(x0t_[nxt]), stsl(x0_r, st + 1))
                    t_x0b[nxt] = dma(r3(x0f_[nxt]), stsl(x0_r, st + 1),
                                     eng="gpsimd")
                wx = [(sDs, t_x0[cur]), (sDg, t_x0b[cur])]
                if st == 0:
                    wx += [(sDg, t_smallw), (sDs, t_cst), (sV, t_aux),
                           (sV, t_zi), (sDg, t_wk), (sDg, t_wv)]

                def ln_block(xsrc_f32, xsrc_bf, wname, bname, dst_f32, dst_bf,
                             first_wait):
                    tq = P.do("scalar",
                              lambda e: e.activation(sqt, xsrc_bf, AF.Square),
                              wait=first_wait, inc=sA)

                    def f_sums(e):
                        for k in range(NKB):
                            e.matmul(PS[0:1, 0:256], onc, kb(xsrc_bf, k),
                                     start=(k == 0), stop=False,
                                     skip_group_check=True)
                        last = None
                        for k in range(NKB):
                            last = e.matmul(PS[0:1, 256:512], onc, kb(sqt, k),
                                            start=False, stop=(k == 7),
                                            skip_group_check=True)
                        return last
                    tp = P.do("tensor", f_sums, wait=first_wait + [(sA, tq)],
                              inc=sP)

                    def f_stats(e):
                        e.tensor_scalar(tmpA[0:1, :], PS[0:1, 0:256],
                                        1.0 / C, None, ALU.mult)
                        e.tensor_mul(tmpB[0:1, :], tmpA[0:1, :], tmpA[0:1, :])
                        e.scalar_tensor_tensor(tmpB[0:1, :], PS[0:1, 256:512],
                                               1.0 / C, tmpB[0:1, :],
                                               ALU.mult, ALU.subtract)
                        e.tensor_copy(sthi[0:1, 0:256], tmpA[0:1, :])
                        return e.tensor_sub(tmpA[0:1, :], tmpA[0:1, :],
                                            sthi[0:1, 0:256])
                    tv = P.do("vector", f_stats, wait=[(sP, tp)], inc=sV)

                    def f_rstd(e):
                        e.activation(tmpC[0:1, :], tmpB[0:1, :], AF.Ln,
                                     bias=aux[0:1, 24:25])
                        return e.activation(tmpC[0:1, :], tmpC[0:1, :],
                                            AF.Exp, scale=-0.5)
                    ta = P.do("scalar", f_rstd, wait=[(sV, tv)], inc=sA)

                    def f_hl(e):
                        e.tensor_copy(sthi[0:1, 256:512], tmpA[0:1, :])
                        e.tensor_copy(sthi[0:1, 512:768], tmpC[0:1, :])
                        e.tensor_sub(tmpC[0:1, :], tmpC[0:1, :],
                                     sthi[0:1, 512:768])
                        return e.tensor_copy(sthi[0:1, 768:1024],
                                             tmpC[0:1, :])
                    tv2 = P.do("vector", f_hl, wait=[(sA, ta)], inc=sV)

                    def f_bcast(e):
                        e.matmul(PO[:, 0:256], onr, sthi[0:1, 0:256],
                                 start=True, stop=False, skip_group_check=True)
                        e.matmul(PO[:, 0:256], onr, sthi[0:1, 256:512],
                                 start=False, stop=True, skip_group_check=True)
                        e.matmul(PO[:, 256:512], onr, sthi[0:1, 512:768],
                                 start=False, stop=False,
                                 skip_group_check=True)
                        return e.matmul(PO[:, 256:512], onr,
                                        sthi[0:1, 768:1024], start=False,
                                        stop=True, skip_group_check=True)
                    tp2 = P.do("tensor", f_bcast, wait=[(sV, tv2)], inc=sP)

                    def f_apply(e):
                        last = None
                        for k in range(NKB):
                            e.tensor_sub(tmpA, kb(xsrc_f32, k), PO[:, 0:256])
                            e.scalar_tensor_tensor(tmpB, tmpA, vcol(wname, k),
                                                   PO[:, 256:512], ALU.mult,
                                                   ALU.mult)
                            if dst_f32 is not None:
                                e.tensor_scalar(kb(dst_f32, k), tmpB,
                                                vcol(bname, k), None, ALU.add)
                                if dst_bf is not None:
                                    last = e.tensor_copy(kb(dst_bf, k),
                                                         kb(dst_f32, k))
                                else:
                                    last = None or e.engine_nop()
                            else:
                                last = e.tensor_scalar(kb(dst_bf, k), tmpB,
                                                       vcol(bname, k), None,
                                                       ALU.add)
                        return last
                    return P.do("vector", f_apply, wait=[(sP, tp2)], inc=sV)

                tln1 = ln_block(x0t[:], x0bf[:], "ln_pre_w", "ln_pre_b",
                                x1t[:], x1f[:], wx)
                tln2 = ln_block(x1t[:], x1f[:], "ln_attn_w", "ln_attn_b",
                                None, xnt[:], [(sV, tln1)])

                def f_dx(e):
                    xv = r3(xnt)
                    dv = r3(dxt)
                    e.tensor_sub(dv[:, :, 1:ST], xv[:, :, 0:ST - 1],
                                 xv[:, :, 1:ST])
                    e.tensor_sub(dv[:, :, 0:1],
                                 cxn.rearrange("p (a o) -> p a o", o=1),
                                 xv[:, :, 0:1])
                    return e.tensor_copy(
                        cxn.rearrange("p (a o) -> p a o", o=1), xv[:, :, ST - 1:ST])
                tdx = P.do("vector", f_dx, wait=[(sV, tln2)], inc=sV)

                def mk_mix(name, wait):
                    def f(e):
                        last = None
                        for k in range(NKB):
                            last = e.scalar_tensor_tensor(
                                kb(sqt, k), kb(dxt, k), vcol(name, k),
                                kb(xnt, k), ALU.mult, ALU.add)
                        return last
                    return P.do("vector", f, wait=wait, inc=sV)

                def proj_big(wsb, dst, extra_wait):
                    tt = [0, 0]
                    tev = [0, 0]
                    for mb in range(8):
                        pb = mb % 2
                        ps = PAB[:, 0:256] if pb == 0 else PAB[:, 512:768]
                        w = list(extra_wait) if mb == 0 else []
                        if mb >= 2:
                            w.append((sV, tev[pb]))

                        def f(e, mb=mb, ps=ps):
                            last = None
                            for k in range(NKB):
                                last = e.matmul(
                                    ps, wsb[:, (k * 8 + mb) * 128:
                                            (k * 8 + mb) * 128 + 128],
                                    kb(sqt, k), start=(k == 0), stop=(k == 7),
                                    skip_group_check=True)
                            return last
                        tt[pb] = P.do("tensor", f, wait=w, inc=sP)
                        tev[pb] = P.do(
                            "vector",
                            lambda e, mb=mb, ps=ps: e.tensor_copy(kb(dst, mb),
                                                                  ps),
                            wait=[(sP, tt[pb])], inc=sV)
                    return tev

                tmx = mk_mix("x_r", [(sV, tdx)])
                er = proj_big(slotA, m_r[:], [(sV, tmx), (sDg, t_wr_st)])
                tmx = mk_mix("x_v", [(sV, er[0]), (sV, er[1])])
                ev = proj_big(wvb, m_v[:], [(sV, tmx)])
                tmx = mk_mix("x_k", [(sV, ev[0]), (sV, ev[1])])
                ek = proj_big(wkb, m_kraw[:], [(sV, tmx)])
                t_wo_st = dma(slotA_r,
                              wo_d.rearrange("(a p) m -> p a m", p=128),
                              wait=[(sV, er[0]), (sV, er[1])], eng="gpsimd")

                tmx = mk_mix("x_w", [(sV, ek[0]), (sV, ek[1])])

                def f_hw(e):
                    last = None
                    for k in range(NKB):
                        last = e.matmul(PS[0:64, 0:256],
                                        w1b[:, k * 64:(k + 1) * 64],
                                        kb(sqt, k), start=(k == 0),
                                        stop=(k == 7), skip_group_check=True)
                    return last
                tphw = P.do("tensor", f_hw, wait=[(sV, tmx)], inc=sP)
                tahw = P.do("scalar",
                            lambda e: e.activation(hw, PS[0:64, 0:256],
                                                   AF.Tanh),
                            wait=[(sP, tphw)], inc=sA)

                tmxa = mk_mix("x_a", [(sP, tphw)])

                def f_ha(e):
                    last = None
                    for k in range(NKB):
                        last = e.matmul(PO[0:64, 0:256],
                                        a1b[:, k * 64:(k + 1) * 64],
                                        kb(sqt, k), start=(k == 0),
                                        stop=(k == 7), skip_group_check=True)
                    return last
                tpha = P.do("tensor", f_ha, wait=[(sV, tmxa)], inc=sP)
                taha = P.do("scalar",
                            lambda e: e.copy(ha, PO[0:64, 0:256]),
                            wait=[(sP, tpha)], inc=sA)

                tmxg = mk_mix("x_g", [(sP, tpha)])

                def f_hg(e):
                    last = None
                    for k in range(NKB):
                        last = e.matmul(PCD[:, 0:256],
                                        g1b[:, k * 128:(k + 1) * 128],
                                        kb(sqt, k), start=(k == 0),
                                        stop=(k == 7), skip_group_check=True)
                    return last
                tphg = P.do("tensor", f_hg, wait=[(sV, tmxg)], inc=sP)

                def f_hg2(e):
                    e.activation(tmpA, PCD[:, 0:256], AF.Tanh, scale=0.5)
                    return None
                tahg0 = P.do("scalar", f_hg2, wait=[(sP, tphg)])
                tahg = P.do("scalar",
                            lambda e: e.activation(hg, tmpA, AF.Copy,
                                                   scale=0.5, bias=0.5),
                            inc=sA)

                def f_w2all(e):
                    last = None
                    for mb in range(8):
                        ps = (PAB[:, (mb % 4) * 256:(mb % 4) * 256 + 256]
                              if mb < 4 else
                              PCD[:, (mb % 4) * 256:(mb % 4) * 256 + 256])
                        last = e.matmul(ps, w2b[:, mb * 128:(mb + 1) * 128],
                                        hw, start=(mb in (0, 2, 4, 6)),
                                        stop=True, skip_group_check=True)
                    return last
                tpw = P.do("tensor", f_w2all,
                           wait=[(sA, tahw), (sA, tahg)], inc=sP)

                def f_Wall(e):
                    for mb in range(8):
                        ps = (PAB[:, (mb % 4) * 256:(mb % 4) * 256 + 256]
                              if mb < 4 else
                              PCD[:, (mb % 4) * 256:(mb % 4) * 256 + 256])
                        e.activation(kb(Wt[:], mb), ps, AF.Tanh,
                                     bias=aux[:, mb:mb + 1], scale=0.5)
                    last = None
                    for half in range(2):
                        last = e.activation(
                            kb2x(Wt[:], half), kb2x(Wt[:], half),
                            AF.Copy, scale=EMB * 0.5, bias=EMB * 0.5)
                    return last
                t_wev = P.do("scalar", f_Wall, wait=[(sP, tpw)], inc=sA)

                def f_a2all(e):
                    last = None
                    for mb in range(8):
                        ps = (PAB[:, (mb % 4) * 256:(mb % 4) * 256 + 256]
                              if mb < 4 else
                              PCD[:, (mb % 4) * 256:(mb % 4) * 256 + 256])
                        last = e.matmul(ps, a2b[:, mb * 128:(mb + 1) * 128],
                                        ha, start=(mb in (0, 2, 4, 6)),
                                        stop=True, skip_group_check=True)
                    return last
                tpa = P.do("tensor", f_a2all,
                           wait=[(sA, taha), (sA, t_wev)], inc=sP)

                def f_aall(e):
                    last = None
                    for mb in range(8):
                        ps = (PAB[:, (mb % 4) * 256:(mb % 4) * 256 + 256]
                              if mb < 4 else
                              PCD[:, (mb % 4) * 256:(mb % 4) * 256 + 256])
                        last = e.activation(kb(sqt, mb), ps, AF.Tanh,
                                            bias=aux[:, 8 + mb:9 + mb],
                                            scale=0.5)
                    return last
                ta2 = P.do("scalar", f_aall, wait=[(sP, tpa)], inc=sA)
                t_aev = P.do("vector",
                             lambda e: e.tensor_scalar(dxt, sqt, 0.5, 0.5,
                                                       ALU.mult, ALU.add),
                             wait=[(sA, ta2)], inc=sV)

                def f_g2all(e):
                    last = None
                    for mb in range(8):
                        ps = (PAB[:, (mb % 4) * 256:(mb % 4) * 256 + 256]
                              if mb < 4 else
                              PCD[:, (mb % 4) * 256:(mb % 4) * 256 + 256])
                        last = e.matmul(ps, g2b[:, mb * 128:(mb + 1) * 128],
                                        hg, start=(mb in (0, 2, 4, 6)),
                                        stop=True, skip_group_check=True)
                    return last
                tpg = P.do("tensor", f_g2all,
                           wait=[(sA, tahg), (sA, ta2), (sV, t_aev)], inc=sP)

                def f_gev(e):
                    e.tensor_copy(kb2x(m_g[:], 0), PAB)
                    return e.tensor_copy(kb2x(m_g[:], 1), PCD)
                t_gev = P.do("vector", f_gev, wait=[(sP, tpg)], inc=sV)

                def f_t1(e):
                    last = None
                    for k in range(NKB):
                        last = e.tensor_scalar(kb(m_kk[:], k), kb(m_kraw[:], k),
                                               vcol("k_k", k), None, ALU.mult)
                    return last
                tkk = P.do("vector", f_t1, wait=[(sV, ek[0]), (sV, ek[1])],
                           inc=sV)
                tkq = P.do("scalar",
                           lambda e: e.activation(sqt, m_kk[:], AF.Square),
                           wait=[(sV, tkk), (sV, t_gev), (sP, tphg)], inc=sA)

                def f_ksum(e):
                    last = None
                    for k in range(NKB):
                        last = e.matmul(PS[0:16, 0:256],
                                        e16[:, k * 16:(k + 1) * 16],
                                        kb(sqt, k), start=(k == 0),
                                        stop=(k == 7), skip_group_check=True)
                    return last
                tks = P.do("tensor", f_ksum, wait=[(sA, tkq)], inc=sP)

                def f_kn1(e):
                    e.activation(tmpA[0:16, :], PS[0:16, 0:256], AF.Ln)
                    return e.activation(tmpA[0:16, :], tmpA[0:16, :], AF.Exp,
                                        scale=0.5)
                tkn = P.do("scalar", f_kn1, wait=[(sP, tks)], inc=sA)

                def f_kn2(e):
                    e.tensor_scalar_max(tmpA[0:16, :], tmpA[0:16, :], 1e-12)
                    e.reciprocal(tmpB[0:16, :], tmpA[0:16, :])
                    e.tensor_copy(sthi[0:16, 0:256], tmpB[0:16, :])
                    e.tensor_sub(tmpB[0:16, :], tmpB[0:16, :],
                                 sthi[0:16, 0:256])
                    return e.tensor_copy(sthi[0:16, 256:512], tmpB[0:16, :])
                tkn2 = P.do("vector", f_kn2, wait=[(sA, tkn)], inc=sV)

                def f_bcall(e):
                    last = None
                    for k in range(NKB):
                        ps = (PAB[:, (k % 4) * 256:(k % 4) * 256 + 256]
                              if k < 4 else
                              PCD[:, (k % 4) * 256:(k % 4) * 256 + 256])
                        fsl = f16[:, k * 128:(k + 1) * 128]
                        e.matmul(ps, fsl, sthi[0:16, 0:256],
                                 start=(k in (0, 2, 4, 6)), stop=False,
                                 skip_group_check=True)
                        last = e.matmul(ps, fsl, sthi[0:16, 256:512],
                                        start=False, stop=True,
                                        skip_group_check=True)
                    return last
                tbc = P.do("tensor", f_bcall,
                           wait=[(sV, tkn2), (sA, ta2)], inc=sP)

                def f_kkall(e):
                    e.tensor_mul(kb2x(m_kk[:], 0), kb2x(m_kk[:], 0), PAB)
                    e.tensor_mul(kb2x(m_kk[:], 1), kb2x(m_kk[:], 1), PCD)
                    for k in range(NKB):
                        e.tensor_scalar(kb(x1f, k), kb(dxt, k),
                                        vcol("k_a", k),
                                        aux[:, 16 + k:17 + k], ALU.mult,
                                        ALU.add)
                    e.tensor_mul(m_kp[:], m_kraw[:], x1f)
                    return e.tensor_mul(m_b[:], m_kk[:], dxt)
                t_kbev = P.do("vector", f_kkall, wait=[(sP, tbc)], inc=sV)

                if "x1" in dbg_d and st == 0:
                    dma(stsl(dbg_d["x1"].rearrange("(a p) t -> p a t", p=128),
                             0), r3(x1t), wait=[(sV, tln1)])
                    for nm, tl in (("r", m_r), ("kp", m_kp), ("kk", m_kk),
                                   ("bb", m_b), ("v", m_v), ("g", m_g)):
                        if nm in dbg_d:
                            dma(stsl(dbg_d[nm].rearrange(
                                "(a p) t -> p a t", p=128), 0),
                                r3(tl), wait=[(sV, t_kbev)], eng="gpsimd")
                    if "W" in dbg_d:
                        dma(stsl(dbg_d["W"].rearrange(
                            "(a p) t -> p a t", p=128), 0),
                            r3(Wt), wait=[(sA, t_wev), (sV, t_kbev)])

                ARr = r3(AR)
                BSr = r3(BS)
                cbr = cbuf.rearrange("p (a q) -> p a q", a=8)
                cir = r3(cibuf)
                ktr = r3(ktil)
                scr = r3(scum)
                sol = [r3(sol0), r3(sol1)]
                tsu = 0
                for ch in range(NCH):
                    c0 = ch * LCH

                    def chsl(tile, k=None, c0=c0):
                        rr = tile.rearrange("p (a t) -> p a t", a=8)
                        if k is None:
                            return rr[:, :, c0:c0 + LCH]
                        return rr[:, k, c0:c0 + LCH]

                    def f_cum(e, c0=c0):
                        last = None
                        for k in range(NKB):
                            last = e.tensor_tensor_scan(
                                scr[:, k, :], chsl(Wt[:], k), zer, 0.0,
                                ALU.add, ALU.add)
                        return last
                    wcu = [(sV, t_kbev), (sA, t_wev)] if ch == 0 \
                        else [(sV, tsu)]
                    tcu = P.do("vector", f_cum, wait=wcu, inc=sV)
                    ton = P.do("vector",
                               lambda e: e.memset(cbr[:, :, 0:1], 1.0),
                               inc=sV)

                    def f_exp(e):
                        e.activation(cbr[:, :, 1:65], scr[:], AF.Exp,
                                     scale=-1.0)
                        e.activation(cir[:], scr[:], AF.Exp, scale=1.0)
                        return e.activation(
                            clast.rearrange("p (a o) -> p a o", o=1),
                            scr[:, :, 63:64], AF.Exp, scale=-1.0)
                    tex = P.do("scalar", f_exp,
                               wait=[(sV, tcu), (sV, ton)], inc=sA)

                    def f_scales(e, c0=c0):
                        e.scalar_tensor_tensor(ARr[:, :, 0:64],
                                               chsl(m_kk[:]), -1.0,
                                               cbr[:, :, 0:64], ALU.mult,
                                               ALU.mult)
                        e.tensor_mul(ARr[:, :, 64:128], chsl(m_r[:]),
                                     cbr[:, :, 1:65])
                        e.tensor_mul(BSr[:, :, 0:64], chsl(m_b[:]), cir[:])
                        return e.tensor_mul(ktr[:], chsl(m_kp[:]), cir[:])
                    tsc = P.do("vector", f_scales, wait=[(sA, tex)], inc=sV)
                    if st == 0 and ch == 0 and "sar" in dbg_d:
                        for nm, tl in (("sar", AR), ("sbs", BS),
                                       ("skt", ktil), ("scb", cbuf),
                                       ("sci", cibuf)):
                            dma(dbg_d[nm], tl, wait=[(sV, tsc)],
                                eng="gpsimd")
                        bar2()

                    def f_tr(e, c0=c0):
                        last = None
                        for k in range(NKB):
                            for hh in range(2):
                                p0 = hh * 64
                                tp = (p0, p0) if hh else None
                                last = e.transpose(
                                    PT[p0:p0 + 64, k * 64:(k + 1) * 64],
                                    chsl(m_v[:], k)[p0:p0 + 64, :],
                                    idb[p0:p0 + 64, :], tile_position=tp)
                                last = e.transpose(
                                    PT[p0:p0 + 64,
                                       512 + k * 64:512 + (k + 1) * 64],
                                    BSr[:, k, 0:64][p0:p0 + 64, :],
                                    idb[p0:p0 + 64, :], tile_position=tp)
                                last = e.transpose(
                                    PT2[p0:p0 + 64, k * 64:(k + 1) * 64],
                                    ktr[:, k, :][p0:p0 + 64, :],
                                    idb[p0:p0 + 64, :], tile_position=tp)
                        return last
                    ttr = P.do("tensor", f_tr, wait=[(sV, tsc)], inc=sP)

                    def f_trev(e):
                        e.tensor_copy(vbtm, PT[:])
                        return e.tensor_copy(ktm, PT2[:, 0:512])
                    ttre = P.do("vector", f_trev, wait=[(sP, ttr)], inc=sV)

                    def f_g1(e):
                        last = None
                        for k in range(NKB):
                            for hh in range(2):
                                p0 = hh * 64
                                tp = (p0, p0) if hh else None
                                last = e.matmul(
                                    PAB[p0:p0 + 64, k * 128:k * 128 + 64],
                                    ARr[:, k, 0:64][p0:p0 + 64, :],
                                    BSr[:, k, 0:64][p0:p0 + 64, :],
                                    start=(k in (0, 4)),
                                    stop=True, tile_position=tp,
                                    skip_group_check=True)
                                last = e.matmul(
                                    PAB[p0:p0 + 64, k * 128 + 64:k * 128 + 128],
                                    ARr[:, k, 0:64][p0:p0 + 64, :],
                                    BSr[:, k, 64:128][p0:p0 + 64, :],
                                    start=False,
                                    stop=True, tile_position=tp,
                                    skip_group_check=True)
                        return last
                    tg1 = P.do("tensor", f_g1, wait=[(sV, tsc)], inc=sP)

                    def f_g1ev(e):
                        last = None
                        for k in range(NKB):
                            last = e.tensor_mul(sol[0][:, k, 64:128],
                                                PAB[:, k * 128:k * 128 + 64],
                                                mk1)
                        return last
                    tg1e = P.do("vector", f_g1ev, wait=[(sP, tg1)], inc=sV)

                    te_w = tg1e
                    for wave in range(2):
                        def f_g2(e, wave=wave):
                            last = None
                            for i, k in enumerate(range(wave * 4,
                                                        wave * 4 + 4)):
                                off = i * 256
                                for hh in range(2):
                                    p0 = hh * 64
                                    tp = (p0, p0) if hh else None
                                    for half in range(2):
                                        last = e.matmul(
                                            PCD[p0:p0 + 64,
                                                off + half * 64:
                                                off + half * 64 + 64],
                                            BSr[:, k, 0:64][p0:p0 + 64, :],
                                            ARr[:, k, half * 64:
                                                half * 64 + 64][p0:p0 + 64, :],
                                            start=(half == 0 and i in (0, 2)),
                                            stop=True, tile_position=tp,
                                            skip_group_check=True)
                                    for half in range(2):
                                        last = e.matmul(
                                            PCD[p0:p0 + 64,
                                                off + 128 + half * 64:
                                                off + 128 + half * 64 + 64],
                                            ktr[:, k, :][p0:p0 + 64, :],
                                            ARr[:, k, half * 64:
                                                half * 64 + 64][p0:p0 + 64, :],
                                            start=False, stop=True,
                                            tile_position=tp,
                                            skip_group_check=True)
                            return last
                        tw = P.do("tensor", f_g2,
                                  wait=[(sV, te_w)] if wave else
                                  [(sV, tsc)], inc=sP)

                        def f_g2ev(e, wave=wave):
                            last = None
                            for i, k in enumerate(range(wave * 4,
                                                        wave * 4 + 4)):
                                off = i * 256
                                e.tensor_mul(sol[0][:, k, 128:192],
                                             PCD[:, off:off + 64],
                                             mk4[:, 0:64])
                                last = e.tensor_mul(
                                    gp3[:, k * 192:k * 192 + 192],
                                    PCD[:, off + 64:off + 256],
                                    mk4[:, 64:256])
                            return last
                        te_w = P.do("vector", f_g2ev, wait=[(sP, tw)], inc=sV)

                    def f_ost(e):
                        last = None
                        for k in range(NKB):
                            for hh in range(2):
                                p0 = hh * 64
                                tp = (p0, p0) if hh else None
                                last = e.matmul(
                                    PO[p0:p0 + 64, k * 64:(k + 1) * 64],
                                    ARr[:, k, 64:128][p0:p0 + 64, :],
                                    BSr[:, k, 64:128][p0:p0 + 64, :],
                                    start=(k == 0), stop=True,
                                    tile_position=tp, skip_group_check=True)
                        return last
                    tos = P.do("tensor", f_ost,
                               wait=[(sV, tsc), (sV, t_kbev)], inc=sP)

                    def f_kap(e):
                        last = None
                        for k in range(NKB):
                            for hh in range(2):
                                p0 = hh * 64
                                tp = (p0, p0) if hh else None
                                last = e.matmul(
                                    PAB[p0:p0 + 64,
                                        k * 128 + 64:k * 128 + 128],
                                    gp3[:, k * 192 + 64:
                                        k * 192 + 128][p0:p0 + 64, :],
                                    vbtm[:, k * 64:(k + 1) * 64][p0:p0 + 64, :],
                                    start=False, stop=True, tile_position=tp,
                                    skip_group_check=True)
                        return last
                    tka = P.do("tensor", f_kap,
                               wait=[(sV, te_w), (sV, tg1e), (sV, ttre)],
                               inc=sP)

                    def f_x0(e):
                        last = None
                        for k in range(NKB):
                            last = e.tensor_copy(
                                sol[0][:, k, 0:64],
                                PAB[:, k * 128 + 64:k * 128 + 128])
                        return last
                    tlv = P.do("vector", f_x0, wait=[(sP, tka)], inc=sV)
                    if st == 0 and ch == 0 and "spre" in dbg_d:
                        dma(dbg_d["spre"], sol0, wait=[(sV, tlv)],
                            eng="gpsimd")
                        bar2()

                    for j in range(6):
                        cb = j % 2
                        nb = 1 - cb
                        dual = (j <= 3)

                        def f_slv(e, cb=cb, dual=dual, j=j):
                            last = None
                            for k in range(NKB):
                                for hh in range(2):
                                    p0 = hh * 64
                                    tp = (p0, p0) if hh else None
                                    nh = 2 if dual else 1
                                    for half in range(nh):
                                        last = e.matmul(
                                            PAB[p0:p0 + 64,
                                                k * 128 + half * 64:
                                                k * 128 + half * 64 + 64],
                                            sol[cb][:, k, 128:192][p0:p0 + 64, :],
                                            sol[cb][:, k, half * 64:
                                                    half * 64 + 64][p0:p0 + 64, :],
                                            start=(half == 0 and k in (0, 4)),
                                            stop=True, tile_position=tp,
                                            skip_group_check=True)
                                    if j <= 4:
                                        last = e.matmul(
                                            PCD[p0:p0 + 64,
                                                k * 64:(k + 1) * 64],
                                            sol[cb][:, k, 64:128][p0:p0 + 64, :],
                                            sol[cb][:, k, 128:192][p0:p0 + 64, :],
                                            start=(k == 0),
                                            stop=True, tile_position=tp,
                                            skip_group_check=True)
                            return last
                        tslv = P.do("tensor", f_slv, wait=[(sV, tlv)], inc=sP)

                        def f_sev(e, cb=cb, nb=nb, dual=dual, j=j):
                            if dual:
                                src = sol[cb].rearrange(
                                    "p a (u q) -> p a u q", u=4)
                                in1 = src[:, :, 0:4:3, :]
                                e.tensor_add(
                                    sol[nb][:, :, 0:128].rearrange(
                                        "p a (u q) -> p a u q", u=2),
                                    PAB.rearrange("p (a u q) -> p a u q",
                                                  a=8, u=2), in1)
                            else:
                                e.tensor_add(
                                    sol[nb][:, :, 0:64],
                                    PAB.rearrange("p (a q) -> p a q",
                                                  a=8)[:, :, 0:64],
                                    sol[cb][:, :, 0:64])
                            if j <= 4:
                                return e.tensor_copy(
                                    sol[nb][:, :, 128:192],
                                    PCD[:, 0:512].rearrange(
                                        "p (a q) -> p a q", q=64))
                            return e.tensor_copy(sol[nb][:, :, 64:128],
                                                 sol[cb][:, :, 64:128])
                        tlv = P.do("vector", f_sev, wait=[(sP, tslv)], inc=sV)

                    if st == 0 and ch == 0 and "ssol" in dbg_d:
                        dma(dbg_d["ssol"], sol0, wait=[(sV, tlv)],
                            eng="gpsimd")
                        dma(dbg_d["sgp3"], gp3, wait=[(sV, tlv)],
                            eng="gpsimd")
                        bar2()
                    ubuf = sol[0]

                    def f_outap(e):
                        last = None
                        for k in range(NKB):
                            for hh in range(2):
                                p0 = hh * 64
                                tp = (p0, p0) if hh else None
                                last = e.matmul(
                                    PO[p0:p0 + 64, k * 64:(k + 1) * 64],
                                    gp3[:, k * 192:k * 192 + 64][p0:p0 + 64, :],
                                    ubuf[:, k, 0:64][p0:p0 + 64, :],
                                    start=False, stop=True, tile_position=tp,
                                    skip_group_check=True)
                                last = e.matmul(
                                    PO[p0:p0 + 64, k * 64:(k + 1) * 64],
                                    gp3[:, k * 192 + 128:
                                        k * 192 + 192][p0:p0 + 64, :],
                                    vbtm[:, k * 64:(k + 1) * 64][p0:p0 + 64, :],
                                    start=False, stop=True, tile_position=tp,
                                    skip_group_check=True)
                        return last
                    toa = P.do("tensor", f_outap, wait=[(sV, tlv)], inc=sP)

                    def f_ds(e):
                        last = None
                        for k in range(NKB):
                            for hh in range(2):
                                p0 = hh * 64
                                tp = (p0, p0) if hh else None
                                last = e.matmul(
                                    PS[p0:p0 + 64, k * 64:(k + 1) * 64],
                                    vbtm[:, 512 + k * 64:
                                         512 + (k + 1) * 64][p0:p0 + 64, :],
                                    ubuf[:, k, 0:64][p0:p0 + 64, :],
                                    start=(k == 0), stop=True,
                                    tile_position=tp, skip_group_check=True)
                                last = e.matmul(
                                    PS[p0:p0 + 64, k * 64:(k + 1) * 64],
                                    ktm[:, k * 64:(k + 1) * 64][p0:p0 + 64, :],
                                    vbtm[:, k * 64:(k + 1) * 64][p0:p0 + 64, :],
                                    start=False, stop=True, tile_position=tp,
                                    skip_group_check=True)
                        return last
                    tds = P.do("tensor", f_ds, wait=[(sV, tlv)], inc=sP)

                    tooe = P.do("vector", lambda e: e.tensor_copy(otm, PO[:]),
                                wait=[(sP, toa)], inc=sV)
                    if st == 0 and ch == 0 and "sotm" in dbg_d:
                        dma(dbg_d["sotm"], otm, wait=[(sV, tooe)],
                            eng="gpsimd")
                        bar2()

                    def f_otr(e):
                        last = None
                        for k in range(NKB):
                            for hh in range(2):
                                p0 = hh * 64
                                tp = (p0, p0) if hh else None
                                last = e.transpose(
                                    PT2[p0:p0 + 64,
                                        512 + k * 64:512 + (k + 1) * 64],
                                    otm[p0:p0 + 64, k * 64:(k + 1) * 64],
                                    idb[p0:p0 + 64, :], tile_position=tp)
                        return last
                    totr = P.do("tensor", f_otr, wait=[(sV, tooe)], inc=sP)
                    tofm = P.do(
                        "vector",
                        lambda e, c0=c0: e.tensor_copy(
                            r3(m_o[:])[:, :, c0:c0 + LCH],
                            PT2.rearrange("p (w a q) -> p w a q",
                                          w=2, a=8)[:, 1, :, :]),
                        wait=[(sP, totr)], inc=sV)

                    def f_sup(e):
                        sf = r3(sfp)
                        e.tensor_add(sfp, sfp, PS[:])
                        for k in range(NKB):
                            e.tensor_scalar(sf[:, k, :], sf[:, k, :],
                                            clast[:, k:k + 1], None, ALU.mult)
                        return e.tensor_copy(BSr[:, :, 64:128], sfp)
                    tsu = P.do("vector", f_sup,
                               wait=[(sP, tds), (sV, tofm)], inc=sV)

                if "o" in dbg_d:
                    dma(stsl(dbg_d["o"].rearrange("(a p) t -> p a t", p=128),
                             st), r3(m_o), wait=[(sV, tsu)], eng="gpsimd")

                tq3 = P.do("scalar",
                           lambda e: e.activation(sqt, m_o[:], AF.Square),
                           wait=[(sV, tsu)], inc=sA)

                def f_osum(e):
                    for k in range(NKB):
                        e.matmul(PS[0:16, 0:256], e16[:, k * 16:(k + 1) * 16],
                                 kb(m_o[:], k), start=(k == 0), stop=False,
                                 skip_group_check=True)
                    last = None
                    for k in range(NKB):
                        last = e.matmul(PS[0:16, 256:512],
                                        e16[:, k * 16:(k + 1) * 16],
                                        kb(sqt, k), start=False, stop=(k == 7),
                                        skip_group_check=True)
                    return last
                tp5 = P.do("tensor", f_osum, wait=[(sA, tq3)], inc=sP)

                def f_gnst(e):
                    e.tensor_scalar(tmpA[0:16, :], PS[0:16, 0:256], 1.0 / 64,
                                    None, ALU.mult)
                    e.tensor_mul(tmpB[0:16, :], tmpA[0:16, :], tmpA[0:16, :])
                    e.scalar_tensor_tensor(tmpB[0:16, :], PS[0:16, 256:512],
                                           1.0 / 64, tmpB[0:16, :], ALU.mult,
                                           ALU.subtract)
                    e.tensor_copy(sthi[0:16, 0:256], tmpA[0:16, :])
                    return e.tensor_sub(tmpA[0:16, :], tmpA[0:16, :],
                                        sthi[0:16, 0:256])
                tv7 = P.do("vector", f_gnst, wait=[(sP, tp5)], inc=sV)

                def f_gnr(e):
                    e.activation(tmpC[0:16, :], tmpB[0:16, :], AF.Ln,
                                 bias=aux[0:16, 25:26])
                    return e.activation(tmpC[0:16, :], tmpC[0:16, :], AF.Exp,
                                        scale=-0.5)
                ta3 = P.do("scalar", f_gnr, wait=[(sV, tv7)], inc=sA)

                def f_gnhl(e):
                    e.tensor_copy(sthi[0:16, 256:512], tmpA[0:16, :])
                    e.tensor_copy(sthi[0:16, 512:768], tmpC[0:16, :])
                    e.tensor_sub(tmpC[0:16, :], tmpC[0:16, :],
                                 sthi[0:16, 512:768])
                    return e.tensor_copy(sthi[0:16, 768:1024], tmpC[0:16, :])
                tv8 = P.do("vector", f_gnhl, wait=[(sA, ta3)], inc=sV)

                def f_rkk(e):
                    last = None
                    for k in range(NKB):
                        last = e.scalar_tensor_tensor(
                            kb(m_kk[:], k), kb(m_r[:], k), vcol("r_k", k),
                            kb(m_kp[:], k), ALU.mult, ALU.mult)
                    return last
                tv10 = P.do("vector", f_rkk, wait=[(sV, tv8)], inc=sV)

                def f_qsum(e):
                    last = None
                    for k in range(NKB):
                        last = e.matmul(PS[0:16, 0:256],
                                        e16[:, k * 16:(k + 1) * 16],
                                        kb(m_kk[:], k), start=(k == 0),
                                        stop=(k == 7), skip_group_check=True)
                    return last
                tp6 = P.do("tensor", f_qsum, wait=[(sV, tv10)], inc=sP)

                def f_qhl(e):
                    e.tensor_copy(sthi[0:16, 1024:1280], PS[0:16, 0:256])
                    e.tensor_sub(tmpA[0:16, :], PS[0:16, 0:256],
                                 sthi[0:16, 1024:1280])
                    return e.tensor_copy(sthi[0:16, 1280:1536],
                                         tmpA[0:16, :])
                tv11 = P.do("vector", f_qhl, wait=[(sP, tp6)], inc=sV)

                t_pev = 0
                for k in range(NKB):
                    def f_bc2(e, k=k):
                        fsl = f16[:, k * 128:(k + 1) * 128]
                        e.matmul(PO[:, 0:256], fsl, sthi[0:16, 0:256],
                                 start=True, stop=False, skip_group_check=True)
                        e.matmul(PO[:, 0:256], fsl, sthi[0:16, 256:512],
                                 start=False, stop=True, skip_group_check=True)
                        e.matmul(PO[:, 256:512], fsl, sthi[0:16, 512:768],
                                 start=False, stop=False,
                                 skip_group_check=True)
                        e.matmul(PO[:, 256:512], fsl, sthi[0:16, 768:1024],
                                 start=False, stop=True, skip_group_check=True)
                        e.matmul(PCD[:, 0:256], fsl, sthi[0:16, 1024:1280],
                                 start=True, stop=False, skip_group_check=True)
                        return e.matmul(PCD[:, 0:256], fsl,
                                        sthi[0:16, 1280:1536], start=False,
                                        stop=True, skip_group_check=True)
                    w = [(sV, tv11)] if k == 0 else [(sV, t_pev)]
                    tb2 = P.do("tensor", f_bc2, wait=w, inc=sP)

                    def f_pap(e, k=k):
                        e.tensor_sub(tmpA, kb(m_o[:], k), PO[:, 0:256])
                        e.scalar_tensor_tensor(tmpB, tmpA, vcol("gn_w", k),
                                               PO[:, 256:512], ALU.mult,
                                               ALU.mult)
                        e.tensor_mul(tmpA, kb(m_v[:], k), PCD[:, 0:256])
                        e.scalar_tensor_tensor(tmpA, tmpA, vcol("gn_b", k),
                                               tmpB, ALU.add, ALU.add)
                        return e.tensor_mul(kb(m_o[:], k), tmpA,
                                            kb(m_g[:], k))
                    t_pev = P.do("vector", f_pap, wait=[(sP, tb2)], inc=sV)

                t_x2ev = 0
                for mb in range(8):
                    ps = PAB[:, 0:256] if mb % 2 == 0 else PAB[:, 512:768]
                    w = [(sV, t_pev), (sDg, t_wo_st)] if mb < 2 \
                        else [(sV, t_x2ev)]

                    def f_wo(e, mb=mb, ps=ps):
                        last = None
                        for k in range(NKB):
                            last = e.matmul(
                                ps, slotA[:, (k * 8 + mb) * 128:
                                          (k * 8 + mb) * 128 + 128],
                                kb(m_o[:], k), start=(k == 0), stop=(k == 7),
                                skip_group_check=True)
                        return last
                    two = P.do("tensor", f_wo, wait=w, inc=sP)
                    t_x2ev = P.do("vector",
                                  lambda e, mb=mb, ps=ps: e.tensor_add(
                                      kb(x1t[:], mb), kb(x1t[:], mb), ps),
                                  wait=[(sP, two)], inc=sV)

                dma(stsl(x2_r, st), r3(x1t), wait=[(sV, t_x2ev)])
                if st + 1 < nst:
                    t_wr_st = dma(slotA_r,
                                  wr_d.rearrange("(a p) m -> p a m", p=128),
                                  wait=[(sV, t_x2ev)], eng="gpsimd")
                if "x2" in dbg_d:
                    dma(stsl(dbg_d["x2"].rearrange("(a p) t -> p a t", p=128),
                             st), r3(x1t), wait=[(sV, t_x2ev)])
            return None

        def emit_pb(P, ctx, dma, bar2, sDs, sDg, sP, sV, sA):
            (x0a, x0b, x1t, Wt, scum, tmpA, tmpB, tmpC, x0f, x0g, x1f, xnt,
             dxt, sqt, m_r, m_kraw, m_kp, m_kk, m_b, m_v, m_g, m_o, hw, ha,
             hg, cxn, cbuf, cibuf, clast, AR, BS, ktil, vbtm, ktm, gp3, sol0,
             sol1, otm, sfp, wkb, wvb, slotA, w1b, a1b, g1b, w2b, a2b, g2b,
             PAB, PT, PT2, PCD, PS, PO) = ctx
            x2t = x0a
            x0t = x0b
            x2f = x0f
            xft = x1f
            kft = xnt
            h1f = x1t
            h1b = sqt
            sqb = m_g
            acA = hg
            acB = m_v
            idf16 = vbtm
            bar2()
            t_i16 = dma(tmpC[0:16, 0:16], if16_d)
            t_i128 = dma(Wt[:, 0:128], if128_d)
            t_rw = dma(a1b[:, 0:128].rearrange("p (a m) -> p a m", a=8),
                       rw_d.rearrange("(a p) m -> p a m", p=128), eng="gpsimd")
            t_s15 = dma(w2b[0:16, 0:128], s15_d)
            t_rb = dma(aux[0:16, 28:29], rb_d)
            tz = P.do("vector", lambda e: e.memset(cxn, 0.0), inc=sV)

            def kb(ap2d, k, w=ST):
                return ap2d[:, k * w:(k + 1) * w]

            def r3(tile, a=8):
                return tile.rearrange("p (a t) -> p a t", a=a)

            def stsl(rr, st, w=ST):
                return rr[:, :, st * w:(st + 1) * w]

            cmb_r = cmb_d
            t_wf = [0, 0, 0, 0]
            for st in range(nst):
                bar2()
                t_x2 = dma(r3(x2t), stsl(x2_r, st))
                t_x2b = dma(r3(x2f), stsl(x2_r, st), eng="gpsimd")
                t_x0 = dma(r3(x0t), stsl(x0_r, st))
                wx = [(sDs, t_x2), (sDg, t_x2b)]
                if st == 0:
                    wx += [(sDs, t_i16), (sDs, t_i128), (sDg, t_rw),
                           (sDs, t_rb), (sV, tz), (sDs, t_s15)]

                tq = P.do("scalar",
                          lambda e: e.activation(sqb, x2f, AF.Square),
                          wait=wx, inc=sA)

                def f_sums(e):
                    for k in range(NKB):
                        e.matmul(PS[0:1, 0:256], onc, kb(x2f, k),
                                 start=(k == 0), stop=False,
                                 skip_group_check=True)
                    last = None
                    for k in range(NKB):
                        last = e.matmul(PS[0:1, 256:512], onc, kb(sqb, k),
                                        start=False, stop=(k == 7),
                                        skip_group_check=True)
                    return last
                tp = P.do("tensor", f_sums, wait=wx + [(sA, tq)], inc=sP)

                def f_stats(e, n=C):
                    e.tensor_scalar(tmpA[0:1, :], PS[0:1, 0:256], 1.0 / n,
                                    None, ALU.mult)
                    e.tensor_mul(tmpB[0:1, :], tmpA[0:1, :], tmpA[0:1, :])
                    e.scalar_tensor_tensor(tmpB[0:1, :], PS[0:1, 256:512],
                                           1.0 / n, tmpB[0:1, :], ALU.mult,
                                           ALU.subtract)
                    e.tensor_copy(sthi[0:1, 0:256], tmpA[0:1, :])
                    return e.tensor_sub(tmpA[0:1, :], tmpA[0:1, :],
                                        sthi[0:1, 0:256])
                tv = P.do("vector", f_stats, wait=[(sP, tp)], inc=sV)

                def f_rstd(e):
                    e.activation(tmpB[0:1, :], tmpB[0:1, :], AF.Ln,
                                 bias=aux[0:1, 24:25])
                    return e.activation(tmpB[0:1, :], tmpB[0:1, :], AF.Exp,
                                        scale=-0.5)
                ta = P.do("scalar", f_rstd, wait=[(sV, tv)], inc=sA)

                def f_hl(e):
                    e.tensor_copy(sthi[0:1, 256:512], tmpA[0:1, :])
                    e.tensor_copy(sthi[0:1, 512:768], tmpB[0:1, :])
                    e.tensor_sub(tmpB[0:1, :], tmpB[0:1, :],
                                 sthi[0:1, 512:768])
                    return e.tensor_copy(sthi[0:1, 768:1024], tmpB[0:1, :])
                tv2 = P.do("vector", f_hl, wait=[(sA, ta)], inc=sV)

                def f_bcast(e):
                    e.matmul(PO[:, 0:256], onr, sthi[0:1, 0:256],
                             start=True, stop=False, skip_group_check=True)
                    e.matmul(PO[:, 0:256], onr, sthi[0:1, 256:512],
                             start=False, stop=True, skip_group_check=True)
                    e.matmul(PO[:, 256:512], onr, sthi[0:1, 512:768],
                             start=False, stop=False, skip_group_check=True)
                    return e.matmul(PO[:, 256:512], onr,
                                    sthi[0:1, 768:1024], start=False,
                                    stop=True, skip_group_check=True)
                tp2 = P.do("tensor", f_bcast, wait=[(sV, tv2)], inc=sP)

                def f_xf(e):
                    last = None
                    for k in range(NKB):
                        e.tensor_sub(tmpA, kb(x2t, k), PO[:, 0:256])
                        e.scalar_tensor_tensor(tmpB, tmpA,
                                               vcol("ln_ffn_w", k),
                                               PO[:, 256:512], ALU.mult,
                                               ALU.mult)
                        last = e.tensor_scalar(kb(xft, k), tmpB,
                                               vcol("ln_ffn_b", k), None,
                                               ALU.add)
                    return last
                tv3 = P.do("vector", f_xf, wait=[(sP, tp2)], inc=sV)

                def f_kf(e):
                    xv = r3(xft)
                    dv = r3(kft)
                    e.tensor_sub(dv[:, :, 1:ST], xv[:, :, 0:ST - 1],
                                 xv[:, :, 1:ST])
                    e.tensor_sub(dv[:, :, 0:1],
                                 cxn.rearrange("p (a o) -> p a o", o=1),
                                 xv[:, :, 0:1])
                    e.tensor_copy(cxn.rearrange("p (a o) -> p a o", o=1),
                                  xv[:, :, ST - 1:ST])
                    last = None
                    for k in range(NKB):
                        last = e.scalar_tensor_tensor(
                            kb(kft, k), kb(kft, k), vcol("ffn_x_k", k),
                            kb(xft, k), ALU.mult, ALU.add)
                    return last
                tv4 = P.do("vector", f_kf, wait=[(sV, tv3)], inc=sV)

                t_ev = 0
                dn_q = []
                t_rl_h = {}
                t_s2_h = {}
                t_dn = 0
                for hb in range(33):
                    if hb < 32:
                        par = hb % 2
                        w1sl = wkb[:, par * 1024:(par + 1) * 1024]
                        w2sl = wkb[:, 2048 + par * 1024:2048 + (par + 1) * 1024]
                        wwait = [] if hb < 2 else [(sP, dn_q[hb - 2][1])]
                        t_w1x = dma(
                            w1sl.rearrange("p (a m) -> p a m", a=8),
                            wf1_d.rearrange("(a p) m -> p a m",
                                            p=128)[:, :, hb * 128:(hb + 1) * 128],
                            wait=wwait, eng="gpsimd")
                        t_w2x = dma(
                            w2sl, wf2_d[hb * 128:(hb + 1) * 128, :],
                            wait=wwait, eng="gpsimd")
                        actps = PS[:, 0:256] if par == 0 else PO[:, 0:256]

                        def f_act(e, w1sl=w1sl, actps=actps):
                            last = None
                            for k in range(NKB):
                                last = e.matmul(actps,
                                                w1sl[:, k * 128:(k + 1) * 128],
                                                kb(kft, k), start=(k == 0),
                                                stop=(k == 7),
                                                skip_group_check=True)
                            return last
                        wv_ = [(sV, tv4), (sDg, t_w1x)]
                        if hb >= 2:
                            wv_.append((sV, t_s2_h[hb - 2]))
                        tac = P.do("tensor", f_act, wait=wv_, inc=sP)
                        act = acA if par == 0 else acB
                        t_rl_h[hb] = P.do(
                            "scalar",
                            lambda e, actps=actps, act=act: e.activation(
                                act[0:128, 0:256], actps, AF.Relu),
                            wait=[(sP, tac)], inc=sA)
                        t_s2_h[hb] = P.do(
                            "vector",
                            lambda e, act=act: e.tensor_mul(
                                act[0:128, 0:256], act[0:128, 0:256],
                                act[0:128, 0:256]),
                            wait=[(sA, t_rl_h[hb])], inc=sV)
                        dn_q.append((hb, None, w2sl, act, t_w2x))
                    if hb >= 1:
                        hbp, _, w2p, actp, t_w2p = dn_q[hb - 1]

                        def f_dn(e, w2sl=w2p, act=actp, hb2=hbp):
                            last = None
                            for mb in range(8):
                                ps = (PAB[:, (mb % 4) * 256:
                                          (mb % 4) * 256 + 256]
                                      if mb < 4 else
                                      PCD[:, (mb % 4) * 256:
                                          (mb % 4) * 256 + 256])
                                last = e.matmul(
                                    ps, w2sl[:, mb * 128:(mb + 1) * 128],
                                    act[0:128, 0:256],
                                    start=(hb2 == 0 and mb in (0, 2, 4, 6)),
                                    stop=(hb2 == 31), skip_group_check=True)
                            return last
                        t_dn = P.do("tensor", f_dn,
                                    wait=[(sV, t_s2_h[hbp]), (sDg, t_w2p)],
                                    inc=sP)
                        dn_q[hb - 1] = (hbp, t_dn, w2p, actp, t_w2p)
                def f_h1pre(e):
                    last = None
                    for mb in range(8):
                        ps = (PAB[:, (mb % 4) * 256:(mb % 4) * 256 + 256]
                              if mb < 4 else
                              PCD[:, (mb % 4) * 256:(mb % 4) * 256 + 256])
                        e.tensor_add(kb(x2t, mb), kb(x2t, mb), ps)
                        last = e.tensor_add(kb(x2t, mb), kb(x2t, mb),
                                            kb(x0t, mb))
                    return last
                tv5 = P.do("vector", f_h1pre, wait=[(sP, t_dn), (sDs, t_x0)],
                           inc=sV)

                def f_cp(e):
                    last = None
                    for k in range(NKB):
                        last = e.tensor_copy(kb(x2f, k), kb(x2t, k))
                    return last
                tv6 = P.do("vector", f_cp, wait=[(sV, tv5)], inc=sV)
                tq2 = P.do("scalar",
                           lambda e: e.activation(sqb, x2f, AF.Square),
                           wait=[(sV, tv6)], inc=sA)
                tp3 = P.do("tensor", f_sums, wait=[(sA, tq2)], inc=sP)
                tv = P.do("vector", f_stats, wait=[(sP, tp3)], inc=sV)
                ta = P.do("scalar", f_rstd, wait=[(sV, tv)], inc=sA)
                tv2 = P.do("vector", f_hl, wait=[(sA, ta)], inc=sV)
                tp4 = P.do("tensor", f_bcast, wait=[(sV, tv2)], inc=sP)

                def f_h1(e):
                    last = None
                    for k in range(NKB):
                        e.tensor_sub(tmpA, kb(x2t, k), PO[:, 0:256])
                        e.tensor_mul(kb(h1f, k), tmpA, PO[:, 256:512])
                        last = e.tensor_copy(kb(h1b, k), kb(h1f, k))
                    return last
                tv7 = P.do("vector", f_h1, wait=[(sP, tp4)], inc=sV)
                dma(stsl(h1_r, st), r3(h1f), wait=[(sV, tv7)])

                def f_lg(e):
                    last = None
                    for k in range(NKB):
                        last = e.matmul(PS[0:16, 0:256],
                                        a1b[:, k * 16:(k + 1) * 16],
                                        kb(h1b, k), start=(k == 0),
                                        stop=(k == 7), skip_group_check=True)
                    return last
                tp5 = P.do("tensor", f_lg, wait=[(sV, tv7)], inc=sP)
                tv8 = P.do("vector",
                           lambda e: e.tensor_scalar(
                               tmpA[0:16, :], PS[0:16, 0:256],
                               aux[0:16, 28:29], None, ALU.add),
                           wait=[(sP, tp5)], inc=sV)

                AX = mybir.AxisListType.X
                for half in range(2):
                    hsl = slice(half * 128, half * 128 + 128)
                    wtt = [(sV, tv8)]
                    if half == 1:
                        wtt.append((sV, tv11))

                    def f_tt1(e, hsl=hsl):
                        e.transpose(PO[:, 0:16], tmpA[0:16, hsl],
                                    tmpC[0:16, 0:16])
                        return e.transpose(PO[:, 32:48], tmpA[0:16, hsl],
                                           tmpC[0:16, 0:16])
                    tt1 = P.do("tensor", f_tt1, wait=wtt, inc=sP)
                    ta2 = P.do("scalar",
                               lambda e: e.activation(tmpB[:, 0:16],
                                                      PO[:, 0:16], AF.Exp),
                               wait=[(sP, tt1)], inc=sA)

                    def f_top2(v):
                        A2 = tmpB
                        B2 = tmpC[:, 100:200]
                        C2 = scum[:, 0:64]

                        def pad():
                            v.tensor_copy(B2[:, 50:52], B2[:, 52:54])
                            v.tensor_copy(B2[:, 52:54], B2[:, 54:56])
                        v.reduce_max(B2[:, 0:1], A2[:, 0:16], axis=AX)
                        pad()
                        v.reduce_sum(B2[:, 1:2], A2[:, 0:16], axis=AX)
                        pad()
                        v.tensor_scalar(A2[:, 16:17], B2[:, 0:1], -1.0,
                                        None, ALU.mult)
                        pad()
                        v.tensor_scalar(B2[:, 16:32], A2[:, 0:16],
                                        A2[:, 16:17], None, ALU.add)
                        pad()
                        v.tensor_scalar(A2[:, 32:48], B2[:, 16:32], 1e6,
                                        1.0, ALU.mult, ALU.add)
                        pad()
                        v.tensor_scalar_max(B2[:, 32:48], A2[:, 32:48], 0.0)
                        pad()
                        v.tensor_scalar_min(A2[:, 48:64], B2[:, 32:48], 1.0)
                        pad()
                        v.scalar_tensor_tensor(B2[:, 64:80], A2[:, 48:64],
                                               -1e6, A2[:, 0:16], ALU.mult,
                                               ALU.add)
                        pad()
                        v.reduce_max(A2[:, 17:18], B2[:, 64:80], axis=AX)
                        pad()
                        v.tensor_scalar(B2[:, 2:3], A2[:, 17:18], -1.0,
                                        None, ALU.mult)
                        pad()
                        v.tensor_tensor(C2[:, 0:1], B2[:, 0:1],
                                        A2[:, 17:18], ALU.add)
                        pad()
                        v.scalar_tensor_tensor(A2[:, 18:19], B2[:, 1:2],
                                               1e-6, C2[:, 0:1], ALU.mult,
                                               ALU.add)
                        pad()
                        v.reciprocal(B2[:, 3:4], A2[:, 18:19])
                        pad()
                        v.tensor_scalar(C2[:, 16:32], A2[:, 0:16],
                                        B2[:, 2:3], None, ALU.add)
                        pad()
                        v.tensor_scalar(B2[:, 16:32], C2[:, 16:32], 1e6,
                                        1.0, ALU.mult, ALU.add)
                        pad()
                        v.tensor_scalar_max(C2[:, 32:48], B2[:, 16:32], 0.0)
                        pad()
                        v.tensor_scalar_min(A2[:, 64:80], C2[:, 32:48], 1.0)
                        pad()
                        v.tensor_tensor(B2[:, 16:32], A2[:, 0:16],
                                        A2[:, 64:80], ALU.mult)
                        pad()
                        return v.tensor_scalar(A2[:, 80:96], B2[:, 16:32],
                                               B2[:, 3:4], None, ALU.mult)
                    tv10 = P.do("vector", f_top2, wait=[(sA, ta2)], inc=sV)

                    def f_tt2(e):
                        e.transpose(PS[0:16, 256:384], tmpB[:, 80:96],
                                    Wt[:, 0:128])
                        return e.transpose(PS[0:16, 384:512],
                                           tmpB[:, 80:96], Wt[:, 0:128])
                    tt2 = P.do("tensor", f_tt2, wait=[(sV, tv10)], inc=sP)
                    tv11 = P.do("vector",
                                lambda e, hsl=hsl: e.tensor_copy(
                                    tmpA[0:16, hsl], PS[0:16, 256:384]),
                                wait=[(sP, tt2)], inc=sV)
                dma(cmb_r[:, st * ST:(st + 1) * ST], tmpA[0:16, :],
                    wait=[(sV, tv11)])
                def f_c15(e):
                    e.tensor_copy(sthi[0:16, 1536:1792], tmpA[0:16, :])
                    e.tensor_sub(tmpB[0:16, :], tmpA[0:16, :],
                                 sthi[0:16, 1536:1792])
                    return e.tensor_copy(sthi[0:16, 1792:2048],
                                         tmpB[0:16, :])
                tv12 = P.do("vector", f_c15, wait=[(sV, tv11)], inc=sV)
                tbc15 = P.do("tensor",
                             lambda e: (
                                 e.matmul(PS[:, 256:512], w2b[0:16, 0:128],
                                          sthi[0:16, 1536:1792], start=True,
                                          stop=False, skip_group_check=True),
                                 e.matmul(PS[:, 256:512], w2b[0:16, 0:128],
                                          sthi[0:16, 1792:2048], start=False,
                                          stop=True,
                                          skip_group_check=True))[-1],
                             wait=[(sV, tv12)], inc=sP)

                def f_rt0(e):
                    last = None
                    for k in range(NKB):
                        last = e.tensor_mul(kb(x2f, k), kb(h1f, k),
                                            PS[:, 256:512])
                    return last
                tv13 = P.do("vector", f_rt0, wait=[(sP, tbc15)], inc=sV)
                dma(stsl(rtd_r, st), r3(x2f), wait=[(sV, tv13)],
                    eng="gpsimd")

        def emit_pc(P, ctx, dma, bar2, sDs, sDg, sP, sV, sA):
            (x0a, x0b, x1t, Wt, scum, tmpA, tmpB, tmpC, x0f, x0g, x1f, xnt,
             dxt, sqt, m_r, m_kraw, m_kp, m_kk, m_b, m_v, m_g, m_o, hw, ha,
             hg, cxn, cbuf, cibuf, clast, AR, BS, ktil, vbtm, ktm, gp3, sol0,
             sol1, otm, sfp, wkb, wvb, slotA, w1b, a1b, g1b, w2b, a2b, g2b,
             PAB, PT, PT2, PCD, PS, PO) = ctx
            hbf_ = [x0f, x0g]
            stg_ = [x0a, x1t]
            cmbL_ = [tmpB[0:16, :], tmpC[0:16, 0:256]]
            cmbB_ = [tmpA, Wt[:, 256:512]]

            def kb(ap2d, k, w=ST):
                return ap2d[:, k * w:(k + 1) * w]

            def r3(tile, a=8):
                return tile.rearrange("p (a t) -> p a t", a=a)

            def stsl(rr, st, w=ST):
                return rr[:, :, st * w:(st + 1) * w]

            bar2()
            t_sel = dma(selb, sel_d)
            t_wg = [0, 0]
            t_wu = [0, 0]
            t_wd = [0, 0]
            t_lastmm = [0, 0]
            hist = {}
            NIT = 15 * nst

            for it in range(NIT + 1):
                if it < NIT:
                    e_i, st = divmod(it, nst)
                    par = e_i % 2
                    bufp = it % 2
                    if st == 0:
                        ww = [] if e_i < 2 else [(sP, t_lastmm[par])]
                        gsl = wkb[:, par * 4096:(par + 1) * 4096]
                        usl = wvb[:, par * 4096:(par + 1) * 4096]
                        dsl = slotA[:, par * 4096:(par + 1) * 4096]
                        t_wg[par] = dma(gsl.rearrange("p (a m) -> p a m", a=8),
                                        egw_d[e_i].rearrange(
                                            "(a p) m -> p a m", p=128),
                                        wait=ww, eng="gpsimd")
                        t_wu[par] = dma(usl.rearrange("p (a m) -> p a m", a=8),
                                        euw_d[e_i].rearrange(
                                            "(a p) m -> p a m", p=128),
                                        eng="gpsimd")
                        t_wd[par] = dma(dsl.rearrange("p (a m) -> p a m", a=4),
                                        edw_d[e_i].rearrange(
                                            "(a p) m -> p a m", p=128),
                                        eng="gpsimd")
                    wh = []
                    if it >= 2:
                        wh = [(sP, hist[it - 2]["dn"]),
                              (sV, hist[it - 2]["chl"])]
                    t_h = dma(r3(hbf_[bufp][:]), stsl(h1_r, st),
                              eng="gpsimd", wait=wh)
                    t_cb = dma(cmbL_[bufp], cmb_d[:, st * ST:(st + 1) * ST],
                               wait=list(wh))
                    hist[it] = {"t_h": t_h, "t_cb": t_cb,
                                "wg": t_wg[par], "wu": t_wu[par],
                                "wd": t_wd[par], "e_i": e_i, "st": st,
                                "par": par, "bufp": bufp}

                    def f_chl(e, cl=cmbL_[bufp], bufp=bufp):
                        e.tensor_copy(sthi[0:16, bufp * 512:bufp * 512 + 256],
                                      cl)
                        e.tensor_sub(scum[0:16, bufp * 256:bufp * 256 + 256],
                                     cl,
                                     sthi[0:16, bufp * 512:bufp * 512 + 256])
                        return e.tensor_copy(
                            sthi[0:16, bufp * 512 + 256:bufp * 512 + 512],
                            scum[0:16, bufp * 256:bufp * 256 + 256])
                    hist[it]["chl"] = P.do("vector", f_chl,
                                           wait=[(sDs, t_cb)], inc=sV)
                    if it == 0:
                        Hn = hist[0]

                        def f_bc0(e, e_i2=Hn["e_i"], bufp2=Hn["bufp"]):
                            e.matmul(PS[:, 256:512],
                                     selb[:, e_i2 * 128:(e_i2 + 1) * 128],
                                     sthi[0:16,
                                          bufp2 * 512:bufp2 * 512 + 256],
                                     start=True, stop=False,
                                     skip_group_check=True)
                            return e.matmul(
                                PS[:, 256:512],
                                selb[:, e_i2 * 128:(e_i2 + 1) * 128],
                                sthi[0:16,
                                     bufp2 * 512 + 256:bufp2 * 512 + 512],
                                start=False, stop=True,
                                skip_group_check=True)
                        Hn["bc"] = P.do("tensor", f_bc0,
                                        wait=[(sV, Hn["chl"]),
                                              (sDs, t_sel)], inc=sP)
                        Hn["cbv"] = P.do(
                            "vector",
                            lambda e, bufp2=Hn["bufp"]: e.tensor_copy(
                                cmbB_[bufp2], PS[:, 256:512]),
                            wait=[(sP, Hn["bc"])], inc=sV)

                if it >= 1:
                    H = hist[it - 1]
                    e_i = H["e_i"]
                    st = H["st"]
                    par = H["par"]
                    bufp = H["bufp"]
                    gsl = wkb[:, par * 4096:(par + 1) * 4096]
                    usl = wvb[:, par * 4096:(par + 1) * 4096]
                    dsl = slotA[:, par * 4096:(par + 1) * 4096]
                    hbf = hbf_[bufp][:]
                    stg = stg_[bufp][:]
                    cbb = cmbB_[bufp]
                    t_ym = 0
                    t_si = 0
                    for ib in range(4):
                        gps = PS[:, 0:256] if ib % 2 == 0 else PO[:, 0:256]
                        ups = PO[:, 0:256] if ib % 2 == 0 else PS[:, 0:256]

                        def f_g(e, gsl=gsl, ib=ib, gps=gps, hbf=hbf):
                            last = None
                            for k in range(NKB):
                                last = e.matmul(
                                    gps, gsl[:, k * 512 + ib * 128:
                                             k * 512 + ib * 128 + 128],
                                    kb(hbf, k), start=(k == 0),
                                    stop=(k == 7), skip_group_check=True)
                            return last
                        wg = []
                        if ib == 0:
                            wg = [(sDg, H["wg"]), (sDg, H["wu"]),
                                  (sDg, H["t_h"]), (sV, H["cbv"])]
                        else:
                            wg.append((sA, t_si))
                            wg.append((sV, t_ym))
                        tg = P.do("tensor", f_g, wait=wg, inc=sP)
                        t_si = P.do("scalar",
                                    lambda e, gps=gps, ib=ib: e.activation(
                                        m_kraw[:, ib * 256:(ib + 1) * 256],
                                        gps, AF.Silu),
                                    wait=[(sP, tg)], inc=sA)

                        def f_u(e, usl=usl, ib=ib, ups=ups, hbf=hbf):
                            last = None
                            for k in range(NKB):
                                last = e.matmul(
                                    ups, usl[:, k * 512 + ib * 128:
                                             k * 512 + ib * 128 + 128],
                                    kb(hbf, k), start=(k == 0),
                                    stop=(k == 7), skip_group_check=True)
                            return last
                        tu = P.do("tensor", f_u, wait=[(sA, t_si)], inc=sP)

                        def f_ym(e, ib=ib, ups=ups, cbb=cbb):
                            e.tensor_tensor(
                                m_kp[:, ib * 256:(ib + 1) * 256],
                                m_kraw[:, ib * 256:(ib + 1) * 256], ups,
                                ALU.mult)
                            return e.tensor_mul(
                                m_kp[:, ib * 256:(ib + 1) * 256],
                                m_kp[:, ib * 256:(ib + 1) * 256], cbb)
                        t_ym = P.do("vector", f_ym, wait=[(sP, tu)], inc=sV)

                    if it < NIT and "bc" not in hist[it]:
                        Hn = hist[it]

                        def f_bc(e, e_i2=Hn["e_i"], bufp2=Hn["bufp"]):
                            e.matmul(PS[:, 256:512],
                                     selb[:, e_i2 * 128:(e_i2 + 1) * 128],
                                     sthi[0:16,
                                          bufp2 * 512:bufp2 * 512 + 256],
                                     start=True, stop=False,
                                     skip_group_check=True)
                            return e.matmul(
                                PS[:, 256:512],
                                selb[:, e_i2 * 128:(e_i2 + 1) * 128],
                                sthi[0:16,
                                     bufp2 * 512 + 256:bufp2 * 512 + 512],
                                start=False, stop=True,
                                skip_group_check=True)
                        Hn["bc"] = P.do("tensor", f_bc,
                                        wait=[(sV, Hn["chl"]), (sV, t_ym),
                                              (sDs, t_sel)], inc=sP)
                        Hn["cbv"] = P.do(
                            "vector",
                            lambda e, bufp2=Hn["bufp"]: e.tensor_copy(
                                cmbB_[bufp2], PS[:, 256:512]),
                            wait=[(sP, Hn["bc"])], inc=sV)

                    def f_d(e, dsl=dsl):
                        last = None
                        for mb in range(8):
                            ps = (PAB[:, (mb % 4) * 256:(mb % 4) * 256 + 256]
                                  if mb < 4 else
                                  PCD[:, (mb % 4) * 256:(mb % 4) * 256 + 256])
                            for ib in range(4):
                                last = e.matmul(
                                    ps, dsl[:, ib * 1024 + mb * 128:
                                            ib * 1024 + mb * 128 + 128],
                                    m_kp[:, ib * 256:(ib + 1) * 256],
                                    start=(ib == 0 and mb in (0, 2, 4, 6)),
                                    stop=(ib == 3), skip_group_check=True)
                        return last
                    wd = [(sV, t_ym), (sDg, H["wd"])]
                    if it - 2 >= 0:
                        wd.append((sV, hist[it - 2]["ev"]))
                    t_d = P.do("tensor", f_d, wait=wd, inc=sP)
                    t_lastmm[par] = t_d
                    H["dn"] = t_d

                    def f_ev(e, stg=stg):
                        last = None
                        for mb in range(8):
                            ps = (PAB[:, (mb % 4) * 256:(mb % 4) * 256 + 256]
                                  if mb < 4 else
                                  PCD[:, (mb % 4) * 256:(mb % 4) * 256 + 256])
                            last = e.tensor_copy(kb(stg, mb), ps)
                        return last
                    wev = [(sP, t_d)]
                    if it - 2 >= 0:
                        wev.append((sDg, hist[it - 2]["acc"]))
                    H["ev"] = P.do("vector", f_ev, wait=wev, inc=sV)
                    H["acc"] = P.do(
                        "gpsimd",
                        lambda e, st=st, stg=stg: e.dma_start(
                            out=stsl(rtd_r, st), in_=r3(stg),
                            accum_op=ALU.add),
                        wait=[(sV, H["ev"])], inc=sDg, inc_by=16)

        def emit_pd(P, ctx, dma, bar2, sDs, sDg, sP, sV, sA):
            (x0a, x0b, x1t, Wt, scum, tmpA, tmpB, tmpC, x0f, x0g, x1f, xnt,
             dxt, sqt, m_r, m_kraw, m_kp, m_kk, m_b, m_v, m_g, m_o, hw, ha,
             hg, cxn, cbuf, cibuf, clast, AR, BS, ktil, vbtm, ktm, gp3, sol0,
             sol1, otm, sfp, wkb, wvb, slotA, w1b, a1b, g1b, w2b, a2b, g2b,
             PAB, PT, PT2, PCD, PS, PO) = ctx
            hbf = x0f
            h1t = x0a
            rtt = x0b
            outt = x1t
            sg_t = m_r
            y_t = m_kp
            hprev = cxn
            bar2()

            def kb(ap2d, k, w=ST):
                return ap2d[:, k * w:(k + 1) * w]

            def r3(tile, a=8):
                return tile.rearrange("p (a t) -> p a t", a=a)

            def stsl(rr, st, w=ST):
                return rr[:, :, st * w:(st + 1) * w]

            t_w1 = dma(wkb[:, 0:4096].rearrange("p (a m) -> p a m", a=8),
                       wc_d[0].rearrange("(a p) m -> p a m", p=128),
                       eng="gpsimd")
            t_w2 = dma(wkb[:, 4096:8192].rearrange("p (a m) -> p a m", a=8),
                       wc_d[1].rearrange("(a p) m -> p a m", p=128),
                       eng="gpsimd")
            t_w3 = dma(wvb[:, 0:4096].rearrange("p (a m) -> p a m", a=8),
                       wc_d[2].rearrange("(a p) m -> p a m", p=128),
                       eng="gpsimd")
            t_w4 = dma(wvb[:, 4096:8192].rearrange("p (a m) -> p a m", a=8),
                       sup_d.rearrange("(a p) m -> p a m", p=128),
                       eng="gpsimd")
            t_w5 = dma(slotA[:, 0:4096].rearrange("p (a m) -> p a m", a=4),
                       sdw_d.rearrange("(a p) m -> p a m", p=128),
                       eng="gpsimd")
            hbr = hbx.rearrange("p (a q) -> p a q", q=260)
            tz = P.do("vector",
                      lambda e: e.memset(hbr[:, :, 0:2], 0.0), inc=sV)
            for st in range(nst):
                bar2()
                t_h = dma(hbr[:, :, 2:258], stsl(h1_r, st), eng="gpsimd")
                if st > 0:
                    t_h = dma(hbr[:, :, 0:2],
                              h1_r[:, :, st * ST - 2:st * ST], eng="gpsimd")
                t_hf = dma(r3(h1t), stsl(h1_r, st))
                t_rt = dma(r3(rtt), stsl(rtd_r, st))
                wx = [(sDg, t_h), (sDs, t_hf), (sDs, t_rt)]
                if st == 0:
                    wx += [(sDg, t_w1), (sDg, t_w2), (sDg, t_w3),
                           (sDg, t_w4), (sDg, t_w5), (sV, tz)]

                def gps_of(ib):
                    return (PS[:, 0:256], PS[:, 256:512], PO[:, 0:256],
                            PO[:, 256:512])[ib]

                def f_conv2(e):
                    last = None
                    for ib in range(4):
                        ps = gps_of(ib)
                        first = True
                        for wsh in range(3):
                            wsl = (wkb[:, 0:4096], wkb[:, 4096:8192],
                                   wvb[:, 0:4096])[wsh]
                            sh = 2 - wsh
                            for k in range(NKB):
                                lhs = wsl[:, k * 512 + ib * 128:
                                          k * 512 + ib * 128 + 128]
                                last = e.matmul(
                                    ps, lhs, hbr[:, k, 2 - sh:258 - sh],
                                    start=first,
                                    stop=(wsh == 2 and k == NKB - 1),
                                    skip_group_check=True)
                                first = False
                    return last
                import os as _os
                tcv = P.do("tensor", f_conv2, wait=wx, inc=sP)

                t_sl = 0
                for ib in range(4):
                    t_sl = P.do("scalar",
                                lambda e, ib=ib: e.activation(
                                    sg_t[:, ib * 256:(ib + 1) * 256],
                                    gps_of(ib), AF.Silu),
                                wait=[(sP, tcv)] if ib == 0 else [],
                                inc=sA)
                t_ym = 0
                for ib in range(4):
                    ups = PS[:, 0:256] if ib % 2 == 0 else PO[:, 0:256]

                    def f_u(e, ib=ib, ups=ups):
                        last = None
                        for k in range(NKB):
                            last = e.matmul(
                                ups, wvb[:, 4096 + k * 512 + ib * 128:
                                         4096 + k * 512 + ib * 128 + 128],
                                hbr[:, k, 2:258], start=(k == 0),
                                stop=(k == 7), skip_group_check=True)
                        return last
                    wu = [(sA, t_sl)] if ib < 2 else [(sV, t_ym)]
                    tu = P.do("tensor", f_u, wait=wu, inc=sP)
                    t_ym = P.do("vector",
                                lambda e, ib=ib, ups=ups: e.tensor_tensor(
                                    y_t[:, ib * 256:(ib + 1) * 256],
                                    sg_t[:, ib * 256:(ib + 1) * 256], ups,
                                    ALU.mult),
                                wait=[(sP, tu)], inc=sV)

                def f_d(e):
                    last = None
                    for mb in range(8):
                        ps = (PAB[:, (mb % 4) * 256:(mb % 4) * 256 + 256]
                              if mb < 4 else
                              PCD[:, (mb % 4) * 256:(mb % 4) * 256 + 256])
                        for ib in range(4):
                            last = e.matmul(
                                ps, slotA[:, ib * 1024 + mb * 128:
                                          ib * 1024 + mb * 128 + 128],
                                y_t[:, ib * 256:(ib + 1) * 256],
                                start=(ib == 0 and mb in (0, 2, 4, 6)),
                                stop=(ib == 3), skip_group_check=True)
                    return last
                t_d = P.do("tensor", f_d, wait=[(sV, t_ym)], inc=sP)

                def f_s(e):
                    last = None
                    for mb in range(8):
                        ps = (PAB[:, (mb % 4) * 256:(mb % 4) * 256 + 256]
                              if mb < 4 else
                              PCD[:, (mb % 4) * 256:(mb % 4) * 256 + 256])
                        e.tensor_add(kb(outt, mb), kb(rtt, mb), ps)
                        last = e.tensor_add(kb(outt, mb), kb(outt, mb),
                                            kb(h1t, mb))
                    return last
                tv5 = P.do("vector", f_s, wait=[(sP, t_d)], inc=sV)

                def f_cp(e):
                    last = None
                    for k in range(NKB):
                        last = e.tensor_copy(kb(x0g, k), kb(outt, k))
                    return last
                tv6 = P.do("vector", f_cp, wait=[(sV, tv5)], inc=sV)
                tq = P.do("scalar",
                          lambda e: e.activation(sqt, x0g, AF.Square),
                          wait=[(sV, tv6)], inc=sA)

                def f_sums(e):
                    for k in range(NKB):
                        e.matmul(PS[0:1, 0:256], onc, kb(x0g, k),
                                 start=(k == 0), stop=False,
                                 skip_group_check=True)
                    last = None
                    for k in range(NKB):
                        last = e.matmul(PS[0:1, 256:512], onc, kb(sqt, k),
                                        start=False, stop=(k == 7),
                                        skip_group_check=True)
                    return last
                tp = P.do("tensor", f_sums, wait=[(sA, tq)], inc=sP)

                def f_stats(e, n=C):
                    e.tensor_scalar(tmpA[0:1, :], PS[0:1, 0:256], 1.0 / n,
                                    None, ALU.mult)
                    e.tensor_mul(tmpB[0:1, :], tmpA[0:1, :], tmpA[0:1, :])
                    e.scalar_tensor_tensor(tmpB[0:1, :], PS[0:1, 256:512],
                                           1.0 / n, tmpB[0:1, :], ALU.mult,
                                           ALU.subtract)
                    e.tensor_copy(sthi[0:1, 0:256], tmpA[0:1, :])
                    return e.tensor_sub(tmpA[0:1, :], tmpA[0:1, :],
                                        sthi[0:1, 0:256])
                tv = P.do("vector", f_stats, wait=[(sP, tp)], inc=sV)

                def f_rstd(e):
                    e.activation(tmpB[0:1, :], tmpB[0:1, :], AF.Ln,
                                 bias=aux[0:1, 24:25])
                    return e.activation(tmpB[0:1, :], tmpB[0:1, :], AF.Exp,
                                        scale=-0.5)
                ta = P.do("scalar", f_rstd, wait=[(sV, tv)], inc=sA)

                def f_hl(e):
                    e.tensor_copy(sthi[0:1, 256:512], tmpA[0:1, :])
                    e.tensor_copy(sthi[0:1, 512:768], tmpB[0:1, :])
                    e.tensor_sub(tmpB[0:1, :], tmpB[0:1, :],
                                 sthi[0:1, 512:768])
                    return e.tensor_copy(sthi[0:1, 768:1024], tmpB[0:1, :])
                tv2 = P.do("vector", f_hl, wait=[(sA, ta)], inc=sV)

                def f_bcast(e):
                    e.matmul(PO[:, 0:256], onr, sthi[0:1, 0:256],
                             start=True, stop=False, skip_group_check=True)
                    e.matmul(PO[:, 0:256], onr, sthi[0:1, 256:512],
                             start=False, stop=True, skip_group_check=True)
                    e.matmul(PO[:, 256:512], onr, sthi[0:1, 512:768],
                             start=False, stop=False, skip_group_check=True)
                    return e.matmul(PO[:, 256:512], onr,
                                    sthi[0:1, 768:1024], start=False,
                                    stop=True, skip_group_check=True)
                tp2 = P.do("tensor", f_bcast, wait=[(sV, tv2)], inc=sP)

                def f_out(e):
                    last = None
                    for k in range(NKB):
                        e.tensor_sub(tmpA, kb(outt, k), PO[:, 0:256])
                        last = e.tensor_mul(kb(outt, k), tmpA,
                                            PO[:, 256:512])
                    return last
                tv7 = P.do("vector", f_out, wait=[(sP, tp2)], inc=sV)
                dma(stsl(out_d.rearrange("(a p) t -> p a t", p=128), st),
                    r3(outt), wait=[(sV, tv7)])

        x0a_ = es.enter_context(nc.sbuf_tensor("x0a", [128, 2048], F32))
        x0b_ = es.enter_context(nc.sbuf_tensor("x0b", [128, 2048], F32))
        x1t_ = es.enter_context(nc.sbuf_tensor("x1t", [128, 2048], F32))
        Wt_ = es.enter_context(nc.sbuf_tensor("Wt", [128, 2048], F32))
        scum_ = es.enter_context(nc.sbuf_tensor("scum", [128, 512], F32))
        tmpA_ = es.enter_context(nc.sbuf_tensor("tmpA", [128, 256], F32))
        tmpB_ = es.enter_context(nc.sbuf_tensor("tmpB", [128, 256], F32))
        tmpC_ = es.enter_context(nc.sbuf_tensor("tmpC", [128, 256], F32))
        x0f_ = es.enter_context(nc.sbuf_tensor("x0f", [128, 2048], BF16))
        x0g_ = es.enter_context(nc.sbuf_tensor("x0g", [128, 2048], BF16))
        x1f_ = es.enter_context(nc.sbuf_tensor("x1f", [128, 2048], BF16))
        xnt_ = es.enter_context(nc.sbuf_tensor("xnt", [128, 2048], BF16))
        dxt_ = es.enter_context(nc.sbuf_tensor("dxt", [128, 2048], BF16))
        sqt_ = es.enter_context(nc.sbuf_tensor("sqt", [128, 2048], BF16))
        m_r_ = es.enter_context(nc.sbuf_tensor("m_r", [128, 2048], BF16))
        m_kraw_ = es.enter_context(nc.sbuf_tensor("m_kraw", [128, 2048], BF16))
        m_kp_ = es.enter_context(nc.sbuf_tensor("m_kp", [128, 2048], BF16))
        m_kk_ = es.enter_context(nc.sbuf_tensor("m_kk", [128, 2048], BF16))
        m_b_ = es.enter_context(nc.sbuf_tensor("m_b", [128, 2048], BF16))
        m_v_ = es.enter_context(nc.sbuf_tensor("m_v", [128, 2048], BF16))
        m_g_ = es.enter_context(nc.sbuf_tensor("m_g", [128, 2048], BF16))
        m_o_ = es.enter_context(nc.sbuf_tensor("m_o", [128, 2048], BF16))
        hw_ = es.enter_context(nc.sbuf_tensor("hw", [64, 256], BF16))
        ha_ = es.enter_context(nc.sbuf_tensor("ha", [64, 256], BF16))
        hg_ = es.enter_context(nc.sbuf_tensor("hg", [128, 256], BF16))
        cxn_ = es.enter_context(nc.sbuf_tensor("cxn", [128, 8], BF16))
        cbuf_ = es.enter_context(nc.sbuf_tensor("cbuf", [128, 520], BF16))
        cibuf_ = es.enter_context(nc.sbuf_tensor("cibuf", [128, 512], BF16))
        clast_ = es.enter_context(nc.sbuf_tensor("clast", [128, 8], F32))
        AR_ = es.enter_context(nc.sbuf_tensor("AR", [128, 1024], BF16))
        BS_ = es.enter_context(nc.sbuf_tensor("BS", [128, 1024], BF16))
        ktil_ = es.enter_context(nc.sbuf_tensor("ktil", [128, 512], BF16))
        vbtm_ = es.enter_context(nc.sbuf_tensor("vbtm", [128, 1024], BF16))
        ktm_ = es.enter_context(nc.sbuf_tensor("ktm", [128, 512], BF16))
        gp3_ = es.enter_context(nc.sbuf_tensor("gp3", [128, 1536], BF16))
        sol0_ = es.enter_context(nc.sbuf_tensor("sol0", [128, 2048], BF16))
        sol1_ = es.enter_context(nc.sbuf_tensor("sol1", [128, 2048], BF16))
        otm_ = es.enter_context(nc.sbuf_tensor("otm", [128, 512], BF16))
        sfp_ = es.enter_context(nc.sbuf_tensor("sfp", [128, 512], F32))
        wkb_ = es.enter_context(nc.sbuf_tensor("wkb", [128, 8192], BF16))
        wvb_ = es.enter_context(nc.sbuf_tensor("wvb", [128, 8192], BF16))
        slotA_ = es.enter_context(nc.sbuf_tensor("slotA", [128, 8192], BF16))
        w1b_ = es.enter_context(nc.sbuf_tensor("w1b", [128, 512], BF16))
        a1b_ = es.enter_context(nc.sbuf_tensor("a1b", [128, 512], BF16))
        g1b_ = es.enter_context(nc.sbuf_tensor("g1b", [128, 1024], BF16))
        w2b_ = es.enter_context(nc.sbuf_tensor("w2b", [64, 1024], BF16))
        a2b_ = es.enter_context(nc.sbuf_tensor("a2b", [64, 1024], BF16))
        g2b_ = es.enter_context(nc.sbuf_tensor("g2b", [128, 1024], BF16))
        PAB_ = es.enter_context(nc.psum_tensor("PAB", [128, 1024], F32))
        PT_ = es.enter_context(nc.psum_tensor("PT", [128, 1024], BF16))
        PT2_ = es.enter_context(nc.psum_tensor("PT2", [128, 1024], BF16))
        PCD_ = es.enter_context(nc.psum_tensor("PCD", [128, 1024], F32))
        PS_ = es.enter_context(nc.psum_tensor("PS", [128, 512], F32))
        PO_ = es.enter_context(nc.psum_tensor("PO", [128, 512], F32))
        if True:
            ctx = tuple(x.ap() for x in (
                x0a_, x0b_, x1t_, Wt_, scum_, tmpA_, tmpB_, tmpC_, x0f_, x0g_,
                x1f_, xnt_, dxt_, sqt_, m_r_, m_kraw_, m_kp_, m_kk_, m_b_,
                m_v_, m_g_, m_o_, hw_, ha_, hg_, cxn_, cbuf_, cibuf_, clast_,
                AR_, BS_, ktil_, vbtm_, ktm_, gp3_, sol0_, sol1_, otm_, sfp_,
                wkb_, wvb_, slotA_, w1b_, a1b_, g1b_, w2b_, a2b_, g2b_,
                PAB_, PT_, PT2_, PCD_, PS_, PO_))
            block.sync(lambda e: emit_all(Pass("sync", e), ctx))
            block.tensor(lambda e: emit_all(Pass("tensor", e), ctx))
            block.vector(lambda e: emit_all(Pass("vector", e), ctx))
            block.scalar(lambda e: emit_all(Pass("scalar", e), ctx))
            block.gpsimd(lambda e: emit_all(Pass("gpsimd", e), ctx))
    return nc, dbg_d


def _prep_inputs(inputs):
    f = np.float32

    def t(x):
        return np.ascontiguousarray(np.asarray(x, f))
    P = {k.replace("L0_attn_", "").replace("L0_ffn_", ""): v
         for k, v in inputs.items()}
    ins = {}
    ins["wr"] = t(P["W_r"]); ins["wk"] = t(P["W_k"])
    ins["wv"] = t(P["W_v"]); ins["wo"] = t(P["W_o"])
    ins["w1"] = t(P["w1"]); ins["a1"] = t(P["a1"]); ins["g1"] = t(P["g1"])
    ins["w2"] = t(P["w2"]); ins["a2"] = t(P["a2"]); ins["g2"] = t(P["g2"])
    ins["wf1"] = t(P["W_ffn1"]); ins["wf2"] = t(P["W_ffn2"])
    ins["egw"] = t(P["expert_gate_w"]); ins["euw"] = t(P["expert_up_w"])
    ins["edw"] = t(P["expert_down_w"])
    ins["wc"] = np.ascontiguousarray(
        np.asarray(P["conv_w"], f).transpose(2, 1, 0))
    ins["sup"] = t(P["shared_up_w"]); ins["sdw"] = t(P["shared_down_w"])
    ins["rw"] = t(P["router_w"])
    ins["rb"] = np.ascontiguousarray(
        np.asarray(P["router_bias"], f).reshape(16, 1))
    vpa = np.zeros((128, NVP), f)
    for name in VP_ORDER:
        v = np.asarray(P[name], f).reshape(-1)
        vpa[:, VPC[name]:VPC[name] + 8] = v.reshape(8, 128).T
    ins["vp"] = vpa
    bf = ml_dtypes.bfloat16
    eye = np.eye(64, dtype=f)
    ins["idb"] = np.concatenate([eye, eye], axis=0).astype(bf)
    p = np.arange(64)[:, None]
    q = np.arange(64)[None, :]
    up_s = (p < q).astype(f)
    up_i = (p <= q).astype(f)
    lo_s = (q < p).astype(f)
    m4 = np.concatenate([up_s, up_i, up_s, up_i], axis=1)
    ins["mk4"] = np.concatenate([m4, m4], axis=0).astype(bf)
    ins["mk1"] = np.concatenate([lo_s, lo_s], axis=0).astype(bf)
    ins["onc"] = np.ones((128, 1), f).astype(bf)
    ins["onr"] = np.ones((1, 128), f).astype(bf)
    e16 = np.zeros((128, 128), f)
    for k in range(8):
        e16[0:64, k * 16 + 2 * k] = 1.0
        e16[64:128, k * 16 + 2 * k + 1] = 1.0
    ins["e16"] = e16.astype(bf)
    ins["if16"] = np.eye(16, dtype=f)
    s15 = np.zeros((16, 128), f)
    s15[15, :] = 1.0
    ins["s15"] = s15.astype(bf)
    sel = np.zeros((16, 1920), f)
    for e2 in range(15):
        sel[e2, e2 * 128:(e2 + 1) * 128] = 1.0
    ins["sel15"] = sel.astype(bf)
    ins["if128"] = np.eye(128, dtype=f)
    f16 = np.zeros((16, 1024), f)
    for k in range(8):
        f16[2 * k, k * 128:k * 128 + 64] = 1.0
        f16[2 * k + 1, k * 128 + 64:k * 128 + 128] = 1.0
    ins["f16"] = f16.astype(bf)
    return ins


def kernel(**inputs):
    x = np.asarray(inputs["L0_attn_x"], np.float32)
    B = x.shape[0]
    shared = _prep_inputs(inputs)
    xt = np.ascontiguousarray(x.transpose(0, 2, 1))
    nc, _ = build(bass.Bass("TRN2", target_bir_lowering=False, debug=False))
    in_maps = []
    for b in range(B):
        m = dict(shared)
        m["x0"] = xt[b]
        in_maps.append(m)
    res = run_bass_kernel_spmd(nc, in_maps, core_ids=list(range(B)))
    out = np.stack([np.asarray(r["out"]) for r in res.results], axis=0)
    return np.ascontiguousarray(out.transpose(0, 2, 1))
